# version 04
# baseline (speedup 1.0000x reference)
import numpy as np
import concourse.bass as bass
import concourse.mybir as mybir
from concourse.bass_utils import run_bass_kernel_spmd

F32 = mybir.dt.float32
BF16 = mybir.dt.bfloat16
I32 = mybir.dt.int32
AF = mybir.ActivationFunctionType
ALU = mybir.AluOpType
AX = mybir.AxisListType

NCORES = 8
T = 2048
NT = 16
F = 1024
NF = 8
H = 16
CAP = 2048.0
L = 512
NSLOT = 8 * L
NST = NSLOT // 128
INTER = 4096
HID = 2048
SBIG = 1048576.0
GELU_S = float(np.float32(np.sqrt(0.044715)))
GELU_A = float(np.float32(2.0 * np.sqrt(2.0 / np.pi)))
RMS_C = float(np.float32(1024.0 ** 0.5 + 1e-6))

ENGN = ["sync", "scalar", "vector", "tensor", "gpsimd"]


class Prog:
    def __init__(self, nc, es):
        self.nc = nc
        self.es = es
        self.q = {en: [] for en in ENGN}
        names = ["mm", "ev", "ac", "gp", "cc", "bar"]
        self.SEM = {n: es.enter_context(nc.semaphore(n)) for n in names}
        self.CNT = {n: 0 for n in names}

    def sem(self, name):
        if name not in self.SEM:
            self.SEM[name] = self.es.enter_context(self.nc.semaphore(name))
            self.CNT[name] = 0
        return name

    def op(self, eng, fn, waits=(), inc=None, inck=1):
        for s, v in waits:
            self.sem(s)
        wl = [(s, v) for (s, v) in waits if v > 0]
        if inc is not None:
            self.sem(inc)
            self.CNT[inc] += inck
        SEM = self.SEM

        def run(e, fn=fn, wl=wl, inc=inc, inck=inck):
            for s, v in wl:
                e.wait_ge(SEM[s], v)
            inst = fn(e)
            if inc is not None:
                inst.then_inc(SEM[inc], inck)
        self.q[eng].append(run)
        return self.CNT[inc] if inc is not None else None

    def m(self, name):
        return self.CNT[name]

    def mark(self, eng, sem, waits=()):
        self.sem(sem)
        for s, v in waits:
            self.sem(s)
        self.CNT[sem] += 1
        SEM = self.SEM
        wl = [(s, v) for (s, v) in waits if v > 0]

        def run(e, wl=wl, sem=sem):
            for s, v in wl:
                e.wait_ge(SEM[s], v)
            e.drain().then_inc(SEM[sem], 1)
        self.q[eng].append(run)
        return self.CNT[sem]

    def barrier(self):
        self.CNT["bar"] += 5
        tgt = self.CNT["bar"]
        for en in ENGN:
            def run(e, tgt=tgt):
                e.drain().then_inc(self.SEM["bar"], 1)
                e.wait_ge(self.SEM["bar"], tgt)
            self.q[en].append(run)

    def replay(self):
        with self.nc.Block() as blk:
            for en in ENGN:
                def section(e, en=en):
                    for f in self.q[en]:
                        f(e)
                getattr(blk, en)(section)


class _StopBuild(Exception):
    pass


def build_nc(debug=False, upto=99, p0_tiles=16):
    nc = bass.Bass("TRN2", target_bir_lowering=False)
    dtn = nc.dram_tensor

    x_sh = dtn("x_sh", [T, F], F32, kind="ExternalInput").ap()
    qkv_w = dtn("qkv_w", [3 * F, F], F32, kind="ExternalInput").ap()
    proj_w = dtn("proj_w", [F, F], F32, kind="ExternalInput").ap()
    conv1_w = dtn("conv1_w", [HID, F], F32, kind="ExternalInput").ap()
    conv2_w = dtn("conv2_w", [F, HID], F32, kind="ExternalInput").ap()
    inter_w = dtn("inter_w", [8 * F, INTER], F32, kind="ExternalInput").ap()
    output_w = dtn("output_w", [8 * INTER, F], F32, kind="ExternalInput").ap()
    ri_w = dtn("ri_w", [F, INTER], F32, kind="ExternalInput").ap()
    ro_w = dtn("ro_w", [INTER, F], F32, kind="ExternalInput").ap()
    wgcat = dtn("wgcat", [F, 10], F32, kind="ExternalInput").ap()
    ident = dtn("ident", [128, 128], F32, kind="ExternalInput").ap()
    iotok = dtn("iotok", [128, NT], I32, kind="ExternalInput").ap()
    e512c = dtn("e512c", [8, 1], F32, kind="ExternalInput").ap()
    prefm = dtn("prefm", [8, 8], F32, kind="ExternalInput").ap()
    bones = dtn("bones", [H, F], F32, kind="ExternalInput").ap()
    bmask = dtn("bmask", [128, 1], F32, kind="ExternalInput").ap()
    bsel0 = dtn("bsel0", [128, 1], F32, kind="ExternalInput").ap()
    bsel1 = dtn("bsel1", [128, 1], F32, kind="ExternalInput").ap()
    s1c_d = dtn("s1c", [128, NF], F32, kind="ExternalInput").ap()
    s2c_d = dtn("s2c", [128, NF], F32, kind="ExternalInput").ap()
    gmc_d = dtn("gmc", [128, NF], F32, kind="ExternalInput").ap()
    nwc_d = dtn("nwc", [128, NF], F32, kind="ExternalInput").ap()
    nbc_d = dtn("nbc", [128, NF], F32, kind="ExternalInput").ap()
    pbc_d = dtn("pbc", [128, NF], F32, kind="ExternalInput").ap()
    c1bc_d = dtn("c1bc", [128, HID // 128], F32, kind="ExternalInput").ap()
    c2bc_d = dtn("c2bc", [128, NF], F32, kind="ExternalInput").ap()
    ibc_d = dtn("ibc", [128, 8 * 32], F32, kind="ExternalInput").ap()
    ribc_d = dtn("ribc", [128, 32], F32, kind="ExternalInput").ap()
    robc_d = dtn("robc", [128, NF], F32, kind="ExternalInput").ap()
    nwr_d = dtn("nwr", [1, F], F32, kind="ExternalInput").ap()
    nbr_d = dtn("nbr", [1, F], F32, kind="ExternalInput").ap()
    gmr_d = dtn("gmr", [1, F], F32, kind="ExternalInput").ap()
    opb_d = dtn("opb", [8, F], F32, kind="ExternalInput").ap()

    y = dtn("y", [T, F], F32, kind="ExternalOutput").ap()

    d_bq = dtn("d_bq", [3 * F, F], BF16).ap()
    d_bp = dtn("d_bp", [F, F], BF16).ap()
    d_b1 = dtn("d_b1", [HID, F], BF16).ap()
    d_b2 = dtn("d_b2", [F, HID], BF16).ap()
    d_xnT = dtn("d_xnT", [F, T], F32).ap()
    d_x1T = dtn("d_x1T", [F, T], BF16).ap()
    d_ao = dtn("d_ao", [T, F], BF16).ap()
    d_rm = dtn("d_rm", [F, T], BF16).ap()
    d_eo = dtn("d_eo", [NSLOT, F], BF16).ap()
    d_tos = dtn("d_tos", [128, NST], I32).ap()
    cc_cnt_i = dtn("cc_cnt_i", [8, 1], F32).ap()
    cc_cnt_o = dtn("cc_cnt_o", [8 * NCORES, 1], F32).ap()
    cc_kv_i = dtn("cc_kv_i", [128, 1040], F32).ap()
    cc_kv_o = dtn("cc_kv_o", [128, 1040], F32).ap()

    dbg = {}
    if debug:
        for name, shape, ddt in [
            ("dbg_xnT", [F, T], F32), ("dbg_Srow", [1, T], F32),
            ("dbg_kv", [64, 1040], F32), ("dbg_poT", [F, T], BF16),
            ("dbg_x2T", [F, T], F32), ("dbg_cnts", [8, 8], F32),
            ("dbg_loc", [8, T], F32), ("dbg_acc", [F, T], F32),
            ("dbg_g", [128, NT], F32), ("dbg_tos", [128, NST], I32),
            ("dbg_eo", [NSLOT, F], BF16), ("dbg_ao", [T, F], BF16),
            ("dbg_dispT", [128, 8 * 512], BF16), ("dbg_h", [128, 32 * 512], BF16),
            ("dbg_rm", [F, T], BF16), ("dbg_Si", [128, NT], I32),
            ("dbg_gc1", [128, NT], F32), ("dbg_c0", [128, NT], F32),
            ("dbg_A", [128, 3 * NT], F32), ("dbg_V", [128, NT], F32),
            ("dbg_offs", [8, 1], F32), ("dbg_loc1", [8, T], F32),
            ("dbg_m1k", [8, T], F32), ("dbg_lgt", [128, 10 * NT], F32),
            ("dbg_ib", [128, 8 * 32], F32), ("dbg_iotok", [128, NT], I32),
            ("dbg_Fi", [128, NT], I32),
        ]:
            dbg[name] = dtn(name, shape, ddt, kind="ExternalOutput").ap()

    from contextlib import ExitStack
    es = ExitStack()
    p = Prog(nc, es)
    vgap = lambda: p.op("vector", lambda e: e.tensor_copy(s_vgap[:, :], s_vgap[:, :]))
    m = p.m

    sbt = lambda n, s, d: es.enter_context(nc.sbuf_tensor(n, s, d))

    s_id = sbt("s_id", [128, 128], F32)
    s_idb = sbt("s_idb", [128, 128], BF16)
    s_iotok = sbt("s_iotok", [128, NT], I32)
    s_e512 = sbt("s_e512", [8, 1], F32)
    s_pref = sbt("s_pref", [8, 8], F32)
    s_wgk = sbt("s_wgk", [128, 8 * 10], F32)
    s_bm = sbt("s_bm", [128, 1], F32)
    s_b0 = sbt("s_b0", [128, 1], F32)
    s_b1c = sbt("s_b1c", [128, 1], F32)
    s_s1 = sbt("s_s1", [128, NF], F32)
    s_s2 = sbt("s_s2", [128, NF], F32)
    s_gm = sbt("s_gm", [128, NF], F32)
    s_nw = sbt("s_nw", [128, NF], F32)
    s_nb = sbt("s_nb", [128, NF], F32)
    s_pb = sbt("s_pb", [128, NF], F32)
    s_pgb = sbt("s_pgb", [128, NF], F32)
    s_c1b = sbt("s_c1b", [128, HID // 128], F32)
    s_c2b = sbt("s_c2b", [128, NF], F32)
    s_c2gb = sbt("s_c2gb", [128, NF], F32)
    s_ib = sbt("s_ib", [128, 8 * 32], F32)
    s_ibs = sbt("s_ibs", [128, 8 * 32], F32)
    s_rib = sbt("s_rib", [128, 32], F32)
    s_ribs = sbt("s_ribs", [128, 32], F32)
    s_rob = sbt("s_rob", [128, NF], F32)
    s_ones128 = sbt("s_ones128", [128, 1], F32)
    s_ones8 = sbt("s_ones8", [8, 1], F32)
    s_onesr = sbt("s_onesr", [1, 128], F32)
    s_onesrb = sbt("s_onesrb", [1, 128], BF16)
    s_ones128b = sbt("s_ones128b", [128, 1], BF16)
    s_cnt = sbt("s_cnt", [8, 1], F32)
    s_vgap = sbt("s_vgap", [128, 1], F32)
    s_fi32 = sbt("s_fi32", [1, 1], I32)
    s_fbf = sbt("s_fbf", [1, 1], BF16)
    if debug:
        s_dbglgt = sbt("s_dbglgt", [128, 10 * NT], F32)
    s_cnts = sbt("s_cnts", [8, 8], F32)
    s_offs = sbt("s_offs", [8, 1], F32)
    s_gcol = sbt("s_gcol", [128, NT], F32)
    s_c0 = sbt("s_c0", [128, NT], F32)
    s_gc1 = sbt("s_gc1", [128, NT], F32)
    s_Si = sbt("s_Si", [128, NT], I32)
    s_Si2 = sbt("s_Si2", [128, NT], I32)
    s_Fi = sbt("s_Fi", [128, NT], I32)
    s_tos = sbt("s_tos", [128, NST], I32)
    s_gmb = sbt("s_gmb", [128, F], BF16)
    s_bigB = sbt("s_bigB", [128, NF * T], F32)
    s_aoT = sbt("s_aoT", [128, NF * T], BF16)

    pA = es.enter_context(nc.psum_tensor("pA", [128, 1024], F32))
    pB = es.enter_context(nc.psum_tensor("pB", [128, 1024], F32))
    pC = es.enter_context(nc.psum_tensor("pC", [128, 1024], F32))
    pTf = es.enter_context(nc.psum_tensor("pTf", [128, 512], F32))
    pTb = es.enter_context(nc.psum_tensor("pTb", [128, 1024], BF16))

    def fm(handle, ft, a, b):
        return handle[:, ft * T + a: ft * T + b]

    _bcregs = {}

    def bc_reg(e, val):
        if val not in _bcregs:
            _bcregs[val] = e.alloc_register(f"bc{val}")
        r = _bcregs[val]
        e.reg_mov(r, val)
        return r

    try:
        for src, dst in [(ident, s_id), (iotok, s_iotok), (e512c, s_e512), (prefm, s_pref),
                         (bmask, s_bm), (bsel0, s_b0), (bsel1, s_b1c),
                         (s1c_d, s_s1), (s2c_d, s_s2), (gmc_d, s_gm), (nwc_d, s_nw),
                         (nbc_d, s_nb), (pbc_d, s_pb), (c1bc_d, s_c1b), (c2bc_d, s_c2b),
                         (ibc_d, s_ib), (ribc_d, s_rib), (robc_d, s_rob)]:
            p.op("sync", lambda e, s=src, d=dst: e.dma_start(out=d[:, :], in_=s[:, :]), inc="c0", inck=16)
        p.op("sync", lambda e: e.dma_start(
            out=s_wgk.ap().rearrange("pp (k c) -> pp k c", k=8),
            in_=wgcat.rearrange("(k pp) c -> pp k c", pp=128)), inc="c0", inck=16)
        consts_ld = m("c0")
        for src, dst in [(qkv_w, d_bq), (proj_w, d_bp), (conv1_w, d_b1), (conv2_w, d_b2)]:
            p.op("gpsimd", lambda e, s=src, d=dst: e.dma_start(out=d[:, :], in_=s[:, :]), inc="wc", inck=16)
        p.op("gpsimd", lambda e: e.dma_start(out=s_fbf[:, :], in_=d_b2[0:1, 0:1]),
             waits=[("wc", m("wc"))], inc="wc", inck=16)
        wcast_done = m("wc")
        for buf, v in [(s_ones128, 1.0), (s_ones8, 1.0), (s_onesr, 1.0), (s_onesrb, 1.0),
                       (s_ones128b, 1.0)]:
            p.op("gpsimd", lambda e, b=buf, v=v: e.memset(b[:, :], v))
        gp_set = p.mark("gpsimd", "gp")
        p.op("vector", lambda e: e.tensor_copy(s_idb[:, :], s_id[:, :]),
             waits=[("c0", consts_ld), ("gp", gp_set)])
        p.op("vector", lambda e: e.tensor_mul(s_pgb[:, :], s_pb[:, :], s_gm[:, :]))
        p.op("vector", lambda e: e.tensor_mul(s_c2gb[:, :], s_c2b[:, :], s_gm[:, :]))
        p.op("vector", lambda e: e.tensor_scalar_mul(s_ibs[:, :], s_ib[:, :], GELU_S))
        p.op("vector", lambda e: e.tensor_scalar_mul(s_ribs[:, :], s_rib[:, :], GELU_S), inc="ev")
        p.barrier()

        if upto < 1:
            raise _StopBuild()
        from contextlib import ExitStack as ES2
        kvscope = ES2()
        s_kv = kvscope.enter_context(nc.sbuf_tensor("s_kv", [64, 1040], F32))
        s_kvst = kvscope.enter_context(nc.sbuf_tensor("s_kvst", [128, 1040], F32))
        s_kvtmp = kvscope.enter_context(nc.sbuf_tensor("s_kvtmp", [64, 1040], F32))
        mscope = ES2()
        s_m1T = mscope.enter_context(nc.sbuf_tensor("s_m1T", [8, T], F32))
        s_loc = mscope.enter_context(nc.sbuf_tensor("s_loc", [8, T], F32))
        ph0 = ES2()
        sb0 = lambda n, s, d: ph0.enter_context(nc.sbuf_tensor(n, s, d))
        rows = ES2()
        s_nwrow = rows.enter_context(nc.sbuf_tensor("s_nwrow", [1, F], F32))
        s_nbrow = rows.enter_context(nc.sbuf_tensor("s_nbrow", [1, F], F32))
        s_gmrow = rows.enter_context(nc.sbuf_tensor("s_gmrow", [1, F], F32))
        for src, dst in [(nwr_d, s_nwrow), (nbr_d, s_nbrow), (gmr_d, s_gmrow)]:
            p.op("sync", lambda e, s=src, d=dst: e.dma_start(out=d[:, :], in_=s[:, :]), inc="c1", inck=16)
        rows_ld = m("c1")
        for c in range(2):
            p.op("tensor", lambda e, c=c: e.matmul(pA[:, c * 512:(c + 1) * 512], s_onesr[:, :],
                 s_nwrow[:, c * 512:(c + 1) * 512], start=True, stop=True),
                 waits=[("c1", rows_ld), ("gp", gp_set)] if c == 0 else ())
            p.op("tensor", lambda e, c=c: e.matmul(pB[:, c * 512:(c + 1) * 512], s_onesr[:, :],
                 s_nbrow[:, c * 512:(c + 1) * 512], start=True, stop=True))
            p.op("tensor", lambda e, c=c: e.matmul(pC[:, c * 512:(c + 1) * 512], s_onesr[:, :],
                 s_gmrow[:, c * 512:(c + 1) * 512], start=True, stop=True))
        bc_mm = p.mark("tensor", "mm")
        rows.close()
        s_nwb = sb0("s_nwb", [128, F], F32)
        s_nbb = sb0("s_nbb", [128, F], F32)
        p.op("vector", lambda e: e.tensor_copy(s_nwb[:, :], pA[:, :]), waits=[("mm", bc_mm)])
        p.op("vector", lambda e: e.tensor_copy(s_nbb[:, :], pB[:, :]))
        bc_ev = p.op("vector", lambda e: e.tensor_copy(s_gmb[:, :], pC[:, :]), inc="ev")

        s_xt = [sb0(f"s_xt{i}", [128, F], F32) for i in range(2)]
        s_xn = [sb0(f"s_xn{i}", [128, F], F32) for i in range(2)]
        s_z2 = [sb0(f"s_z2{i}", [128, F], F32) for i in range(2)]
        s_scr = sb0("s_scr", [128, F], F32)
        s_aotmp = [sb0(f"s_aotmp{i}", [128, F], F32) for i in range(2)]
        s_aotok = [sb0(f"s_aotok{i}", [128, F], BF16) for i in range(2)]
        s_xr1tok = [sb0(f"s_xr1tok{i}", [128, F], BF16) for i in range(2)]
        s_xnsp = [sb0(f"s_xnsp{i}", [128, F], F32) for i in range(2)]
        s_x1sp = [sb0(f"s_x1sp{i}", [128, F], BF16) for i in range(2)]
        s_m1t = sb0("s_m1t", [128, 8], F32)
        s_lgt = sb0("s_lgt", [128, 10], F32)
        s_cA = sb0("s_cA", [128, 1], F32)
        s_cB = sb0("s_cB", [128, 1], F32)
        s_cC = sb0("s_cC", [128, 1], F32)
        s_cD = sb0("s_cD", [128, 1], F32)
        s_cE = sb0("s_cE", [128, 1], F32)
        if debug:
            s_dbgA = sb0("s_dbgA", [128, 3 * NT], F32)
            s_dbgV = sb0("s_dbgV", [128, NT], F32)

        ac_xfree = [0] * NT
        mm_xntr = [0] * NT
        mm_z2tr = [0] * NT
        mm_xrtr = [0] * NT
        mm_lg = [0] * NT
        ev_aoTc = [0] * NT
        ev_ao9 = [0] * NT
        sc_ao = [0] * NT
        ld_spl = [0] * NT
        ld_x1sp = [0] * NT
        ev_ptf_last = [0] * NT
        ev_lgt_t = [0] * NT
        ev_xr1c_t = [0] * NT

        for t in range(p0_tiles):
            par = t % 2
            lw = [("ev", bc_ev)] if t == 0 else []
            if t >= 2:
                lw.append(("ac", ac_xfree[t - 2]))
            ldx = p.op("sync", lambda e, t=t, par=par: e.dma_start(out=s_xt[par][:, :],
                       in_=x_sh[t * 128:(t + 1) * 128, :]), waits=lw, inc=f"x{par}", inck=16)
            p.op("vector", lambda e, par=par: e.reduce_sum(s_cA[:, :], s_xt[par][:, :], axis=AX.X),
                 waits=[(f"x{par}", ldx)])
            vgap()
            ev1 = p.op("vector", lambda e: e.tensor_scalar_mul(s_cB[:, :], s_cA[:, :], -1.0 / F), inc="ev")
            acw = [("ev", ev1)]
            if t >= 1:
                acw.append(("ev", ev_ao9[t - 1]))
            if debug:
                p.op("scalar", lambda e, t=t: e.activation(
                    s_dbgA[:, 3 * t:3 * t + 1], s_cB[:, 0:1], AF.Identity), waits=list(acw), inc="ac")
            ac1 = p.op("scalar", lambda e, par=par: e.activation(s_scr[:, :], s_xt[par][:, :], AF.Square,
                       bias=s_cB[:, 0:1], scale=1.0, accum_out=s_cA[:, 0:1]), waits=acw, inc="ac")
            p.op("vector", lambda e: e.tensor_scalar(s_cC[:, :], s_cA[:, :], 1.0 / F, 1e-5,
                 ALU.mult, ALU.add), waits=[("ac", ac1)])
            ev2 = p.mark("vector", "ev")
            ac2 = p.op("scalar", lambda e: e.activation(s_cC[:, :], s_cC[:, :], AF.Sqrt),
                       waits=[("ev", ev2)], inc="ac")
            p.op("vector", lambda e: e.reciprocal(s_cD[:, :], s_cC[:, :]), waits=[("ac", ac2)])
            vgap()
            ev3 = p.op("vector", lambda e: e.tensor_mul(s_cE[:, :], s_cB[:, :], s_cD[:, :]), inc="ev")
            if debug:
                p.op("vector", lambda e, t=t: e.tensor_copy(s_dbgV[:, t:t + 1], s_cE[:, 0:1]))
            xnw = [("ev", ev3)]
            if t >= 2:
                xnw += [("mm", mm_xntr[t - 2]), ("mm", mm_xrtr[t - 2]), ("mm", mm_z2tr[t - 2])]
            if debug:
                p.op("scalar", lambda e, t=t: e.activation(
                    s_dbgA[:, 3 * t + 1:3 * t + 2], s_cD[:, 0:1], AF.Identity),
                    waits=list(xnw), inc="ac")
                p.op("scalar", lambda e, t=t: e.activation(
                    s_dbgA[:, 3 * t + 2:3 * t + 3], s_cE[:, 0:1], AF.Identity), inc="ac")
            acxn = p.op("scalar", lambda e, par=par: e.activation(s_xn[par][:, :], s_xt[par][:, :], AF.Identity,
                        bias=s_cE[:, 0:1], scale=s_cD[:, 0:1]), waits=xnw, inc="ac")
            ac_xfree[t] = acxn
            p.op("vector", lambda e, par=par: e.reduce_sum(s_cA[:, :], s_xn[par][:, :], axis=AX.X),
                 waits=[("ac", acxn)])
            vgap()
            ev4 = p.op("vector", lambda e: e.tensor_scalar_mul(s_cB[:, :], s_cA[:, :], -1.0 / F), inc="ev")
            ac3 = p.op("scalar", lambda e, par=par: e.activation(s_scr[:, :], s_xn[par][:, :], AF.Square,
                       bias=s_cB[:, 0:1], scale=1.0, accum_out=s_cA[:, 0:1]), waits=[("ev", ev4)], inc="ac")
            p.op("vector", lambda e: e.tensor_scalar(s_cC[:, :], s_cA[:, :], 1.0 / F, 1e-12,
                 ALU.mult, ALU.add), waits=[("ac", ac3)])
            ev5 = p.mark("vector", "ev")
            ac4 = p.op("scalar", lambda e: e.activation(s_cC[:, :], s_cC[:, :], AF.Sqrt),
                       waits=[("ev", ev5)], inc="ac")
            p.op("vector", lambda e: e.reciprocal(s_cD[:, :], s_cC[:, :]), waits=[("ac", ac4)])
            vgap()
            ev6 = p.op("vector", lambda e: e.tensor_mul(s_cE[:, :], s_cB[:, :], s_cD[:, :]), inc="ev")
            z2w = [("ev", ev6)]
            if t >= 2:
                z2w += [("ev", ev_ao9[t - 2])]
            acz2 = p.op("scalar", lambda e, par=par: e.activation(s_z2[par][:, :], s_xn[par][:, :], AF.Identity,
                        bias=s_cE[:, 0:1], scale=s_cD[:, 0:1]), waits=z2w, inc="ac")
            ac5 = p.op("scalar", lambda e, par=par: e.activation(s_scr[:, :], s_xn[par][:, :], AF.Square,
                       bias=0.0, scale=1.0, accum_out=s_cA[:, 0:1]), inc="ac")
            ac6 = p.op("scalar", lambda e: e.activation(s_cB[:, :], s_cA[:, :], AF.Sqrt), inc="ac")
            p.op("vector", lambda e: e.reciprocal(s_cC[:, :], s_cB[:, :]), waits=[("ac", ac6)])
            vgap()
            ev7 = p.op("vector", lambda e: e.tensor_scalar_mul(s_cD[:, :], s_cC[:, :], RMS_C), inc="ev")
            xrw = [("ev", ev7)]
            if t >= 2:
                xrw.append(("mm", mm_xrtr[t - 2]))
            acxr = p.op("scalar", lambda e, par=par: e.activation(s_xr1tok[par][:, :], s_xn[par][:, :],
                        AF.Identity, bias=0.0, scale=s_cD[:, 0:1]), waits=xrw, inc="ac")
            aow = [("ac", acz2), ("ac", ac5)]
            if t >= 2:
                aow.append((f"a{par}", sc_ao[t - 2]))
            p.op("vector", lambda e, par=par: e.tensor_mul(s_scr[:, :], s_z2[par][:, :], s_nwb[:, :]),
                 waits=aow)
            ev_ao9[t] = p.op("vector", lambda e, par=par: e.tensor_add(s_aotok[par][:, :], s_scr[:, :],
                             s_nbb[:, :]), inc="ev")
            sc_ao[t] = p.op("scalar", lambda e, t=t, par=par: e.dma_start(
                out=d_ao[t * 128:(t + 1) * 128, :], in_=s_aotok[par][:, :]),
                waits=[("ev", ev_ao9[t])], inc=f"a{par}", inck=16)
            ev_rounds = []
            for r in range(2):
                tw = [("ac", acxn)]
                if t == 0 and r == 0:
                    pass
                elif r == 0:
                    tw.append(("ev", ev_ptf_last[t - 1]))
                else:
                    tw.append(("ev", ev_rounds[0]))
                if r == 0 and t >= 2:
                    tw.append((f"sp{par}", ld_spl[t - 2]))
                for j in range(4):
                    ft = r * 4 + j
                    p.op("tensor", lambda e, par=par, ft=ft, j=j: e.transpose(
                        pTf[:, j * 128:(j + 1) * 128], s_xn[par][:, ft * 128:(ft + 1) * 128], s_id[:, :]),
                        waits=tw if j == 0 else ())
                mmr = p.mark("tensor", "mm")
                ev_rounds.append(p.op("vector", lambda e, par=par, r=r: e.tensor_copy(
                    s_xnsp[par][:, r * 512:(r + 1) * 512], pTf[:, :]), waits=[("mm", mmr)], inc="ev"))
            mm_xntr[t] = m("mm")
            ld_spl[t] = p.op("sync", lambda e, t=t, par=par: e.dma_start(
                out=d_xnT.rearrange("(f pp) tok -> pp f tok", pp=128)[:, :, t * 128:(t + 1) * 128],
                in_=s_xnsp[par].ap().rearrange("pp (f c) -> pp f c", f=NF)),
                waits=[("ev", ev_rounds[1])], inc=f"sp{par}", inck=16)
            ac_blocks = []
            for r in range(2):
                if r == 0:
                    tw = [("ac", acz2), ("ev", ev_rounds[1])]
                    if t >= 2:
                        tw += [("mm", mm_lg[t - 2]), ("ev", ev_aoTc[t - 2])]
                else:
                    tw = [("ac", m("ac"))]
                for j in range(4):
                    ft = r * 4 + j
                    p.op("tensor", lambda e, par=par, ft=ft, j=j: e.transpose(
                        pTf[:, j * 128:(j + 1) * 128], s_z2[par][:, ft * 128:(ft + 1) * 128], s_id[:, :]),
                        waits=tw if j == 0 else ())
                mmr = p.mark("tensor", "mm")
                for j in range(4):
                    ft = r * 4 + j
                    p.op("scalar", lambda e, par=par, ft=ft, j=j: e.activation(
                        s_aotmp[par][:, ft * 128:(ft + 1) * 128], pTf[:, j * 128:(j + 1) * 128], AF.Identity,
                        bias=s_nb[:, ft:ft + 1], scale=s_nw[:, ft:ft + 1]),
                        waits=[("mm", mmr)] if j == 0 else ())
                ac_blocks.append(p.mark("scalar", "ac"))
            mm_z2tr[t] = m("mm")
            ao_ac = ac_blocks[1]
            ev_aoTc[t] = p.op("vector", lambda e, par=par, t=t: e.tensor_copy(
                s_aoT.ap().rearrange("pp (f tok) -> pp f tok", f=NF)[:, :, t * 128:(t + 1) * 128],
                s_aotmp[par].ap().rearrange("pp (f c) -> pp f c", f=NF)),
                waits=[("ac", ao_ac)], inc="ev")
            xtw = [("ac", acxr)]
            if t >= 1:
                xtw.append(("ev", ev_xr1c_t[t - 1]))
            if t >= 2:
                xtw.append((f"q{par}", ld_x1sp[t - 2]))
            for ft in range(NF):
                p.op("tensor", lambda e, par=par, ft=ft: e.transpose(
                    pTb[:, ft * 128:(ft + 1) * 128], s_xr1tok[par][:, ft * 128:(ft + 1) * 128], s_idb[:, :]),
                    waits=xtw if ft == 0 else ())
            mmx = p.mark("tensor", "mm")
            mm_xrtr[t] = mmx
            ev_xr1c_t[t] = p.op("vector", lambda e, par=par: e.tensor_copy(
                s_x1sp[par][:, :], pTb[:, :]), waits=[("mm", mmx)], inc="ev")
            ld_x1sp[t] = p.op("sync", lambda e, t=t, par=par: e.dma_start(
                out=d_x1T.rearrange("(f pp) tok -> pp f tok", pp=128)[:, :, t * 128:(t + 1) * 128],
                in_=s_x1sp[par].ap().rearrange("pp (f c) -> pp f c", f=NF)),
                waits=[("ev", ev_xr1c_t[t])], inc=f"q{par}", inck=16)
            lgw = [("ac", ao_ac)]
            if t >= 1:
                lgw.append(("ev", ev_lgt_t[t - 1]))
            for kt in range(NF):
                p.op("tensor", lambda e, par=par, kt=kt: e.matmul(
                    pC[:, 0:10], s_aotmp[par][:, kt * 128:(kt + 1) * 128],
                    s_wgk[:, kt * 10:(kt + 1) * 10], start=(kt == 0), stop=(kt == NF - 1)),
                    waits=lgw if kt == 0 else ())
            mm_lg[t] = p.mark("tensor", "mm")
            ev10 = p.op("vector", lambda e: e.tensor_copy(s_lgt[:, :], pC[:, 0:10]),
                        waits=[("mm", mm_lg[t])], inc="ev")
            ev_lgt_t[t] = ev10
            if debug:
                p.op("vector", lambda e, t=t: e.tensor_copy(
                    s_dbglgt[:, 10 * t:10 * (t + 1)], s_lgt[:, :]))
            vgap()
            p.op("vector", lambda e: e.reduce_max(s_cA[:, :], s_lgt[:, 0:8], axis=AX.X))
            vgap()
            ev11 = p.op("vector", lambda e: e.tensor_scalar_mul(s_cB[:, :], s_cA[:, :], -1.0), inc="ev")
            ac7 = p.op("scalar", lambda e: e.activation(s_scr[:, 0:8], s_lgt[:, 0:8], AF.Exp,
                       bias=s_cB[:, 0:1], scale=1.0, accum_out=s_cC[:, 0:1]), waits=[("ev", ev11)], inc="ac")
            p.op("vector", lambda e, t=t: e.reciprocal(s_gcol[:, t:t + 1], s_cC[:, :]), waits=[("ac", ac7)])
            p.op("vector", lambda e: e.tensor_scalar(s_m1t[:, :], s_lgt[:, 0:8], s_cA[:, 0:1], None,
                 ALU.is_equal))
            ev12 = p.op("vector", lambda e: e.tensor_sub(s_cD[:, :], s_lgt[:, 8:9], s_lgt[:, 9:10]), inc="ev")
            ac8 = p.op("scalar", lambda e, t=t: e.activation(s_c0[:, t:t + 1], s_cD[:, 0:1], AF.Sigmoid),
                       waits=[("ev", ev12)], inc="ac")
            p.op("vector", lambda e, t=t: e.tensor_scalar(s_cE[:, :], s_c0[:, t:t + 1], -1.0, 1.0,
                 ALU.mult, ALU.add), waits=[("ac", ac8)])
            vgap()
            ev13 = p.op("vector", lambda e, t=t: e.tensor_mul(s_gc1[:, t:t + 1], s_gcol[:, t:t + 1],
                        s_cE[:, 0:1]), inc="ev")
            mmt = p.op("tensor", lambda e: e.transpose(pTf[0:8, 0:128], s_m1t[:, :], s_id[:, :]),
                       waits=[("ev", ev13), ("ac", m("ac"))], inc="mm")
            ev_ptf_last[t] = p.op("vector", lambda e, t=t: e.tensor_copy(
                s_m1T[0:8, t * 128:(t + 1) * 128], pTf[0:8, 0:128]), waits=[("mm", mmt)], inc="ev")
        x1sp_tot = [("q0", m("q0")), ("q1", m("q1"))]
        xnsp_tot = [("sp0", m("sp0")), ("sp1", m("sp1"))]
        if debug and upto < 4:
            p.op("sync", lambda e: e.dma_start(out=dbg["dbg_xnT"][:, :], in_=d_xnT[:, :]),
                 waits=list(xnsp_tot), inc="dbg", inck=16)
            p.op("sync", lambda e: e.dma_start(out=dbg["dbg_ao"][:, :], in_=d_ao[:, :]),
                 waits=[("a0", m("a0")), ("a1", m("a1"))], inc="dbg", inck=16)
        if debug:
            p.op("sync", lambda e: e.dma_start(out=dbg["dbg_A"][:, :], in_=s_dbgA[:, :]),
                 waits=[("ac", m("ac"))], inc="dbg", inck=16)
            p.op("sync", lambda e: e.dma_start(out=dbg["dbg_V"][:, :], in_=s_dbgV[:, :]),
                 waits=[("ev", m("ev"))], inc="dbg", inck=16)
        ph0.close()
        p.barrier()

        if upto < 2:
            raise _StopBuild()
        p.op("vector", lambda e: e.reduce_sum(s_cnt[:, :], s_m1T[:, :], axis=AX.X))
        ev15 = p.op("vector", lambda e: e.tensor_copy(s_vgap[:, :], s_vgap[:, :]), inc="ev")
        g1 = p.op("sync", lambda e: e.dma_start(out=cc_cnt_i[:, :], in_=s_cnt[:, :]),
                  waits=[("ev", ev15)], inc="g0", inck=16)
        cc1 = p.op("gpsimd", lambda e: e.collective_compute(
            "AllGather", ALU.bypass, [list(range(NCORES))], [cc_cnt_i[:, :]], [cc_cnt_o[:, :]]),
            waits=[("g0", g1)], inc="cc")

        if upto < 3:
            raise _StopBuild()
        phkv = ES2()
        sbk = lambda n, s, d: phkv.enter_context(nc.sbuf_tensor(n, s, d))
        s_wkv = sbk("s_wkv", [128, NF * HID], BF16)
        s_x1t = [sbk(f"s_x1t{i}", [128, NF * 128], BF16) for i in range(2)]
        s_kc = [sbk(f"s_kc{i}", [128, F], BF16) for i in range(2)]
        s_vc = [sbk(f"s_vc{i}", [128, F], BF16) for i in range(2)]
        s_ksq = sbk("s_ksq", [128, F], F32)

        wkv_ld = []
        wkv_fold = []
        for kt in range(NF):
            ww = [("wc", wcast_done)] if kt == 0 else []
            if kt >= 2:
                ww.append(("ev", wkv_fold[kt - 2]))
            wkv_ld.append(p.op("sync", lambda e, kt=kt: e.dma_start(
                out=s_wkv[:, kt * HID:(kt + 1) * HID],
                in_=d_bq[F:3 * F, kt * 128:(kt + 1) * 128], transpose=True),
                waits=ww, inc=f"wk{kt % 2}", inck=16))
            wkv_fold.append(p.op("vector", lambda e, kt=kt: e.tensor_scalar_mul(
                s_wkv[:, kt * HID:(kt + 1) * HID], s_wkv[:, kt * HID:(kt + 1) * HID], s_s1[:, kt:kt + 1]),
                waits=[(f"wk{kt % 2}", wkv_ld[kt])], inc="ev"))

        ac_k = [0] * NT
        ev_v = [0] * NT
        mm_kvt = [0] * NT
        mm_kvacc = [0] * NT
        kvadd = [0] * NT
        ld_x1 = [0] * NT
        p.op("gpsimd", lambda e: e.memset(s_kv[:, :], 0.0))
        gp_kv0 = p.mark("gpsimd", "gp")
        for t in range(NT):
            par = t % 2
            xw = list(x1sp_tot) if t == 0 else []
            if t >= 2:
                xw.append(("mm", mm_kvt[t - 2]))
            ld_x1[t] = p.op("sync", lambda e, t=t, par=par: e.dma_start(
                out=s_x1t[par].ap().rearrange("pp (f c) -> pp f c", f=NF),
                in_=d_x1T.rearrange("(f pp) tok -> pp f tok", pp=128)[:, :, t * 128:(t + 1) * 128]),
                waits=xw, inc=f"k{par}", inck=16)
            mw = [("ev", wkv_fold[-1]), (f"k{par}", ld_x1[t])]
            if t >= 1:
                mw.append(("ac", ac_k[t - 1]))
                mw.append(("ev", ev_v[t - 1]))
            for kt in range(NF):
                for half in range(2):
                    p.op("tensor", lambda e, par=par, kt=kt, half=half: e.matmul(
                        pA[:, half * 512:(half + 1) * 512],
                        s_x1t[par][:, kt * 128:(kt + 1) * 128],
                        s_wkv[:, kt * HID + half * 512: kt * HID + (half + 1) * 512],
                        start=(kt == 0), stop=(kt == NF - 1)), waits=mw if (kt == 0 and half == 0) else ())
                    p.op("tensor", lambda e, par=par, kt=kt, half=half: e.matmul(
                        pB[:, half * 512:(half + 1) * 512],
                        s_x1t[par][:, kt * 128:(kt + 1) * 128],
                        s_wkv[:, kt * HID + 1024 + half * 512: kt * HID + 1024 + (half + 1) * 512],
                        start=(kt == 0), stop=(kt == NF - 1)))
            mm_kvt[t] = p.mark("tensor", "mm")
            kw = [("mm", mm_kvt[t])]
            if t >= 2:
                kw.append(("mm", mm_kvacc[t - 2]))
            ack1 = p.op("scalar", lambda e: e.activation(s_ksq[:, :], pA[:, :], AF.Square),
                        waits=kw, inc="ac")
            ack2 = p.op("scalar", lambda e, par=par: e.activation(s_kc[par][:, :], s_ksq[:, :], AF.Exp,
                        scale=-0.5), inc="ac")
            ac_k[t] = ack2
            vw = [("mm", mm_kvt[t])]
            if t >= 2:
                vw.append(("mm", mm_kvacc[t - 2]))
            ev_v[t] = p.op("vector", lambda e, par=par: e.tensor_copy(s_vc[par][:, :], pB[:, :]),
                           waits=vw, inc="ev")
            kvw = [("ac", ack2), ("ev", ev_v[t])]
            if t == 0:
                kvw.append(("gp", gp_kv0))
            else:
                kvw.append(("ev", kvadd[t - 1]))
            for h in range(H):
                if h < 7:
                    pdst, cA0 = pC, h * 65
                elif h < 14:
                    pdst, cA0 = pC, 512 + (h - 7) * 65
                else:
                    pdst, cA0 = pTf, (h - 14) * 65
                p.op("tensor", lambda e, par=par, h=h, cA0=cA0, pdst=pdst: e.matmul(
                    pdst[0:64, cA0:cA0 + 64], s_kc[par][:, h * 64:(h + 1) * 64],
                    s_vc[par][:, h * 64:(h + 1) * 64], start=True, stop=False),
                    waits=kvw if h == 0 else ())
                p.op("tensor", lambda e, par=par, h=h, cA0=cA0, pdst=pdst: e.matmul(
                    pdst[0:64, cA0 + 64:cA0 + 65], s_kc[par][:, h * 64:(h + 1) * 64],
                    s_ones128b[:, 0:1], start=False, stop=True))
            mm_kvacc[t] = p.mark("tensor", "mm")
            p.op("vector", lambda e: e.tensor_add(s_kv[:, 0:455], s_kv[:, 0:455], pC[0:64, 0:455]),
                 waits=[("mm", mm_kvacc[t])])
            p.op("vector", lambda e: e.tensor_add(s_kv[:, 455:910], s_kv[:, 455:910], pC[0:64, 512:967]))
            kvadd[t] = p.op("vector", lambda e: e.tensor_add(
                s_kv[:, 910:1040], s_kv[:, 910:1040], pTf[0:64, 0:130]), inc="ev")
        p.op("vector", lambda e: e.tensor_copy(s_kvst[0:64, :], s_kv[:, :]),
             waits=[("ev", kvadd[NT - 1])])
        p.op("vector", lambda e: e.tensor_copy(s_kvst[64:128, :], s_kv[:, :]))
        ev18 = p.op("vector", lambda e: e.tensor_scalar_mul(s_kvst[:, :], s_kvst[:, :], s_bm[:, 0:1]),
                    inc="ev")
        g2 = p.op("sync", lambda e: e.dma_start(out=cc_kv_i[:, :], in_=s_kvst[:, :]),
                  waits=[("ev", ev18)], inc="g0", inck=16)
        cc2 = p.op("gpsimd", lambda e: e.collective_compute(
            "AllReduce", ALU.add, [list(range(NCORES))], [cc_kv_i[:, :]], [cc_kv_o[:, :]]),
            waits=[("g0", g2)], inc="cc")
        phkv.close()

        if upto < 4:
            raise _StopBuild()
        phg = ES2()
        sbg = lambda n, s, d: phg.enter_context(nc.sbuf_tensor(n, s, d))
        s_Srow = sbg("s_Srow", [1, T], F32)
        s_tmpr = sbg("s_tmpr", [1, T], F32)

        g3 = p.op("sync", lambda e: e.dma_start(
            out=s_cnts[:, :], in_=cc_cnt_o.rearrange("(r ee) one -> ee (r one)", ee=8),
            allow_slow_non_contiguous=True), waits=[("cc", cc1)], inc="g1", inck=16)
        p.op("vector", lambda e: e.scalar_tensor_tensor(
            s_cnts[:, :], s_cnts[:, :], 1.0, s_pref[:, :], ALU.mult, ALU.mult,
            accum_out=s_offs[:, 0:1]), waits=[("g1", g3)])
        p.op("gpsimd", lambda e: e.memset(s_loc[:, 0:1], 0.0))
        gp0b = p.mark("gpsimd", "gp")
        p.op("vector", lambda e: e.tensor_tensor_scan(
            s_loc[:, 1:T], s_m1T[:, 0:T - 1], s_m1T[:, 0:T - 1], 0.0, ALU.add, ALU.bypass),
            waits=[("gp", gp0b)])
        p.op("vector", lambda e: e.tensor_scalar_add(s_loc[:, :], s_loc[:, :], s_offs[:, 0:1]))
        if debug:
            s_dbgl1 = phg.enter_context(nc.sbuf_tensor("s_dbgl1", [8, T], F32))
            s_dbgmk = phg.enter_context(nc.sbuf_tensor("s_dbgmk", [8, T], F32))
            p.op("vector", lambda e: e.tensor_copy(s_dbgl1[:, :], s_loc[:, :]))
        vgap()
        p.op("vector", lambda e: e.scalar_tensor_tensor(
            s_m1T[:, :], s_loc[:, :], CAP, s_m1T[:, :], ALU.is_lt, ALU.mult))
        if debug:
            p.op("vector", lambda e: e.tensor_copy(s_dbgmk[:, :], s_m1T[:, :]))
        evm1k = p.mark("vector", "ev")
        p.op("gpsimd", lambda e: e.memset(s_loc[:, 0:1], 0.0), waits=[("ev", evm1k)])
        gp1 = p.mark("gpsimd", "gp")
        p.op("vector", lambda e: e.tensor_tensor_scan(
            s_loc[:, 1:T], s_m1T[:, 0:T - 1], s_m1T[:, 0:T - 1], 0.0, ALU.add, ALU.bypass),
            waits=[("gp", gp1)])
        ev21 = p.op("vector", lambda e: e.scalar_tensor_tensor(
            s_loc[:, :], s_loc[:, :], s_e512[:, 0:1], s_m1T[:, :], ALU.add, ALU.mult), inc="ev")
        for c in range(4):
            off = (c % 2) * 512
            dst = pA if c < 2 else pB
            p.op("tensor", lambda e, c=c, off=off, dst=dst: e.matmul(
                dst[0:1, off:off + 512], s_ones8[:, :], s_loc[:, c * 512:(c + 1) * 512],
                start=True, stop=True),
                waits=[("ev", ev21), ("ev", m("ev")), ("ac", m("ac")), ("gp", gp_set)]
            if c == 0 else ())
        mmS = p.mark("tensor", "mm")
        for c in range(4):
            off = (c % 2) * 512
            dst = pA if c < 2 else pB
            p.op("vector", lambda e, c=c, off=off, dst=dst: e.tensor_copy(
                s_Srow[:, c * 512:(c + 1) * 512], dst[0:1, off:off + 512]), waits=[("mm", mmS)])
        evSr = p.mark("vector", "ev")
        for c in range(4):
            off = (c % 2) * 512
            dst = pA if c < 2 else pB
            p.op("tensor", lambda e, c=c, off=off, dst=dst: e.matmul(
                dst[0:1, off:off + 512], s_ones8[:, :], s_m1T[:, c * 512:(c + 1) * 512],
                start=True, stop=True), waits=[("ev", evSr)] if c == 0 else ())
        mmK = p.mark("tensor", "mm")
        for c in range(4):
            off = (c % 2) * 512
            dst = pA if c < 2 else pB
            p.op("vector", lambda e, c=c, off=off, dst=dst: e.tensor_copy(
                s_tmpr[:, c * 512:(c + 1) * 512], dst[0:1, off:off + 512]), waits=[("mm", mmK)])
        vgap()
        p.op("vector", lambda e: e.scalar_tensor_tensor(
            s_Srow[:, :], s_tmpr[:, :], -SBIG, s_Srow[:, :], ALU.mult, ALU.add))
        vgap()
        ev22 = p.op("vector", lambda e: e.tensor_scalar_add(s_Srow[:, :], s_Srow[:, :], SBIG), inc="ev")
        evprev = ev22
        for t in range(NT):
            mmt = p.op("tensor", lambda e, t=t: e.transpose(
                pTf[:, 0:1], s_Srow[:, t * 128:(t + 1) * 128], s_id[0:1, 0:1]),
                waits=[("ev", evprev)], inc="mm")
            p.op("vector", lambda e, t=t: e.tensor_copy(s_Si[:, t:t + 1], pTf[:, 0:1]),
                 waits=[("mm", mmt)])
            vgap()
            p.op("vector", lambda e, t=t: e.tensor_single_scalar(
                s_Fi[:, t:t + 1], s_Si[:, t:t + 1], 127, ALU.bitwise_and))
            p.op("vector", lambda e, t=t: e.tensor_single_scalar(
                s_Si2[:, t:t + 1], s_Si[:, t:t + 1], 7, ALU.logical_shift_right))
            p.op("vector", lambda e, t=t: e.tensor_single_scalar(
                s_Fi[:, t:t + 1], s_Fi[:, t:t + 1], 5, ALU.logical_shift_left))
            vgap()
            evprev = p.op("vector", lambda e, t=t: e.tensor_add(
                s_Fi[:, t:t + 1], s_Fi[:, t:t + 1], s_Si2[:, t:t + 1]), inc="ev")
        sf_ev = evprev
        p.op("gpsimd", lambda e: e.memset(s_tos[:, :], 1 << 28))
        gp2 = p.mark("gpsimd", "gp")
        g4 = p.op("sync", lambda e: e.dma_start(out=d_tos[:, :], in_=s_tos[:, :]), waits=[("gp", gp2)],
                  inc="g2", inck=16)
        g4 = p.op("sync", lambda e: e.dma_start(out=s_fi32[:, :], in_=d_tos[0:1, 0:1]),
                  waits=[("g2", m("g2"))], inc="g2", inck=16)
        last_sc = 0
        for t in range(NT):
            w = [("g2", g4), ("ev", sf_ev)] if t == 0 else []
            last_sc = p.op("gpsimd", lambda e, t=t: e.indirect_dma_start(
                d_tos.rearrange("pp (c one) -> (pp c) one", one=1),
                bass.IndirectOffsetOnAxis(s_Fi[:, t:t + 1], 0),
                s_iotok[:, t:t + 1], None, bounds_check=bc_reg(e, NSLOT - 1), oob_is_err=False),
                waits=w, inc="ts", inck=16)
        last_sc = p.op("gpsimd", lambda e: e.dma_start(out=s_fi32[:, :], in_=d_tos[0:1, 0:1]),
                       waits=[("ts", m("ts"))], inc="ts", inck=16)
        g5 = p.op("sync", lambda e: e.dma_start(out=s_tos[:, :], in_=d_tos[:, :]),
                  waits=[("ts", last_sc)], inc="g3", inck=16)
        if debug:
            p.op("sync", lambda e: e.dma_start(out=dbg["dbg_Srow"][:, :], in_=s_Srow[:, :]),
                 inc="dbg", inck=16)
            p.op("sync", lambda e: e.dma_start(out=dbg["dbg_cnts"][:, :], in_=s_cnts[:, :]),
                 inc="dbg", inck=16)
            p.op("sync", lambda e: e.dma_start(out=dbg["dbg_loc"][:, :], in_=s_loc[:, :]),
                 inc="dbg", inck=16)
            p.op("sync", lambda e: e.dma_start(out=dbg["dbg_g"][:, :], in_=s_gcol[:, :]),
                 inc="dbg", inck=16)
            p.op("sync", lambda e: e.dma_start(out=dbg["dbg_tos"][:, :], in_=s_tos[:, :]),
                 waits=[("g3", g5)], inc="dbg", inck=16)
            p.op("sync", lambda e: e.dma_start(out=dbg["dbg_xnT"][:, :], in_=d_xnT[:, :]),
                 waits=list(xnsp_tot), inc="dbg", inck=16)
            p.op("sync", lambda e: e.dma_start(out=dbg["dbg_offs"][:, :], in_=s_offs[:, :]),
                 inc="dbg", inck=16)
            p.op("sync", lambda e: e.dma_start(out=dbg["dbg_loc1"][:, :], in_=s_dbgl1[:, :]),
                 inc="dbg", inck=16)
            p.op("sync", lambda e: e.dma_start(out=dbg["dbg_m1k"][:, :], in_=s_dbgmk[:, :]),
                 inc="dbg", inck=16)
            p.op("sync", lambda e: e.dma_start(out=dbg["dbg_lgt"][:, :], in_=s_dbglgt[:, :]),
                 inc="dbg", inck=16)
        phg.close()
        mscope.close()
        p.barrier()

        if upto < 5:
            raise _StopBuild()
        phq = ES2()
        sbq = lambda n, s, d: phq.enter_context(nc.sbuf_tensor(n, s, d))
        s_poT = sbq("s_poT", [128, NF * T], BF16)
        s_zr = sbq("s_zr", [H, T], F32)
        s_bo = sbq("s_bo", [H, F], F32)
        s_kvb = sbq("s_kvb", [128, 8 * 65], BF16)
        s_dent = [sbq(f"s_dent{i}", [1, 512], F32) for i in range(2)]

        g6b = p.op("sync", lambda e: e.dma_start(out=s_bo[:, :], in_=bones[:, :]), inc="g4", inck=16)
        g6 = p.op("sync", lambda e: e.dma_start(out=s_kvst[:, :], in_=cc_kv_o[:, :]),
                  waits=[("cc", cc2)], inc="g5", inck=16)
        p.op("vector", lambda e: e.tensor_scalar_mul(s_kv[:, :], s_kvst[0:64, :], s_b0[0:64, 0:1]),
             waits=[("g5", g6)])
        p.op("vector", lambda e: e.tensor_scalar_mul(s_kvtmp[:, :], s_kvst[64:128, :],
             s_b1c[64:128, 0:1]))
        ev23 = p.op("vector", lambda e: e.tensor_add(s_kv[:, :], s_kv[:, :], s_kvtmp[:, :]), inc="ev")
        p.op("vector", lambda e: e.tensor_copy(
            s_kvb[0:64, :].rearrange("pp (pr c) -> pp pr c", c=65),
            s_kv.ap().rearrange("pp (pr two c) -> pp pr two c", two=2, c=65)[:, :, 0, :]))
        ev25 = p.op("vector", lambda e: e.tensor_copy(
            s_kvb[64:128, :].rearrange("pp (pr c) -> pp pr c", c=65),
            s_kv.ap().rearrange("pp (pr two c) -> pp pr two c", two=2, c=65)[:, :, 1, :]), inc="ev")
        if debug:
            p.op("sync", lambda e: e.dma_start(out=dbg["dbg_kv"][:, :], in_=s_kv[:, :]),
                 waits=[("ev", ev23)], inc="dbg", inck=16)

        qsub = ES2()
        s_wq = qsub.enter_context(nc.sbuf_tensor("s_wq", [128, NF * F], BF16))
        s_x1c = qsub.enter_context(nc.sbuf_tensor("s_x1c", [128, NF * 512], BF16))
        s_qsq = qsub.enter_context(nc.sbuf_tensor("s_qsq", [128, 512], F32))
        s_qph = qsub.enter_context(nc.sbuf_tensor("s_qph", [128, 512], BF16))

        wq_ld = []
        wq_fold = []
        for kt in range(NF):
            ww = []
            if kt >= 2:
                ww.append(("ev", wq_fold[kt - 2]))
            wq_ld.append(p.op("sync", lambda e, kt=kt: e.dma_start(
                out=s_wq[:, kt * F:(kt + 1) * F], in_=d_bq[0:F, kt * 128:(kt + 1) * 128], transpose=True),
                waits=ww, inc=f"qw{kt % 2}", inck=16))
            wq_fold.append(p.op("vector", lambda e, kt=kt: e.tensor_scalar_mul(
                s_wq[:, kt * F:(kt + 1) * F], s_wq[:, kt * F:(kt + 1) * F], s_s1[:, kt:kt + 1]),
                waits=[(f"qw{kt % 2}", wq_ld[kt])], inc="ev"))

        dent_dma = [0, 0]
        mm_q_prev = 0
        ld_x1c = [0] * 4
        for c in range(4):
            cwx = list(x1sp_tot) if c == 0 else [("mm", m("mm"))]
            ld_x1c[c] = p.op("sync", lambda e, c=c: e.dma_start(
                out=s_x1c.ap().rearrange("pp (f cc) -> pp f cc", f=NF),
                in_=d_x1T.rearrange("(f pp) tok -> pp f tok", pp=128)[:, :, c * 512:(c + 1) * 512]),
                waits=cwx, inc="xc", inck=16)
            for pr in range(NF):
                qw = [("ev", wq_fold[-1]), ("xc", ld_x1c[c]), ("ac", m("ac"))]
                for kt in range(NF):
                    p.op("tensor", lambda e, pr=pr, kt=kt, c=c: e.matmul(
                        pA[:, 0:512], s_wq[:, kt * F + pr * 128: kt * F + pr * 128 + 128],
                        s_x1c[:, kt * 512:(kt + 1) * 512],
                        start=(kt == 0), stop=(kt == NF - 1)), waits=qw if kt == 0 else ())
                mmq = p.mark("tensor", "mm")
                acq1 = p.op("scalar", lambda e: e.activation(s_qsq[:, :], pA[:, 0:512], AF.Square),
                            waits=[("mm", mmq)], inc="ac")
                acq2 = p.op("scalar", lambda e: e.activation(s_qph[:, :], s_qsq[:, :], AF.Exp,
                            scale=-0.5), inc="ac")
                nw_ = [("ac", acq2), ("ev", ev25), ("ev", m("ev"))]
                p.op("tensor", lambda e, pr=pr: e.matmul(
                    pC[0:65, 0:512], s_kvb[0:64, pr * 65:(pr + 1) * 65],
                    s_qph[0:64, :], start=True, stop=True), waits=nw_)
                p.op("tensor", lambda e, pr=pr: e.matmul(
                    pC[0:65, 512:1024], s_kvb[64:128, pr * 65:(pr + 1) * 65],
                    s_qph[64:128, :], start=True, stop=True))
                mmn = p.mark("tensor", "mm")
                p.op("vector", lambda e, pr=pr, c=c: e.tensor_copy(
                    fm(s_poT, pr, c * 512, (c + 1) * 512)[0:64, :], pC[0:64, 0:512]),
                    waits=[("mm", mmn)])
                p.op("vector", lambda e, pr=pr, c=c: e.tensor_copy(
                    fm(s_poT, pr, c * 512, (c + 1) * 512)[64:128, :], pC[0:64, 512:1024]))
                dw0 = [("dn0", dent_dma[0])] if dent_dma[0] else []
                p.op("vector", lambda e: e.tensor_copy(s_dent[0][:, :], pC[64:65, 0:512]), waits=dw0)
                dw1 = [("dn1", dent_dma[1])] if dent_dma[1] else []
                p.op("vector", lambda e: e.tensor_copy(s_dent[1][:, :], pC[64:65, 512:1024]), waits=dw1)
                for hh in range(2):
                    p.op("vector", lambda e, hh=hh: e.tensor_scalar_add(
                        s_dent[hh][:, :], s_dent[hh][:, :], 1e-6))
                    p.op("vector", lambda e, hh=hh: e.reciprocal(s_dent[hh][:, :], s_dent[hh][:, :]))
                    evz = p.op("vector", lambda e: e.tensor_copy(s_vgap[:, :], s_vgap[:, :]),
                               inc="ev")
                    dent_dma[hh] = p.op("scalar", lambda e, pr=pr, hh=hh, c=c: e.dma_start(
                        out=s_zr[2 * pr + hh:2 * pr + hh + 1, c * 512:(c + 1) * 512],
                        in_=s_dent[hh][:, :]), waits=[("ev", evz)], inc=f"dn{hh}", inck=16)
            mm_q_prev = m("mm")
        qsub.close()
        zw = [("dn0", m("dn0")), ("dn1", m("dn1")), ("ac", m("ac")), ("ev", m("ev"))]
        first = True
        for ft in range(NF):
            for c in range(4):
                dst = pA if (c % 2 == 0) else pB
                sl = (ft % 2) * 512
                mmz = p.op("tensor", lambda e, ft=ft, c=c, dst=dst, sl=sl: e.matmul(
                    dst[:, sl:sl + 512], s_bo[:, ft * 128:(ft + 1) * 128],
                    s_zr[:, c * 512:(c + 1) * 512], start=True, stop=True),
                    waits=zw if first else [("ev", m("ev"))], inc="mm")
                first = False
                p.op("vector", lambda e, ft=ft, c=c, dst=dst, sl=sl: e.tensor_mul(
                    fm(s_poT, ft, c * 512, (c + 1) * 512), fm(s_poT, ft, c * 512, (c + 1) * 512),
                    dst[:, sl:sl + 512]), waits=[("mm", mmz)])
                p.mark("vector", "ev")
        poz_ev = m("ev")
        if debug:
            p.op("sync", lambda e: e.dma_start(out=dbg["dbg_poT"].rearrange("(f pp) tok -> pp f tok", pp=128),
                 in_=s_poT.ap().rearrange("pp (f tok) -> pp f tok", f=NF)),
                 waits=[("ev", poz_ev)], inc="dbg", inck=16)

        if upto < 6:
            raise _StopBuild()
        pjsub = ES2()
        s_wp = pjsub.enter_context(nc.sbuf_tensor("s_wp", [128, NF * F], BF16))
        s_xnt = [pjsub.enter_context(nc.sbuf_tensor(f"s_xnt{i}", [128, 512], F32)) for i in range(2)]
        wp_ld = []
        wp_fold = []
        for kt in range(NF):
            ww = [("wc", wcast_done), ("mm", m("mm")), ("ac", m("ac")),
                  ("ev", m("ev"))] if kt == 0 else []
            if kt >= 2:
                ww.append(("ev", wp_fold[kt - 2]))
            wp_ld.append(p.op("sync", lambda e, kt=kt: e.dma_start(
                out=s_wp[:, kt * F:(kt + 1) * F], in_=d_bp[0:F, kt * 128:(kt + 1) * 128], transpose=True),
                waits=ww, inc=f"pw{kt % 2}", inck=16))
            wp_fold.append(p.op("vector", lambda e, kt=kt: e.tensor_mul(
                s_wp[:, kt * F:(kt + 1) * F], s_wp[:, kt * F:(kt + 1) * F], s_gmb[:, :]),
                waits=[(f"pw{kt % 2}", wp_ld[kt])], inc="ev"))
        ev_pj = {}
        for ot in range(NF):
            for c in range(4):
                idx = ot * 4 + c
                lw = ([("ev", wp_fold[-1])] + list(xnsp_tot)) if idx == 0 else []
                if idx >= 2:
                    lw.append(("ev", ev_pj[idx - 2]))
                xl = p.op("sync", lambda e, ot=ot, c=c: e.dma_start(
                    out=s_xnt[c % 2][:, :], in_=d_xnT[ot * 128:(ot + 1) * 128, c * 512:(c + 1) * 512]),
                    waits=lw, inc=f"pj{c % 2}", inck=16)
                dst = pA if c % 2 == 0 else pB
                sl = (ot % 2) * 512
                for kt in range(NF):
                    p.op("tensor", lambda e, ot=ot, c=c, kt=kt, dst=dst, sl=sl: e.matmul(
                        dst[:, sl:sl + 512], s_wp[:, kt * F + ot * 128: kt * F + ot * 128 + 128],
                        fm(s_poT, kt, c * 512, (c + 1) * 512),
                        start=(kt == 0), stop=(kt == NF - 1)),
                        waits=([("ev", wp_fold[-1])] if idx == 0 else [("ev", m("ev"))]) if kt == 0 else ())
                mmp = p.mark("tensor", "mm")
                ev_pj[idx] = p.op("vector", lambda e, ot=ot, c=c, dst=dst, sl=sl: e.scalar_tensor_tensor(
                    fm(s_bigB, ot, c * 512, (c + 1) * 512), dst[:, sl:sl + 512],
                    s_pgb[:, ot:ot + 1], s_xnt[c % 2][:, :], ALU.add, ALU.add),
                    waits=[("mm", mmp), (f"pj{c % 2}", xl)], inc="ev")
        if debug:
            p.op("sync", lambda e: e.dma_start(out=dbg["dbg_x2T"].rearrange("(f pp) tok -> pp f tok", pp=128),
                 in_=s_bigB.ap().rearrange("pp (f tok) -> pp f tok", f=NF)),
                 waits=[("ev", m("ev"))], inc="dbg", inck=16)
        pjsub.close()
        phq.close()
        kvscope.close()
        p.barrier()

        if upto < 7:
            raise _StopBuild()
        phc = ES2()
        sbc = lambda n, s, d: phc.enter_context(nc.sbuf_tensor(n, s, d))
        s_wcv = sbc("s_wcv", [128, 16 * F], BF16)
        s_xc1 = sbc("s_xc1", [128, NF * 512], BF16)
        s_hdd = sbc("s_hdd", [128, 16 * 512], BF16)
        s_xsq = sbc("s_xsq", [128, 512], F32)
        s_r2 = sbc("s_r2", [1, 512], F32)
        s_xnt2 = [sbc(f"s_xnt2{i}", [128, 512], F32) for i in range(2)]

        for c in range(4):
            prev_mm = m("mm")
            for ft in range(NF):
                acs = p.op("scalar", lambda e, ft=ft, c=c: e.activation(
                    s_xsq[:, :], fm(s_bigB, ft, c * 512, (c + 1) * 512), AF.Square),
                    waits=[("mm", prev_mm)], inc="ac")
                prev_mm = p.op("tensor", lambda e, ft=ft: e.matmul(
                    pC[0:1, 0:512], s_ones128[:, 0:1], s_xsq[:, :],
                    start=(ft == 0), stop=(ft == NF - 1)), waits=[("ac", acs)], inc="mm")
            acr = p.op("scalar", lambda e: e.activation(s_r2[:, :], pC[0:1, 0:512], AF.Sqrt),
                       waits=[("mm", prev_mm)], inc="ac")
            p.op("vector", lambda e: e.reciprocal(s_r2[:, :], s_r2[:, :]), waits=[("ac", acr)])
            vgap()
            evr = p.op("vector", lambda e: e.tensor_scalar_mul(s_r2[:, :], s_r2[:, :], RMS_C), inc="ev")
            mmb = p.op("tensor", lambda e: e.matmul(pC[:, 512:1024], s_onesr[:, :], s_r2[:, :],
                       start=True, stop=True), waits=[("ev", evr)], inc="mm")
            for ft in range(NF):
                p.op("vector", lambda e, ft=ft, c=c: e.tensor_mul(
                    s_xc1[:, ft * 512:(ft + 1) * 512], fm(s_bigB, ft, c * 512, (c + 1) * 512),
                    pC[:, 512:1024]), waits=[("mm", mmb)] if ft == 0 else ())
            evx = p.mark("vector", "ev")
            w1_ld = []
            w1f = []
            for kt in range(NF):
                w1w = []
                if kt == 0:
                    w1w = [("mm", m("mm"))] if c > 0 else [("wc", wcast_done)]
                if kt >= 2:
                    w1w.append(("ev", w1f[kt - 2]))
                w1_ld.append(p.op("sync", lambda e, kt=kt: e.dma_start(
                    out=s_wcv[:, kt * HID:(kt + 1) * HID], in_=d_b1[0:HID, kt * 128:(kt + 1) * 128],
                    transpose=True), waits=w1w, inc=f"w1{kt % 2}", inck=16))
                w1f.append(p.op("vector", lambda e, kt=kt: e.tensor_scalar_mul(
                    s_wcv[:, kt * HID:(kt + 1) * HID], s_wcv[:, kt * HID:(kt + 1) * HID],
                    s_s2[:, kt:kt + 1]), waits=[(f"w1{kt % 2}", w1_ld[kt])], inc="ev"))
            ac_prev = [0, 0]
            for ot in range(16):
                dst = pA if ot % 2 == 0 else pB
                hw = [("ev", w1f[-1]), ("ev", evx)] if ot == 0 else []
                if ot >= 2:
                    hw.append(("ac", ac_prev[ot % 2]))
                for kt in range(NF):
                    p.op("tensor", lambda e, ot=ot, kt=kt, dst=dst: e.matmul(
                        dst[:, 0:512], s_wcv[:, kt * HID + ot * 128: kt * HID + ot * 128 + 128],
                        s_xc1[:, kt * 512:(kt + 1) * 512], start=(kt == 0), stop=(kt == NF - 1)),
                        waits=hw if kt == 0 else ())
                mmc = p.mark("tensor", "mm")
                ac_prev[ot % 2] = p.op("scalar", lambda e, ot=ot, dst=dst: e.activation(
                    s_hdd[:, ot * 512:(ot + 1) * 512], dst[:, 0:512], AF.Gelu,
                    bias=s_c1b[:, ot:ot + 1]), waits=[("mm", mmc)], inc="ac")
            hdd_ac = m("ac")
            w2_ld = []
            w2f = []
            for kt in range(16):
                w2w = [("ac", hdd_ac), ("wc", wcast_done)] if kt == 0 else []
                if kt >= 2:
                    w2w.append(("ev", w2f[kt - 2]))
                w2_ld.append(p.op("sync", lambda e, kt=kt: e.dma_start(
                    out=s_wcv[:, kt * F:(kt + 1) * F], in_=d_b2[0:F, kt * 128:(kt + 1) * 128],
                    transpose=True), waits=w2w, inc=f"w2{kt % 2}", inck=16))
                w2f.append(p.op("vector", lambda e, kt=kt: e.tensor_mul(
                    s_wcv[:, kt * F:(kt + 1) * F], s_wcv[:, kt * F:(kt + 1) * F], s_gmb[:, :]),
                    waits=[(f"w2{kt % 2}", w2_ld[kt])], inc="ev"))
            ev_c2 = {}
            for ot in range(NF):
                xw = [("ev", w2f[-1])] if ot == 0 else []
                if ot >= 2:
                    xw.append(("ev", ev_c2[ot - 2]))
                xl = p.op("sync", lambda e, ot=ot, c=c: e.dma_start(
                    out=s_xnt2[ot % 2][:, :], in_=d_xnT[ot * 128:(ot + 1) * 128, c * 512:(c + 1) * 512]),
                    waits=xw + (xnsp_tot if (c == 0 and ot == 0) else []), inc=f"cv{ot % 2}", inck=16)
                dst = pA if ot % 2 == 0 else pB
                for kt in range(16):
                    p.op("tensor", lambda e, ot=ot, kt=kt, dst=dst: e.matmul(
                        dst[:, 512:1024], s_wcv[:, kt * F + ot * 128: kt * F + ot * 128 + 128],
                        s_hdd[:, kt * 512:(kt + 1) * 512], start=(kt == 0), stop=(kt == 15)),
                        waits=([("ev", w2f[-1])] if ot == 0 else [("ev", ev_c2.get(ot - 2, 0))])
                        if kt == 0 else ())
                mm2 = p.mark("tensor", "mm")
                p.op("vector", lambda e, ot=ot, c=c, dst=dst: e.scalar_tensor_tensor(
                    fm(s_bigB, ot, c * 512, (c + 1) * 512), dst[:, 512:1024], s_c2gb[:, ot:ot + 1],
                    fm(s_bigB, ot, c * 512, (c + 1) * 512), ALU.add, ALU.add),
                    waits=[("mm", mm2), (f"cv{ot % 2}", xl)])
                ev_c2[ot] = p.op("vector", lambda e, ot=ot, c=c: e.tensor_add(
                    fm(s_bigB, ot, c * 512, (c + 1) * 512), fm(s_bigB, ot, c * 512, (c + 1) * 512),
                    s_xnt2[ot % 2][:, :]), inc="ev")
        phc.close()
        if debug:
            p.op("sync", lambda e: e.dma_start(out=dbg["dbg_acc"].rearrange("(f pp) tok -> pp f tok", pp=128),
                 in_=s_bigB.ap().rearrange("pp (f tok) -> pp f tok", f=NF)),
                 waits=[("ev", m("ev"))], inc="dbg", inck=16)
        p.barrier()

        if upto < 8:
            raise _StopBuild()
        phr = ES2()
        sbr = lambda n, s, d: phr.enter_context(nc.sbuf_tensor(n, s, d))
        s_wri = [sbr(f"s_wri{i}", [128, 8 * 1024], BF16) for i in range(2)]
        s_rh = sbr("s_rh", [128, 32 * 512], BF16)
        s_wro = sbr("s_wro", [128, 32 * 128], BF16)
        s_sA = sbr("s_sA", [128, 512], F32)
        s_sB = sbr("s_sB", [128, 512], F32)
        s_rme = sbr("s_rme", [128, 512], BF16)

        def gelu_chain(psum_fn, bias_col, biass_col, out_fn, scrA, scrB, pre_waits):
            a1 = p.op("scalar", lambda e: e.activation(scrA[:, :], psum_fn(), AF.Square,
                      bias=biass_col, scale=GELU_S), waits=pre_waits, inc="ac")
            p.op("vector", lambda e: e.tensor_scalar_add(scrB[:, :], scrA[:, :], 1.0),
                 waits=[("ac", a1)])
            e2 = p.op("vector", lambda e: e.scalar_tensor_tensor(
                scrA[:, :], psum_fn(), bias_col, scrB[:, :], ALU.add, ALU.mult), inc="ev")
            a2 = p.op("scalar", lambda e: e.activation(scrB[:, :], scrA[:, :], AF.Sigmoid,
                      scale=GELU_A), waits=[("ev", e2)], inc="ac")
            e3 = p.op("vector", lambda e: e.scalar_tensor_tensor(
                out_fn(), psum_fn(), bias_col, scrB[:, :], ALU.add, ALU.mult),
                waits=[("ac", a2)], inc="ev")
            return e3

        wmm_prev = {}
        rm_prev = [None]
        for c in range(4):
            for g in range(4):
                glw = []
                if not (c == 0 and g < 2):
                    glw.append(("mm", wmm_prev.get((c, g), m("mm"))))
                gl = p.op("gpsimd", lambda e, g=g: e.dma_start(
                    out=s_wri[g % 2][:, :],
                    in_=ri_w.rearrange("(kt pp) i -> pp kt i", pp=128)[:, :, g * 1024:(g + 1) * 1024]),
                    waits=glw, inc=f"ri{g % 2}", inck=16)
                for it in range(8):
                    ig = g * 8 + it
                    dst = pA if it % 2 == 0 else pB
                    mw = [(f"ri{g % 2}", gl), ("ev", m("ev"))] if it == 0 else []
                    if it >= 2:
                        mw.append(("ev", m("ev")))
                    for kt in range(NF):
                        p.op("tensor", lambda e, g=g, it=it, kt=kt, dst=dst, c=c: e.matmul(
                            dst[:, 0:512], s_wri[g % 2][:, kt * 1024 + it * 128: kt * 1024 + it * 128 + 128],
                            fm(s_aoT, kt, c * 512, (c + 1) * 512), start=(kt == 0), stop=(kt == NF - 1)),
                            waits=mw if kt == 0 else ())
                    mmi = p.mark("tensor", "mm")
                    psf = (lambda dst: (lambda: dst[:, 0:512]))(dst)
                    outf = (lambda ig: (lambda: s_rh[:, ig * 512:(ig + 1) * 512]))(ig)
                    gelu_chain(psf, s_rib[:, ig:ig + 1], s_ribs[:, ig:ig + 1], outf, s_sA, s_sB,
                               [("mm", mmi)])
                wmm_prev[(c, g + 2)] = m("mm")
                wmm_prev[(c + 1, g - 2)] = m("mm")
            rh_ev = m("ev")
            for ot in range(NF):
                olw = [] if (c == 0 and ot == 0) else [("mm", m("mm"))]
                ol = p.op("gpsimd", lambda e, ot=ot: e.dma_start(
                    out=s_wro[:, :],
                    in_=ro_w.rearrange("(kt pp) o -> pp kt o", pp=128)[:, :, ot * 128:(ot + 1) * 128]),
                    waits=olw, inc="ro", inck=16)
                dst = pA if ot % 2 == 0 else pB
                ow = [("ro", ol), ("ev", rh_ev)]
                for kt in range(32):
                    p.op("tensor", lambda e, ot=ot, kt=kt, dst=dst: e.matmul(
                        dst[:, 512:1024], s_wro[:, kt * 128:(kt + 1) * 128],
                        s_rh[:, kt * 512:(kt + 1) * 512], start=(kt == 0), stop=(kt == 31)),
                        waits=ow if kt == 0 else ())
                mmo = p.mark("tensor", "mm")
                acw2 = [("mm", mmo)]
                if rm_prev[0] is not None:
                    acw2.append(rm_prev[0])
                aco = p.op("scalar", lambda e, ot=ot, dst=dst: e.activation(
                    s_rme[:, :], dst[:, 512:1024], AF.Identity, bias=s_rob[:, ot:ot + 1]),
                    waits=acw2, inc="ac")
                rmk = p.op("scalar", lambda e, ot=ot, c=c: e.dma_start(
                    out=d_rm[ot * 128:(ot + 1) * 128, c * 512:(c + 1) * 512], in_=s_rme[:, :]),
                    waits=[("ac", aco)], inc=f"rm{ot % 2}", inck=16)
                rm_prev[0] = (f"rm{ot % 2}", rmk)
        p.op("scalar", lambda e: e.dma_start(out=s_fbf[:, :], in_=d_rm[0:1, 0:1]),
             waits=[("rm0", m("rm0")), ("rm1", m("rm1"))], inc="rm0", inck=16)
        phr.close()
        p.barrier()

        if upto < 9:
            raise _StopBuild()
        phe = ES2()
        sbe = lambda n, s, d: phe.enter_context(nc.sbuf_tensor(n, s, d))
        s_dt = [sbe(f"s_dt{i}", [128, F], BF16) for i in range(2)]
        s_dispT = sbe("s_dispT", [128, NF * 512], BF16)
        s_h = sbe("s_h", [128, 32 * 512], BF16)
        s_wie = [sbe(f"s_wie{i}", [128, 8 * 1024], BF16) for i in range(2)]
        s_wo2 = sbe("s_wo2", [128, 8 * 512], BF16)
        s_eA = sbe("s_eA", [128, 512], F32)
        s_eB = sbe("s_eB", [128, 512], F32)
        s_eo = sbe("s_eo", [128, 512], BF16)
        s_opbe = sbe("s_opbe", [1, F], BF16)

        mm_dtr_prev = 0
        ev_dtr_lastexp = 0
        eo_prev = [None]
        for ee in range(8):
            opw = [("mm", m("mm"))] if ee > 0 else [("g3", g5)]
            opl = p.op("gpsimd", lambda e, ee=ee: e.dma_start(
                out=s_opbe[:, :], in_=opb_d[ee:ee + 1, :]), waits=opw, inc="ob", inck=16)
            ev_dtr = 0
            for sti in range(4):
                st = ee * 4 + sti
                gw = []
                if ee == 0 and sti == 0:
                    gw = [("g3", g5), ("a0", m("a0")), ("a1", m("a1"))]
                if sti >= 2:
                    gw.append(("mm", m("mm")))
                elif ee > 0:
                    gw.append(("mm", mm_dtr_prev))
                p.op("gpsimd", lambda e, sti=sti: e.memset(s_dt[sti % 2][:, :], 0.0), waits=gw)
                gpm = p.mark("gpsimd", "gp")
                gath = p.op("gpsimd", lambda e, sti=sti, st=st: e.indirect_dma_start(
                    s_dt[sti % 2][:, :], None, d_ao[:, :],
                    bass.IndirectOffsetOnAxis(s_tos[:, st:st + 1], 0),
                    bounds_check=bc_reg(e, T - 1), oob_is_err=False), waits=[("gp", gpm)],
                    inc=f"dt{sti % 2}", inck=16)
                tw = [(f"dt{sti % 2}", gath)]
                if sti > 0:
                    tw.append(("ev", ev_dtr))
                elif ee > 0:
                    tw.append(("ev", ev_dtr_lastexp))
                for ft in range(NF):
                    p.op("tensor", lambda e, sti=sti, ft=ft: e.transpose(
                        pTb[:, ft * 128:(ft + 1) * 128], s_dt[sti % 2][:, ft * 128:(ft + 1) * 128],
                        s_idb[:, :]), waits=tw if ft == 0 else ())
                mmtr = p.mark("tensor", "mm")
                cpw = [("mm", mmtr)]
                if sti == 0 and ee > 0:
                    cpw.append(("mm", mm_dtr_prev))
                ev_dtr = p.op("vector", lambda e, sti=sti: e.tensor_copy(
                    s_dispT.ap().rearrange("pp (f s) -> pp f s", f=NF)[:, :, sti * 128:(sti + 1) * 128],
                    pTb.ap().rearrange("pp (f cc) -> pp f cc", f=NF)), waits=cpw, inc="ev")
            ev_dtr_lastexp = ev_dtr
            mm_dtr_prev = m("mm")
            disp_ev = ev_dtr
            for g in range(4):
                glw = []
                if not (ee == 0 and g < 2):
                    glw.append(("mm", m("mm")))
                gl = p.op("gpsimd", lambda e, ee=ee, g=g: e.dma_start(
                    out=s_wie[g % 2][:, :],
                    in_=inter_w[ee * F:(ee + 1) * F, :].rearrange("(kt pp) i -> pp kt i", pp=128)
                    [:, :, g * 1024:(g + 1) * 1024]), waits=glw, inc=f"wi{g % 2}", inck=16)
                for it in range(8):
                    ig = g * 8 + it
                    dst = pA if it % 2 == 0 else pB
                    mw = [(f"wi{g % 2}", gl), ("ev", disp_ev), ("ev", m("ev"))] if it == 0 else []
                    if it >= 2:
                        mw.append(("ev", m("ev")))
                    for kt in range(NF):
                        p.op("tensor", lambda e, g=g, it=it, kt=kt, dst=dst: e.matmul(
                            dst[:, 0:512], s_wie[g % 2][:, kt * 1024 + it * 128: kt * 1024 + it * 128 + 128],
                            s_dispT[:, kt * 512:(kt + 1) * 512], start=(kt == 0), stop=(kt == NF - 1)),
                            waits=mw if kt == 0 else ())
                    mmi = p.mark("tensor", "mm")
                    psf = (lambda dst: (lambda: dst[:, 0:512]))(dst)
                    outf = (lambda ig: (lambda: s_h[:, ig * 512:(ig + 1) * 512]))(ig)
                    gelu_chain(psf, s_ib[:, ee * 32 + ig: ee * 32 + ig + 1],
                               s_ibs[:, ee * 32 + ig: ee * 32 + ig + 1], outf, s_eA, s_eB,
                               [("mm", mmi)])
            h_ev = m("ev")
            for oh in range(2):
                for kg in range(4):
                    olw = []
                    if not (ee == 0 and oh == 0 and kg == 0):
                        olw.append(("mm", m("mm")))
                    ol = p.op("gpsimd", lambda e, ee=ee, kg=kg, oh=oh: e.dma_start(
                        out=s_wo2[:, :],
                        in_=output_w[ee * INTER + kg * 1024: ee * INTER + (kg + 1) * 1024,
                                     oh * 512:(oh + 1) * 512]
                        .rearrange("(kt pp) o -> pp kt o", pp=128)), waits=olw, inc="wo", inck=16)
                    for kt in range(8):
                        kglob = kg * 8 + kt
                        for mt in range(4):
                            dstp = pA if mt < 2 else pB
                            doff = (mt % 2) * 512
                            mw = []
                            if kglob == 0 and mt == 0:
                                mw = [("wo", ol), ("ev", h_ev), ("ev", m("ev"))]
                            elif kt == 0 and mt == 0:
                                mw = [("wo", ol)]
                            p.op("tensor", lambda e, kg=kg, kt=kt, mt=mt, dstp=dstp, doff=doff,
                                 kglob=kglob: e.matmul(
                                dstp[:, doff:doff + 512],
                                s_h[:, kglob * 512 + mt * 128: kglob * 512 + mt * 128 + 128],
                                s_wo2[:, kt * 512:(kt + 1) * 512],
                                start=(kglob == 0), stop=False), waits=mw)
                    p.mark("tensor", "mm")
                for mt in range(4):
                    dstp = pA if mt < 2 else pB
                    doff = (mt % 2) * 512
                    bw = [("ob", opl), ("gp", gp_set)] if mt == 0 else []
                    p.op("tensor", lambda e, oh=oh, mt=mt, dstp=dstp, doff=doff: e.matmul(
                        dstp[:, doff:doff + 512], s_onesrb[:, :],
                        s_opbe[:, oh * 512:(oh + 1) * 512], start=False, stop=True), waits=bw)
                    mme = p.mark("tensor", "mm")
                    evw = [("mm", mme)]
                    if eo_prev[0] is not None:
                        evw.append(eo_prev[0])
                    eve = p.op("vector", lambda e, dstp=dstp, doff=doff: e.tensor_copy(
                        s_eo[:, :], dstp[:, doff:doff + 512]), waits=evw, inc="ev")
                    eon = f"eo{(oh * 4 + mt) % 2}"
                    eok = p.op("scalar", lambda e, ee=ee, oh=oh, mt=mt: e.dma_start(
                        out=d_eo[ee * 512 + mt * 128: ee * 512 + (mt + 1) * 128,
                                 oh * 512:(oh + 1) * 512], in_=s_eo[:, :]),
                        waits=[("ev", eve)], inc=eon, inck=16)
                    eo_prev[0] = (eon, eok)
        p.op("scalar", lambda e: e.dma_start(out=s_fbf[:, :], in_=d_eo[0:1, 0:1]),
             waits=[("eo0", m("eo0")), ("eo1", m("eo1"))], inc="eo0", inck=16)
        if debug:
            p.op("sync", lambda e: e.dma_start(out=dbg["dbg_dispT"][:, :], in_=s_dispT[:, :]),
                 waits=[("ev", m("ev")), ("mm", m("mm"))], inc="dbg", inck=16)
            p.op("sync", lambda e: e.dma_start(out=dbg["dbg_h"][:, :], in_=s_h[:, :]),
                 waits=[("ev", m("ev")), ("mm", m("mm"))], inc="dbg", inck=16)
        phe.close()
        p.barrier()
        if debug:
            p.op("sync", lambda e: e.dma_start(out=dbg["dbg_eo"][:, :], in_=d_eo[:, :]),
                 waits=[("eo0", m("eo0")), ("eo1", m("eo1"))], inc="dbg", inck=16)
            p.op("sync", lambda e: e.dma_start(out=dbg["dbg_ao"][:, :], in_=d_ao[:, :]),
                 inc="dbg", inck=16)
            p.op("sync", lambda e: e.dma_start(out=dbg["dbg_rm"][:, :], in_=d_rm[:, :]),
                 waits=[("rm0", m("rm0")), ("rm1", m("rm1"))], inc="dbg", inck=16)
            p.op("sync", lambda e: e.dma_start(out=dbg["dbg_Si"][:, :], in_=s_Si[:, :]),
                 inc="dbg", inck=16)
            p.op("sync", lambda e: e.dma_start(out=dbg["dbg_gc1"][:, :], in_=s_gc1[:, :]),
                 inc="dbg", inck=16)
            p.op("sync", lambda e: e.dma_start(out=dbg["dbg_ib"][:, :], in_=s_ib[:, :]),
                 inc="dbg", inck=16)
            p.op("sync", lambda e: e.dma_start(out=dbg["dbg_iotok"][:, :], in_=s_iotok[:, :]),
                 inc="dbg", inck=16)
            p.op("sync", lambda e: e.dma_start(out=dbg["dbg_Fi"][:, :], in_=s_Fi[:, :]),
                 inc="dbg", inck=16)
            p.op("sync", lambda e: e.dma_start(out=dbg["dbg_c0"][:, :], in_=s_c0[:, :]),
                 inc="dbg", inck=16)

        if upto < 10:
            raise _StopBuild()
        phf = ES2()
        sbf = lambda n, s, d: phf.enter_context(nc.sbuf_tensor(n, s, d))
        f_x = [sbf(f"f_x{i}", [128, F], F32) for i in range(2)]
        f_rm = [sbf(f"f_rm{i}", [128, F], BF16) for i in range(2)]
        f_moe = [sbf(f"f_moe{i}", [128, F], BF16) for i in range(2)]
        f_o1 = sbf("f_o1", [128, F], F32)
        f_out = [sbf(f"f_out{i}", [128, F], F32) for i in range(2)]

        ld_fx = [0] * NT
        ld_frm = [0] * NT
        wq_moe = [0] * NT
        ev_o2 = [0] * NT
        ev_done = [0] * NT
        fin_dma = [0] * NT
        for t in range(NT):
            par = t % 2
            xw = [("ev", ev_done[t - 2])] if t >= 2 else []
            ld_fx[t] = p.op("sync", lambda e, t=t, par=par: e.dma_start(
                out=f_x[par][:, :], in_=x_sh[t * 128:(t + 1) * 128, :]), waits=xw,
                inc=f"fx{par}", inck=16)
            frmw = [("rm0", m("rm0")), ("rm1", m("rm1"))] if t == 0 else (
                [("mm", m("mm"))] if t >= 2 else [])
            ld_frm[t] = p.op("sync", lambda e, t=t, par=par: e.dma_start(
                out=f_rm[par].ap().rearrange("pp (f c) -> pp f c", f=NF),
                in_=d_rm.rearrange("(f pp) tok -> pp f tok", pp=128)[:, :, t * 128:(t + 1) * 128]),
                waits=frmw, inc=f"fr{par}", inck=16)
            gw = [("ev", ev_o2[t - 2])] if t >= 2 else (
                [("eo0", m("eo0")), ("eo1", m("eo1"))] if t == 0 else [])
            p.op("gpsimd", lambda e, par=par: e.memset(f_moe[par][:, :], 0.0), waits=gw)
            gpm = p.mark("gpsimd", "gp")
            wq_moe[t] = p.op("gpsimd", lambda e, t=t, par=par: e.indirect_dma_start(
                f_moe[par][:, :], None, d_eo[:, :],
                bass.IndirectOffsetOnAxis(s_Si[:, t:t + 1], 0),
                bounds_check=bc_reg(e, NSLOT - 1), oob_is_err=False), waits=[("gp", gpm)],
                inc=f"mo{par}", inck=16)
            tw = [(f"fr{par}", ld_frm[t])]
            if t >= 1:
                tw.append(("ev", ev_o2[t - 1]))
            for ft in range(NF):
                p.op("tensor", lambda e, par=par, ft=ft: e.transpose(
                    pTb[:, ft * 128:(ft + 1) * 128], f_rm[par][:, ft * 128:(ft + 1) * 128], s_idb[:, :]),
                    waits=tw if ft == 0 else ())
            mmr = p.mark("tensor", "mm")
            p.op("vector", lambda e, t=t, par=par: e.scalar_tensor_tensor(
                f_o1[:, :], pTb[:, :], s_c0[:, t:t + 1], f_x[par][:, :], ALU.mult, ALU.add),
                waits=[("mm", mmr), (f"fx{par}", ld_fx[t]), (f"mo{par}", wq_moe[t])])
            ev_o2[t] = p.op("vector", lambda e, t=t, par=par: e.scalar_tensor_tensor(
                f_o1[:, :], f_moe[par][:, :], s_gc1[:, t:t + 1], f_o1[:, :], ALU.mult, ALU.add),
                inc="ev")
            for r in range(2):
                tw2 = [("ev", ev_o2[t])] if r == 0 else [("ev", m("ev"))]
                if t >= 1 and r == 0:
                    tw2.append(("ev", ev_done[t - 1]))
                if t >= 2 and r == 0:
                    tw2.append((f"fy{par}", fin_dma[t - 2]))
                for j in range(4):
                    ft = r * 4 + j
                    p.op("tensor", lambda e, ft=ft, j=j, t=t: e.transpose(
                        pTf[:, j * 128:(j + 1) * 128], fm(s_bigB, ft, t * 128, (t + 1) * 128),
                        s_id[:, :]), waits=tw2 if j == 0 else ())
                mmr2 = p.mark("tensor", "mm")
                p.op("vector", lambda e, r=r, t=t, par=par: e.scalar_tensor_tensor(
                    f_out[par][:, r * 512:(r + 1) * 512], pTf[:, :], 1.0,
                    f_o1[:, r * 512:(r + 1) * 512], ALU.mult, ALU.add), waits=[("mm", mmr2)])
                ev_done[t] = p.mark("vector", "ev")
            fw = [("ev", ev_done[t])]
            if t >= 2:
                fw.append((f"fy{par}", fin_dma[t - 2]))
            fin_dma[t] = p.op("sync", lambda e, t=t, par=par: e.dma_start(
                out=y[t * 128:(t + 1) * 128, :], in_=f_out[par][:, :]), waits=fw,
                inc=f"fy{par}", inck=16)
        p.op("sync", lambda e: e.nop(), waits=[("fy0", m("fy0")), ("fy1", m("fy1"))])
        phf.close()

    except _StopBuild:
        pass
    p.op("sync", lambda e: e.nop(),
         waits=[(s, p.m(s)) for s in list(p.CNT) if s not in ("bar",)])
    p.replay()
    try:
        es.close()
    except AssertionError:
        pass
    return nc, dbg


def _host_inputs(inputs, core):
    f32 = np.float32
    d = {}
    d["x_sh"] = inputs["L0_attn_x"].reshape(-1, F)[core * T:(core + 1) * T]
    d["qkv_w"] = inputs["L0_attn_qkv_w"]
    d["proj_w"] = inputs["L0_attn_proj_w"]
    d["conv1_w"] = inputs["L0_attn_conv1_w"]
    d["conv2_w"] = inputs["L0_attn_conv2_w"]
    d["inter_w"] = inputs["L0_ffn_inter_w"].reshape(8 * F, INTER)
    d["output_w"] = inputs["L0_ffn_output_w"].reshape(8 * INTER, F)
    d["ri_w"] = inputs["L0_ffn_res_inter_w"]
    d["ro_w"] = inputs["L0_ffn_res_output_w"]
    d["wgcat"] = np.concatenate([inputs["L0_ffn_wg"], inputs["L0_ffn_res_coef"]], axis=1)
    d["ident"] = np.eye(128, dtype=f32)
    d["iotok"] = np.arange(T, dtype=np.int32).reshape(NT, 128).T
    d["e512c"] = (np.arange(8, dtype=f32) * L).reshape(8, 1)
    d["prefm"] = np.broadcast_to((np.arange(8) < core).astype(f32)[None, :], (8, 8))
    bo = np.zeros((H, F), dtype=f32)
    for g in range(H):
        bo[g, g * 64:(g + 1) * 64] = 1.0
    d["bones"] = bo
    b = core // 4
    bm = np.zeros((128, 1), dtype=f32)
    bm[b * 64:(b + 1) * 64] = 1.0
    d["bmask"] = bm
    sel0 = np.zeros((128, 1), dtype=f32); sel0[0:64] = 1.0 if b == 0 else 0.0
    sel1 = np.zeros((128, 1), dtype=f32); sel1[64:128] = 1.0 if b == 1 else 0.0
    d["bsel0"] = sel0
    d["bsel1"] = sel1
    col = lambda v: v.reshape(-1, 128).T
    d["s1c"] = col(inputs["L0_attn_scale1"])
    d["s2c"] = col(inputs["L0_attn_scale2"])
    d["gmc"] = col(inputs["L0_attn_gamma"])
    d["nwc"] = col(inputs["L0_ffn_attn_nw"])
    d["nbc"] = col(inputs["L0_ffn_attn_nb"])
    d["pbc"] = col(inputs["L0_attn_proj_b"])
    d["c1bc"] = col(inputs["L0_attn_conv1_b"])
    d["c2bc"] = col(inputs["L0_attn_conv2_b"])
    d["ibc"] = inputs["L0_ffn_inter_b"].reshape(8, 32, 128).transpose(2, 0, 1).reshape(128, 8 * 32)
    d["ribc"] = col(inputs["L0_ffn_res_inter_b"])
    d["robc"] = col(inputs["L0_ffn_res_output_b"])
    d["nwr"] = inputs["L0_ffn_attn_nw"].reshape(1, F)
    d["nbr"] = inputs["L0_ffn_attn_nb"].reshape(1, F)
    d["gmr"] = inputs["L0_attn_gamma"].reshape(1, F)
    d["opb"] = inputs["L0_ffn_output_b"]
    return {k: np.ascontiguousarray(np.asarray(v)) for k, v in d.items()}


def _build_upto(debug, upto):
    try:
        return build_nc(debug=debug, upto=upto)
    except _StopBuild:
        raise


def _run(inputs, debug=False, upto=99):
    nc, dbg = build_nc(debug=debug, upto=upto)
    in_maps = [_host_inputs(inputs, c) for c in range(NCORES)]
    res = run_bass_kernel_spmd(nc, in_maps, core_ids=list(range(NCORES)))
    outs = [np.asarray(res.results[c]["y"]) for c in range(NCORES)]
    yfull = np.concatenate(outs, axis=0).reshape(2, 8192, F)
    if debug:
        return yfull, res
    return yfull


def kernel(**inputs):
    return _run(inputs, debug=False)


# version 05 (final)
# speedup vs baseline: 1.2161x; 1.2161x over previous
import numpy as np
import concourse.bass as bass
import concourse.mybir as mybir
from concourse.bass_utils import run_bass_kernel_spmd

F32 = mybir.dt.float32
BF16 = mybir.dt.bfloat16
I32 = mybir.dt.int32
AF = mybir.ActivationFunctionType
ALU = mybir.AluOpType
AX = mybir.AxisListType

NCORES = 8
T = 2048
NT = 16
F = 1024
NF = 8
H = 16
CAP = 2048.0
L = 512
NSLOT = 8 * L
NST = NSLOT // 128
INTER = 4096
HID = 2048
SBIG = 1048576.0
GELU_S = float(np.float32(np.sqrt(0.044715)))
GELU_A = float(np.float32(2.0 * np.sqrt(2.0 / np.pi)))
RMS_C = float(np.float32(1024.0 ** 0.5 + 1e-6))

ENGN = ["sync", "scalar", "vector", "tensor", "gpsimd"]


class Prog:
    def __init__(self, nc, es):
        self.nc = nc
        self.es = es
        self.q = {en: [] for en in ENGN}
        names = ["mm", "ev", "ac", "gp", "cc", "bar"]
        self.SEM = {n: es.enter_context(nc.semaphore(n)) for n in names}
        self.CNT = {n: 0 for n in names}

    def sem(self, name):
        if name not in self.SEM:
            self.SEM[name] = self.es.enter_context(self.nc.semaphore(name))
            self.CNT[name] = 0
        return name

    def op(self, eng, fn, waits=(), inc=None, inck=1):
        for s, v in waits:
            self.sem(s)
        wl = [(s, v) for (s, v) in waits if v > 0]
        if inc is not None:
            self.sem(inc)
            self.CNT[inc] += inck
        SEM = self.SEM

        def run(e, fn=fn, wl=wl, inc=inc, inck=inck):
            for s, v in wl:
                e.wait_ge(SEM[s], v)
            inst = fn(e)
            if inc is not None:
                inst.then_inc(SEM[inc], inck)
        self.q[eng].append(run)
        return self.CNT[inc] if inc is not None else None

    def m(self, name):
        return self.CNT[name]

    def mark(self, eng, sem, waits=()):
        self.sem(sem)
        for s, v in waits:
            self.sem(s)
        self.CNT[sem] += 1
        SEM = self.SEM
        wl = [(s, v) for (s, v) in waits if v > 0]

        def run(e, wl=wl, sem=sem):
            for s, v in wl:
                e.wait_ge(SEM[s], v)
            e.drain().then_inc(SEM[sem], 1)
        self.q[eng].append(run)
        return self.CNT[sem]

    def barrier(self):
        self.CNT["bar"] += 5
        tgt = self.CNT["bar"]
        for en in ENGN:
            def run(e, tgt=tgt):
                e.drain().then_inc(self.SEM["bar"], 1)
                e.wait_ge(self.SEM["bar"], tgt)
            self.q[en].append(run)

    def replay(self):
        with self.nc.Block() as blk:
            for en in ENGN:
                def section(e, en=en):
                    for f in self.q[en]:
                        f(e)
                getattr(blk, en)(section)


class _StopBuild(Exception):
    pass


def build_nc(debug=False, upto=99, p0_tiles=16):
    nc = bass.Bass("TRN2", target_bir_lowering=False)
    dtn = nc.dram_tensor

    x_sh = dtn("x_sh", [T, F], F32, kind="ExternalInput").ap()
    qkv_w = dtn("qkv_w", [3 * F, F], F32, kind="ExternalInput").ap()
    proj_w = dtn("proj_w", [F, F], F32, kind="ExternalInput").ap()
    conv1_w = dtn("conv1_w", [HID, F], F32, kind="ExternalInput").ap()
    conv2_w = dtn("conv2_w", [F, HID], F32, kind="ExternalInput").ap()
    inter_w = dtn("inter_w", [8 * F, INTER], F32, kind="ExternalInput").ap()
    output_w = dtn("output_w", [8 * INTER, F], F32, kind="ExternalInput").ap()
    ri_w = dtn("ri_w", [F, INTER], F32, kind="ExternalInput").ap()
    ro_w = dtn("ro_w", [INTER, F], F32, kind="ExternalInput").ap()
    wgcat = dtn("wgcat", [F, 10], F32, kind="ExternalInput").ap()
    ident = dtn("ident", [128, 128], F32, kind="ExternalInput").ap()
    iotok = dtn("iotok", [128, NT], I32, kind="ExternalInput").ap()
    e512c = dtn("e512c", [8, 1], F32, kind="ExternalInput").ap()
    prefm = dtn("prefm", [8, 8], F32, kind="ExternalInput").ap()
    bones = dtn("bones", [H, F], F32, kind="ExternalInput").ap()
    bmask = dtn("bmask", [128, 1], F32, kind="ExternalInput").ap()
    bsel0 = dtn("bsel0", [128, 1], F32, kind="ExternalInput").ap()
    bsel1 = dtn("bsel1", [128, 1], F32, kind="ExternalInput").ap()
    s1c_d = dtn("s1c", [128, NF], F32, kind="ExternalInput").ap()
    s2c_d = dtn("s2c", [128, NF], F32, kind="ExternalInput").ap()
    gmc_d = dtn("gmc", [128, NF], F32, kind="ExternalInput").ap()
    nwc_d = dtn("nwc", [128, NF], F32, kind="ExternalInput").ap()
    nbc_d = dtn("nbc", [128, NF], F32, kind="ExternalInput").ap()
    pbc_d = dtn("pbc", [128, NF], F32, kind="ExternalInput").ap()
    c1bc_d = dtn("c1bc", [128, HID // 128], F32, kind="ExternalInput").ap()
    c2bc_d = dtn("c2bc", [128, NF], F32, kind="ExternalInput").ap()
    ibc_d = dtn("ibc", [128, 8 * 32], F32, kind="ExternalInput").ap()
    ribc_d = dtn("ribc", [128, 32], F32, kind="ExternalInput").ap()
    robc_d = dtn("robc", [128, NF], F32, kind="ExternalInput").ap()
    nwr_d = dtn("nwr", [1, F], F32, kind="ExternalInput").ap()
    nbr_d = dtn("nbr", [1, F], F32, kind="ExternalInput").ap()
    gmr_d = dtn("gmr", [1, F], F32, kind="ExternalInput").ap()
    opb_d = dtn("opb", [8, F], F32, kind="ExternalInput").ap()

    y = dtn("y", [T, F], F32, kind="ExternalOutput").ap()

    d_bq = dtn("d_bq", [3 * F, F], BF16).ap()
    d_bp = dtn("d_bp", [F, F], BF16).ap()
    d_b1 = dtn("d_b1", [HID, F], BF16).ap()
    d_b2 = dtn("d_b2", [F, HID], BF16).ap()
    d_xnT = dtn("d_xnT", [F, T], F32).ap()
    d_x1T = dtn("d_x1T", [F, T], BF16).ap()
    d_ao = dtn("d_ao", [T, F], BF16).ap()
    d_rm = dtn("d_rm", [F, T], BF16).ap()
    d_eo = dtn("d_eo", [NSLOT, F], BF16).ap()
    d_tos = dtn("d_tos", [128, NST], I32).ap()
    cc_cnt_i = dtn("cc_cnt_i", [8, 1], F32).ap()
    cc_cnt_o = dtn("cc_cnt_o", [8 * NCORES, 1], F32).ap()
    cc_kv_i = dtn("cc_kv_i", [128, 1040], F32).ap()
    cc_kv_o = dtn("cc_kv_o", [128, 1040], F32).ap()

    dbg = {}
    if debug:
        for name, shape, ddt in [
            ("dbg_xnT", [F, T], F32), ("dbg_Srow", [1, T], F32),
            ("dbg_kv", [64, 1040], F32), ("dbg_poT", [F, T], BF16),
            ("dbg_x2T", [F, T], F32), ("dbg_cnts", [8, 8], F32),
            ("dbg_loc", [8, T], F32), ("dbg_acc", [F, T], F32),
            ("dbg_g", [128, NT], F32), ("dbg_tos", [128, NST], I32),
            ("dbg_eo", [NSLOT, F], BF16), ("dbg_ao", [T, F], BF16),
            ("dbg_dispT", [128, 8 * 512], BF16), ("dbg_h", [128, 32 * 512], BF16),
            ("dbg_rm", [F, T], BF16), ("dbg_Si", [128, NT], I32),
            ("dbg_gc1", [128, NT], F32), ("dbg_c0", [128, NT], F32),
            ("dbg_A", [128, 3 * NT], F32), ("dbg_V", [128, NT], F32),
            ("dbg_offs", [8, 1], F32), ("dbg_loc1", [8, T], F32),
            ("dbg_m1k", [8, T], F32), ("dbg_lgt", [128, 10 * NT], F32),
            ("dbg_ib", [128, 8 * 32], F32), ("dbg_iotok", [128, NT], I32),
            ("dbg_Fi", [128, NT], I32),
        ]:
            dbg[name] = dtn(name, shape, ddt, kind="ExternalOutput").ap()

    from contextlib import ExitStack
    es = ExitStack()
    p = Prog(nc, es)
    vgap = lambda: p.op("vector", lambda e: e.tensor_copy(s_vgap[:, :], s_vgap[:, :]))
    m = p.m

    sbt = lambda n, s, d: es.enter_context(nc.sbuf_tensor(n, s, d))

    s_id = sbt("s_id", [128, 128], F32)
    s_idb = sbt("s_idb", [128, 128], BF16)
    s_iotok = sbt("s_iotok", [128, NT], I32)
    s_e512 = sbt("s_e512", [8, 1], F32)
    s_pref = sbt("s_pref", [8, 8], F32)
    s_wgk = sbt("s_wgk", [128, 8 * 10], F32)
    s_bm = sbt("s_bm", [128, 1], F32)
    s_b0 = sbt("s_b0", [128, 1], F32)
    s_b1c = sbt("s_b1c", [128, 1], F32)
    s_s1 = sbt("s_s1", [128, NF], F32)
    s_s2 = sbt("s_s2", [128, NF], F32)
    s_gm = sbt("s_gm", [128, NF], F32)
    s_nw = sbt("s_nw", [128, NF], F32)
    s_nb = sbt("s_nb", [128, NF], F32)
    s_pb = sbt("s_pb", [128, NF], F32)
    s_pgb = sbt("s_pgb", [128, NF], F32)
    s_c1b = sbt("s_c1b", [128, HID // 128], F32)
    s_c2b = sbt("s_c2b", [128, NF], F32)
    s_c2gb = sbt("s_c2gb", [128, NF], F32)
    s_ib = sbt("s_ib", [128, 8 * 32], F32)
    s_ibs = sbt("s_ibs", [128, 8 * 32], F32)
    s_rib = sbt("s_rib", [128, 32], F32)
    s_ribs = sbt("s_ribs", [128, 32], F32)
    s_rob = sbt("s_rob", [128, NF], F32)
    s_ones128 = sbt("s_ones128", [128, 1], F32)
    s_ones8 = sbt("s_ones8", [8, 1], F32)
    s_onesr = sbt("s_onesr", [1, 128], F32)
    s_onesrb = sbt("s_onesrb", [1, 128], BF16)
    s_ones128b = sbt("s_ones128b", [128, 1], BF16)
    s_cnt = sbt("s_cnt", [8, 1], F32)
    s_vgap = sbt("s_vgap", [128, 1], F32)
    s_fi32 = sbt("s_fi32", [1, 1], I32)
    s_fbf = sbt("s_fbf", [1, 1], BF16)
    if debug:
        s_dbglgt = sbt("s_dbglgt", [128, 10 * NT], F32)
    s_cnts = sbt("s_cnts", [8, 8], F32)
    s_offs = sbt("s_offs", [8, 1], F32)
    s_gcol = sbt("s_gcol", [128, NT], F32)
    s_c0 = sbt("s_c0", [128, NT], F32)
    s_gc1 = sbt("s_gc1", [128, NT], F32)
    s_Si = sbt("s_Si", [128, NT], I32)
    s_Si2 = sbt("s_Si2", [128, NT], I32)
    s_Fi = sbt("s_Fi", [128, NT], I32)
    s_tos = sbt("s_tos", [128, NST], I32)
    s_gmb = sbt("s_gmb", [128, F], BF16)
    s_bigB = sbt("s_bigB", [128, NF * T], F32)
    s_aoT = sbt("s_aoT", [128, NF * T], BF16)

    pA = es.enter_context(nc.psum_tensor("pA", [128, 1024], F32))
    pB = es.enter_context(nc.psum_tensor("pB", [128, 1024], F32))
    pC = es.enter_context(nc.psum_tensor("pC", [128, 1024], F32))
    pTf = es.enter_context(nc.psum_tensor("pTf", [128, 512], F32))
    pTb = es.enter_context(nc.psum_tensor("pTb", [128, 1024], BF16))

    def fm(handle, ft, a, b):
        return handle[:, ft * T + a: ft * T + b]

    _bcregs = {}

    def bc_reg(e, val):
        if val not in _bcregs:
            _bcregs[val] = e.alloc_register(f"bc{val}")
        r = _bcregs[val]
        e.reg_mov(r, val)
        return r

    try:
        for src, dst in [(ident, s_id), (iotok, s_iotok), (e512c, s_e512), (prefm, s_pref),
                         (bmask, s_bm), (bsel0, s_b0), (bsel1, s_b1c),
                         (s1c_d, s_s1), (s2c_d, s_s2), (gmc_d, s_gm), (nwc_d, s_nw),
                         (nbc_d, s_nb), (pbc_d, s_pb), (c1bc_d, s_c1b), (c2bc_d, s_c2b),
                         (ibc_d, s_ib), (ribc_d, s_rib), (robc_d, s_rob)]:
            p.op("sync", lambda e, s=src, d=dst: e.dma_start(out=d[:, :], in_=s[:, :]), inc="c0", inck=16)
        p.op("sync", lambda e: e.dma_start(
            out=s_wgk.ap().rearrange("pp (k c) -> pp k c", k=8),
            in_=wgcat.rearrange("(k pp) c -> pp k c", pp=128)), inc="c0", inck=16)
        consts_ld = m("c0")
        for src, dst in [(qkv_w, d_bq), (proj_w, d_bp), (conv1_w, d_b1), (conv2_w, d_b2)]:
            p.op("gpsimd", lambda e, s=src, d=dst: e.dma_start(out=d[:, :], in_=s[:, :]), inc="wc", inck=16)
        p.op("gpsimd", lambda e: e.dma_start(out=s_fbf[:, :], in_=d_b2[0:1, 0:1]),
             waits=[("wc", m("wc"))], inc="wc", inck=16)
        wcast_done = m("wc")
        for buf, v in [(s_ones128, 1.0), (s_ones8, 1.0), (s_onesr, 1.0), (s_onesrb, 1.0),
                       (s_ones128b, 1.0)]:
            p.op("gpsimd", lambda e, b=buf, v=v: e.memset(b[:, :], v))
        gp_set = p.mark("gpsimd", "gp")
        p.op("vector", lambda e: e.tensor_copy(s_idb[:, :], s_id[:, :]),
             waits=[("c0", consts_ld), ("gp", gp_set)])
        p.op("vector", lambda e: e.tensor_mul(s_pgb[:, :], s_pb[:, :], s_gm[:, :]))
        p.op("vector", lambda e: e.tensor_mul(s_c2gb[:, :], s_c2b[:, :], s_gm[:, :]))
        p.op("vector", lambda e: e.tensor_scalar_mul(s_ibs[:, :], s_ib[:, :], GELU_S))
        p.op("vector", lambda e: e.tensor_scalar_mul(s_ribs[:, :], s_rib[:, :], GELU_S), inc="ev")
        p.barrier()

        if upto < 1:
            raise _StopBuild()
        from contextlib import ExitStack as ES2
        kvscope = ES2()
        s_kv = kvscope.enter_context(nc.sbuf_tensor("s_kv", [64, 1040], F32))
        s_kvst = kvscope.enter_context(nc.sbuf_tensor("s_kvst", [128, 1040], F32))
        s_kvtmp = kvscope.enter_context(nc.sbuf_tensor("s_kvtmp", [64, 1040], F32))
        mscope = ES2()
        s_m1T = mscope.enter_context(nc.sbuf_tensor("s_m1T", [8, T], F32))
        s_loc = mscope.enter_context(nc.sbuf_tensor("s_loc", [8, T], F32))
        ph0 = ES2()
        sb0 = lambda n, s, d: ph0.enter_context(nc.sbuf_tensor(n, s, d))
        rows = ES2()
        s_nwrow = rows.enter_context(nc.sbuf_tensor("s_nwrow", [1, F], F32))
        s_nbrow = rows.enter_context(nc.sbuf_tensor("s_nbrow", [1, F], F32))
        s_gmrow = rows.enter_context(nc.sbuf_tensor("s_gmrow", [1, F], F32))
        for src, dst in [(nwr_d, s_nwrow), (nbr_d, s_nbrow), (gmr_d, s_gmrow)]:
            p.op("sync", lambda e, s=src, d=dst: e.dma_start(out=d[:, :], in_=s[:, :]), inc="c1", inck=16)
        rows_ld = m("c1")
        for c in range(2):
            p.op("tensor", lambda e, c=c: e.matmul(pA[:, c * 512:(c + 1) * 512], s_onesr[:, :],
                 s_nwrow[:, c * 512:(c + 1) * 512], start=True, stop=True),
                 waits=[("c1", rows_ld), ("gp", gp_set)] if c == 0 else ())
            p.op("tensor", lambda e, c=c: e.matmul(pB[:, c * 512:(c + 1) * 512], s_onesr[:, :],
                 s_nbrow[:, c * 512:(c + 1) * 512], start=True, stop=True))
            p.op("tensor", lambda e, c=c: e.matmul(pC[:, c * 512:(c + 1) * 512], s_onesr[:, :],
                 s_gmrow[:, c * 512:(c + 1) * 512], start=True, stop=True))
        bc_mm = p.mark("tensor", "mm")
        rows.close()
        s_nwb = sb0("s_nwb", [128, F], F32)
        s_nbb = sb0("s_nbb", [128, F], F32)
        p.op("vector", lambda e: e.tensor_copy(s_nwb[:, :], pA[:, :]), waits=[("mm", bc_mm)])
        p.op("vector", lambda e: e.tensor_copy(s_nbb[:, :], pB[:, :]))
        bc_ev = p.op("vector", lambda e: e.tensor_copy(s_gmb[:, :], pC[:, :]), inc="ev")

        s_xt = [sb0(f"s_xt{i}", [128, F], F32) for i in range(2)]
        s_xn = [sb0(f"s_xn{i}", [128, F], F32) for i in range(2)]
        s_z2 = [sb0(f"s_z2{i}", [128, F], F32) for i in range(2)]
        s_scr = sb0("s_scr", [128, F], F32)
        s_aotmp = [sb0(f"s_aotmp{i}", [128, F], F32) for i in range(2)]
        s_aotok = [sb0(f"s_aotok{i}", [128, F], BF16) for i in range(2)]
        s_xr1tok = [sb0(f"s_xr1tok{i}", [128, F], BF16) for i in range(2)]
        s_xnsp = [sb0(f"s_xnsp{i}", [128, F], F32) for i in range(2)]
        s_x1sp = [sb0(f"s_x1sp{i}", [128, F], BF16) for i in range(2)]
        s_m1t = sb0("s_m1t", [128, 8], F32)
        s_lgt = sb0("s_lgt", [128, 10], F32)
        s_cA = sb0("s_cA", [128, 1], F32)
        s_cB = sb0("s_cB", [128, 1], F32)
        s_cC = sb0("s_cC", [128, 1], F32)
        s_cD = sb0("s_cD", [128, 1], F32)
        s_cE = sb0("s_cE", [128, 1], F32)
        if debug:
            s_dbgA = sb0("s_dbgA", [128, 3 * NT], F32)
            s_dbgV = sb0("s_dbgV", [128, NT], F32)

        ac_xfree = [0] * NT
        mm_xntr = [0] * NT
        mm_z2tr = [0] * NT
        mm_xrtr = [0] * NT
        mm_lg = [0] * NT
        ev_aoTc = [0] * NT
        ev_ao9 = [0] * NT
        sc_ao = [0] * NT
        ld_spl = [0] * NT
        ld_x1sp = [0] * NT
        ev_ptf_last = [0] * NT
        ev_lgt_t = [0] * NT
        ev_xr1c_t = [0] * NT

        for t in range(p0_tiles):
            par = t % 2
            lw = [("ev", bc_ev)] if t == 0 else []
            if t >= 2:
                lw.append(("ac", ac_xfree[t - 2]))
            ldx = p.op("sync", lambda e, t=t, par=par: e.dma_start(out=s_xt[par][:, :],
                       in_=x_sh[t * 128:(t + 1) * 128, :]), waits=lw, inc=f"x{par}", inck=16)
            p.op("vector", lambda e, par=par: e.reduce_sum(s_cA[:, :], s_xt[par][:, :], axis=AX.X),
                 waits=[(f"x{par}", ldx)])
            vgap()
            ev1 = p.op("vector", lambda e: e.tensor_scalar_mul(s_cB[:, :], s_cA[:, :], -1.0 / F), inc="ev")
            acw = [("ev", ev1)]
            if t >= 1:
                acw.append(("ev", ev_ao9[t - 1]))
            if debug:
                p.op("scalar", lambda e, t=t: e.activation(
                    s_dbgA[:, 3 * t:3 * t + 1], s_cB[:, 0:1], AF.Identity), waits=list(acw), inc="ac")
            ac1 = p.op("scalar", lambda e, par=par: e.activation(s_scr[:, :], s_xt[par][:, :], AF.Square,
                       bias=s_cB[:, 0:1], scale=1.0, accum_out=s_cA[:, 0:1]), waits=acw, inc="ac")
            p.op("vector", lambda e: e.tensor_scalar(s_cC[:, :], s_cA[:, :], 1.0 / F, 1e-5,
                 ALU.mult, ALU.add), waits=[("ac", ac1)])
            ev2 = p.mark("vector", "ev")
            ac2 = p.op("scalar", lambda e: e.activation(s_cC[:, :], s_cC[:, :], AF.Sqrt),
                       waits=[("ev", ev2)], inc="ac")
            p.op("vector", lambda e: e.reciprocal(s_cD[:, :], s_cC[:, :]), waits=[("ac", ac2)])
            vgap()
            ev3 = p.op("vector", lambda e: e.tensor_mul(s_cE[:, :], s_cB[:, :], s_cD[:, :]), inc="ev")
            if debug:
                p.op("vector", lambda e, t=t: e.tensor_copy(s_dbgV[:, t:t + 1], s_cE[:, 0:1]))
            xnw = [("ev", ev3)]
            if t >= 2:
                xnw += [("mm", mm_xntr[t - 2]), ("mm", mm_xrtr[t - 2]), ("mm", mm_z2tr[t - 2])]
            if debug:
                p.op("scalar", lambda e, t=t: e.activation(
                    s_dbgA[:, 3 * t + 1:3 * t + 2], s_cD[:, 0:1], AF.Identity),
                    waits=list(xnw), inc="ac")
                p.op("scalar", lambda e, t=t: e.activation(
                    s_dbgA[:, 3 * t + 2:3 * t + 3], s_cE[:, 0:1], AF.Identity), inc="ac")
            acxn = p.op("scalar", lambda e, par=par: e.activation(s_xn[par][:, :], s_xt[par][:, :], AF.Identity,
                        bias=s_cE[:, 0:1], scale=s_cD[:, 0:1]), waits=xnw, inc="ac")
            ac_xfree[t] = acxn
            p.op("vector", lambda e, par=par: e.reduce_sum(s_cA[:, :], s_xn[par][:, :], axis=AX.X),
                 waits=[("ac", acxn)])
            vgap()
            ev4 = p.op("vector", lambda e: e.tensor_scalar_mul(s_cB[:, :], s_cA[:, :], -1.0 / F), inc="ev")
            ac3 = p.op("scalar", lambda e, par=par: e.activation(s_scr[:, :], s_xn[par][:, :], AF.Square,
                       bias=s_cB[:, 0:1], scale=1.0, accum_out=s_cA[:, 0:1]), waits=[("ev", ev4)], inc="ac")
            p.op("vector", lambda e: e.tensor_scalar(s_cC[:, :], s_cA[:, :], 1.0 / F, 1e-12,
                 ALU.mult, ALU.add), waits=[("ac", ac3)])
            ev5 = p.mark("vector", "ev")
            ac4 = p.op("scalar", lambda e: e.activation(s_cC[:, :], s_cC[:, :], AF.Sqrt),
                       waits=[("ev", ev5)], inc="ac")
            p.op("vector", lambda e: e.reciprocal(s_cD[:, :], s_cC[:, :]), waits=[("ac", ac4)])
            vgap()
            ev6 = p.op("vector", lambda e: e.tensor_mul(s_cE[:, :], s_cB[:, :], s_cD[:, :]), inc="ev")
            z2w = [("ev", ev6)]
            if t >= 2:
                z2w += [("ev", ev_ao9[t - 2])]
            acz2 = p.op("scalar", lambda e, par=par: e.activation(s_z2[par][:, :], s_xn[par][:, :], AF.Identity,
                        bias=s_cE[:, 0:1], scale=s_cD[:, 0:1]), waits=z2w, inc="ac")
            ac5 = p.op("scalar", lambda e, par=par: e.activation(s_scr[:, :], s_xn[par][:, :], AF.Square,
                       bias=0.0, scale=1.0, accum_out=s_cA[:, 0:1]), inc="ac")
            ac6 = p.op("scalar", lambda e: e.activation(s_cB[:, :], s_cA[:, :], AF.Sqrt), inc="ac")
            p.op("vector", lambda e: e.reciprocal(s_cC[:, :], s_cB[:, :]), waits=[("ac", ac6)])
            vgap()
            ev7 = p.op("vector", lambda e: e.tensor_scalar_mul(s_cD[:, :], s_cC[:, :], RMS_C), inc="ev")
            xrw = [("ev", ev7)]
            if t >= 2:
                xrw.append(("mm", mm_xrtr[t - 2]))
            acxr = p.op("scalar", lambda e, par=par: e.activation(s_xr1tok[par][:, :], s_xn[par][:, :],
                        AF.Identity, bias=0.0, scale=s_cD[:, 0:1]), waits=xrw, inc="ac")
            aow = [("ac", acz2), ("ac", ac5)]
            if t >= 2:
                aow.append((f"a{par}", sc_ao[t - 2]))
            p.op("vector", lambda e, par=par: e.tensor_mul(s_scr[:, :], s_z2[par][:, :], s_nwb[:, :]),
                 waits=aow)
            ev_ao9[t] = p.op("vector", lambda e, par=par: e.tensor_add(s_aotok[par][:, :], s_scr[:, :],
                             s_nbb[:, :]), inc="ev")
            sc_ao[t] = p.op("scalar", lambda e, t=t, par=par: e.dma_start(
                out=d_ao[t * 128:(t + 1) * 128, :], in_=s_aotok[par][:, :]),
                waits=[("ev", ev_ao9[t])], inc=f"a{par}", inck=16)
            ev_rounds = []
            for r in range(2):
                tw = [("ac", acxn)]
                if t == 0 and r == 0:
                    pass
                elif r == 0:
                    tw.append(("ev", ev_ptf_last[t - 1]))
                else:
                    tw.append(("ev", ev_rounds[0]))
                if r == 0 and t >= 2:
                    tw.append((f"sp{par}", ld_spl[t - 2]))
                for j in range(4):
                    ft = r * 4 + j
                    p.op("tensor", lambda e, par=par, ft=ft, j=j: e.transpose(
                        pTf[:, j * 128:(j + 1) * 128], s_xn[par][:, ft * 128:(ft + 1) * 128], s_id[:, :]),
                        waits=tw if j == 0 else ())
                mmr = p.mark("tensor", "mm")
                ev_rounds.append(p.op("vector", lambda e, par=par, r=r: e.tensor_copy(
                    s_xnsp[par][:, r * 512:(r + 1) * 512], pTf[:, :]), waits=[("mm", mmr)], inc="ev"))
            mm_xntr[t] = m("mm")
            ld_spl[t] = p.op("sync", lambda e, t=t, par=par: e.dma_start(
                out=d_xnT.rearrange("(f pp) tok -> pp f tok", pp=128)[:, :, t * 128:(t + 1) * 128],
                in_=s_xnsp[par].ap().rearrange("pp (f c) -> pp f c", f=NF)),
                waits=[("ev", ev_rounds[1])], inc=f"sp{par}", inck=16)
            ac_blocks = []
            for r in range(2):
                if r == 0:
                    tw = [("ac", acz2), ("ev", ev_rounds[1])]
                    if t >= 2:
                        tw += [("mm", mm_lg[t - 2]), ("ev", ev_aoTc[t - 2])]
                else:
                    tw = [("ac", m("ac"))]
                for j in range(4):
                    ft = r * 4 + j
                    p.op("tensor", lambda e, par=par, ft=ft, j=j: e.transpose(
                        pTf[:, j * 128:(j + 1) * 128], s_z2[par][:, ft * 128:(ft + 1) * 128], s_id[:, :]),
                        waits=tw if j == 0 else ())
                mmr = p.mark("tensor", "mm")
                for j in range(4):
                    ft = r * 4 + j
                    p.op("scalar", lambda e, par=par, ft=ft, j=j: e.activation(
                        s_aotmp[par][:, ft * 128:(ft + 1) * 128], pTf[:, j * 128:(j + 1) * 128], AF.Identity,
                        bias=s_nb[:, ft:ft + 1], scale=s_nw[:, ft:ft + 1]),
                        waits=[("mm", mmr)] if j == 0 else ())
                ac_blocks.append(p.mark("scalar", "ac"))
            mm_z2tr[t] = m("mm")
            ao_ac = ac_blocks[1]
            ev_aoTc[t] = p.op("vector", lambda e, par=par, t=t: e.tensor_copy(
                s_aoT.ap().rearrange("pp (f tok) -> pp f tok", f=NF)[:, :, t * 128:(t + 1) * 128],
                s_aotmp[par].ap().rearrange("pp (f c) -> pp f c", f=NF)),
                waits=[("ac", ao_ac)], inc="ev")
            xtw = [("ac", acxr)]
            if t >= 1:
                xtw.append(("ev", ev_xr1c_t[t - 1]))
            if t >= 2:
                xtw.append((f"q{par}", ld_x1sp[t - 2]))
            for ft in range(NF):
                p.op("tensor", lambda e, par=par, ft=ft: e.transpose(
                    pTb[:, ft * 128:(ft + 1) * 128], s_xr1tok[par][:, ft * 128:(ft + 1) * 128], s_idb[:, :]),
                    waits=xtw if ft == 0 else ())
            mmx = p.mark("tensor", "mm")
            mm_xrtr[t] = mmx
            ev_xr1c_t[t] = p.op("vector", lambda e, par=par: e.tensor_copy(
                s_x1sp[par][:, :], pTb[:, :]), waits=[("mm", mmx)], inc="ev")
            ld_x1sp[t] = p.op("sync", lambda e, t=t, par=par: e.dma_start(
                out=d_x1T.rearrange("(f pp) tok -> pp f tok", pp=128)[:, :, t * 128:(t + 1) * 128],
                in_=s_x1sp[par].ap().rearrange("pp (f c) -> pp f c", f=NF)),
                waits=[("ev", ev_xr1c_t[t])], inc=f"q{par}", inck=16)
            lgw = [("ac", ao_ac)]
            if t >= 1:
                lgw.append(("ev", ev_lgt_t[t - 1]))
            for kt in range(NF):
                p.op("tensor", lambda e, par=par, kt=kt: e.matmul(
                    pC[:, 0:10], s_aotmp[par][:, kt * 128:(kt + 1) * 128],
                    s_wgk[:, kt * 10:(kt + 1) * 10], start=(kt == 0), stop=(kt == NF - 1)),
                    waits=lgw if kt == 0 else ())
            mm_lg[t] = p.mark("tensor", "mm")
            ev10 = p.op("vector", lambda e: e.tensor_copy(s_lgt[:, :], pC[:, 0:10]),
                        waits=[("mm", mm_lg[t])], inc="ev")
            ev_lgt_t[t] = ev10
            if debug:
                p.op("vector", lambda e, t=t: e.tensor_copy(
                    s_dbglgt[:, 10 * t:10 * (t + 1)], s_lgt[:, :]))
            vgap()
            p.op("vector", lambda e: e.reduce_max(s_cA[:, :], s_lgt[:, 0:8], axis=AX.X))
            vgap()
            ev11 = p.op("vector", lambda e: e.tensor_scalar_mul(s_cB[:, :], s_cA[:, :], -1.0), inc="ev")
            ac7 = p.op("scalar", lambda e: e.activation(s_scr[:, 0:8], s_lgt[:, 0:8], AF.Exp,
                       bias=s_cB[:, 0:1], scale=1.0, accum_out=s_cC[:, 0:1]), waits=[("ev", ev11)], inc="ac")
            p.op("vector", lambda e, t=t: e.reciprocal(s_gcol[:, t:t + 1], s_cC[:, :]), waits=[("ac", ac7)])
            p.op("vector", lambda e: e.tensor_scalar(s_m1t[:, :], s_lgt[:, 0:8], s_cA[:, 0:1], None,
                 ALU.is_equal))
            ev12 = p.op("vector", lambda e: e.tensor_sub(s_cD[:, :], s_lgt[:, 8:9], s_lgt[:, 9:10]), inc="ev")
            ac8 = p.op("scalar", lambda e, t=t: e.activation(s_c0[:, t:t + 1], s_cD[:, 0:1], AF.Sigmoid),
                       waits=[("ev", ev12)], inc="ac")
            p.op("vector", lambda e, t=t: e.tensor_scalar(s_cE[:, :], s_c0[:, t:t + 1], -1.0, 1.0,
                 ALU.mult, ALU.add), waits=[("ac", ac8)])
            vgap()
            ev13 = p.op("vector", lambda e, t=t: e.tensor_mul(s_gc1[:, t:t + 1], s_gcol[:, t:t + 1],
                        s_cE[:, 0:1]), inc="ev")
            mmt = p.op("tensor", lambda e: e.transpose(pTf[0:8, 0:128], s_m1t[:, :], s_id[:, :]),
                       waits=[("ev", ev13), ("ac", m("ac"))], inc="mm")
            ev_ptf_last[t] = p.op("vector", lambda e, t=t: e.tensor_copy(
                s_m1T[0:8, t * 128:(t + 1) * 128], pTf[0:8, 0:128]), waits=[("mm", mmt)], inc="ev")
        x1sp_tot = [("q0", m("q0")), ("q1", m("q1"))]
        xnsp_tot = [("sp0", m("sp0")), ("sp1", m("sp1"))]
        if debug and upto < 4:
            p.op("sync", lambda e: e.dma_start(out=dbg["dbg_xnT"][:, :], in_=d_xnT[:, :]),
                 waits=list(xnsp_tot), inc="dbg", inck=16)
            p.op("sync", lambda e: e.dma_start(out=dbg["dbg_ao"][:, :], in_=d_ao[:, :]),
                 waits=[("a0", m("a0")), ("a1", m("a1"))], inc="dbg", inck=16)
        if debug:
            p.op("sync", lambda e: e.dma_start(out=dbg["dbg_A"][:, :], in_=s_dbgA[:, :]),
                 waits=[("ac", m("ac"))], inc="dbg", inck=16)
            p.op("sync", lambda e: e.dma_start(out=dbg["dbg_V"][:, :], in_=s_dbgV[:, :]),
                 waits=[("ev", m("ev"))], inc="dbg", inck=16)
        ph0.close()
        p.barrier()

        if upto < 2:
            raise _StopBuild()
        p.op("vector", lambda e: e.reduce_sum(s_cnt[:, :], s_m1T[:, :], axis=AX.X))
        ev15 = p.op("vector", lambda e: e.tensor_copy(s_vgap[:, :], s_vgap[:, :]), inc="ev")
        g1 = p.op("sync", lambda e: e.dma_start(out=cc_cnt_i[:, :], in_=s_cnt[:, :]),
                  waits=[("ev", ev15)], inc="g0", inck=16)
        cc1 = p.op("gpsimd", lambda e: e.collective_compute(
            "AllGather", ALU.bypass, [list(range(NCORES))], [cc_cnt_i[:, :]], [cc_cnt_o[:, :]]),
            waits=[("g0", g1)], inc="cc")

        if upto < 3:
            raise _StopBuild()
        phkv = ES2()
        sbk = lambda n, s, d: phkv.enter_context(nc.sbuf_tensor(n, s, d))
        s_wkv = sbk("s_wkv", [128, NF * HID], BF16)
        s_x1t = [sbk(f"s_x1t{i}", [128, NF * 128], BF16) for i in range(2)]
        s_kc = [sbk(f"s_kc{i}", [128, F], BF16) for i in range(2)]
        s_vc = [sbk(f"s_vc{i}", [128, F], BF16) for i in range(2)]
        s_ksq = sbk("s_ksq", [128, F], F32)

        wkv_ld = []
        wkv_fold = []
        for kt in range(NF):
            ww = [("wc", wcast_done)] if kt == 0 else []
            if kt >= 2:
                ww.append(("ev", wkv_fold[kt - 2]))
            wkv_ld.append(p.op("sync", lambda e, kt=kt: e.dma_start(
                out=s_wkv[:, kt * HID:(kt + 1) * HID],
                in_=d_bq[F:3 * F, kt * 128:(kt + 1) * 128], transpose=True),
                waits=ww, inc=f"wk{kt % 2}", inck=16))
            wkv_fold.append(p.op("vector", lambda e, kt=kt: e.tensor_scalar_mul(
                s_wkv[:, kt * HID:(kt + 1) * HID], s_wkv[:, kt * HID:(kt + 1) * HID], s_s1[:, kt:kt + 1]),
                waits=[(f"wk{kt % 2}", wkv_ld[kt])], inc="ev"))

        ac_k = [0] * NT
        ev_v = [0] * NT
        mm_kvt = [0] * NT
        mm_kvacc = [0] * NT
        kvadd = [0] * NT
        ld_x1 = [0] * NT
        p.op("gpsimd", lambda e: e.memset(s_kv[:, :], 0.0))
        gp_kv0 = p.mark("gpsimd", "gp")
        for t in range(NT):
            par = t % 2
            xw = list(x1sp_tot) if t == 0 else []
            if t >= 2:
                xw.append(("mm", mm_kvt[t - 2]))
            ld_x1[t] = p.op("sync", lambda e, t=t, par=par: e.dma_start(
                out=s_x1t[par].ap().rearrange("pp (f c) -> pp f c", f=NF),
                in_=d_x1T.rearrange("(f pp) tok -> pp f tok", pp=128)[:, :, t * 128:(t + 1) * 128]),
                waits=xw, inc=f"k{par}", inck=16)
            mw = [("ev", wkv_fold[-1]), (f"k{par}", ld_x1[t])]
            if t >= 1:
                mw.append(("ac", ac_k[t - 1]))
                mw.append(("ev", ev_v[t - 1]))
            for kt in range(NF):
                for half in range(2):
                    p.op("tensor", lambda e, par=par, kt=kt, half=half: e.matmul(
                        pA[:, half * 512:(half + 1) * 512],
                        s_x1t[par][:, kt * 128:(kt + 1) * 128],
                        s_wkv[:, kt * HID + half * 512: kt * HID + (half + 1) * 512],
                        start=(kt == 0), stop=(kt == NF - 1)), waits=mw if (kt == 0 and half == 0) else ())
                    p.op("tensor", lambda e, par=par, kt=kt, half=half: e.matmul(
                        pB[:, half * 512:(half + 1) * 512],
                        s_x1t[par][:, kt * 128:(kt + 1) * 128],
                        s_wkv[:, kt * HID + 1024 + half * 512: kt * HID + 1024 + (half + 1) * 512],
                        start=(kt == 0), stop=(kt == NF - 1)))
            mm_kvt[t] = p.mark("tensor", "mm")
            kw = [("mm", mm_kvt[t])]
            if t >= 2:
                kw.append(("mm", mm_kvacc[t - 2]))
            ack1 = p.op("scalar", lambda e: e.activation(s_ksq[:, :], pA[:, :], AF.Square),
                        waits=kw, inc="ac")
            ack2 = p.op("scalar", lambda e, par=par: e.activation(s_kc[par][:, :], s_ksq[:, :], AF.Exp,
                        scale=-0.5), inc="ac")
            ac_k[t] = ack2
            vw = [("mm", mm_kvt[t])]
            if t >= 2:
                vw.append(("mm", mm_kvacc[t - 2]))
            ev_v[t] = p.op("vector", lambda e, par=par: e.tensor_copy(s_vc[par][:, :], pB[:, :]),
                           waits=vw, inc="ev")
            kvw = [("ac", ack2), ("ev", ev_v[t])]
            if t == 0:
                kvw.append(("gp", gp_kv0))
            else:
                kvw.append(("ev", kvadd[t - 1]))
            for h in range(H):
                if h < 7:
                    pdst, cA0 = pC, h * 65
                elif h < 14:
                    pdst, cA0 = pC, 512 + (h - 7) * 65
                else:
                    pdst, cA0 = pTf, (h - 14) * 65
                p.op("tensor", lambda e, par=par, h=h, cA0=cA0, pdst=pdst: e.matmul(
                    pdst[0:64, cA0:cA0 + 64], s_kc[par][:, h * 64:(h + 1) * 64],
                    s_vc[par][:, h * 64:(h + 1) * 64], start=True, stop=False),
                    waits=kvw if h == 0 else ())
                p.op("tensor", lambda e, par=par, h=h, cA0=cA0, pdst=pdst: e.matmul(
                    pdst[0:64, cA0 + 64:cA0 + 65], s_kc[par][:, h * 64:(h + 1) * 64],
                    s_ones128b[:, 0:1], start=False, stop=True))
            mm_kvacc[t] = p.mark("tensor", "mm")
            p.op("vector", lambda e: e.tensor_add(s_kv[:, 0:455], s_kv[:, 0:455], pC[0:64, 0:455]),
                 waits=[("mm", mm_kvacc[t])])
            p.op("vector", lambda e: e.tensor_add(s_kv[:, 455:910], s_kv[:, 455:910], pC[0:64, 512:967]))
            kvadd[t] = p.op("vector", lambda e: e.tensor_add(
                s_kv[:, 910:1040], s_kv[:, 910:1040], pTf[0:64, 0:130]), inc="ev")
        p.op("vector", lambda e: e.tensor_copy(s_kvst[0:64, :], s_kv[:, :]),
             waits=[("ev", kvadd[NT - 1])])
        p.op("vector", lambda e: e.tensor_copy(s_kvst[64:128, :], s_kv[:, :]))
        ev18 = p.op("vector", lambda e: e.tensor_scalar_mul(s_kvst[:, :], s_kvst[:, :], s_bm[:, 0:1]),
                    inc="ev")
        g2 = p.op("sync", lambda e: e.dma_start(out=cc_kv_i[:, :], in_=s_kvst[:, :]),
                  waits=[("ev", ev18)], inc="g0", inck=16)
        cc2 = p.op("gpsimd", lambda e: e.collective_compute(
            "AllReduce", ALU.add, [list(range(NCORES))], [cc_kv_i[:, :]], [cc_kv_o[:, :]]),
            waits=[("g0", g2)], inc="cc")
        phkv.close()

        if upto < 4:
            raise _StopBuild()
        phg = ES2()
        sbg = lambda n, s, d: phg.enter_context(nc.sbuf_tensor(n, s, d))
        s_Srow = sbg("s_Srow", [1, T], F32)
        s_tmpr = sbg("s_tmpr", [1, T], F32)

        g3 = p.op("sync", lambda e: e.dma_start(
            out=s_cnts[:, :], in_=cc_cnt_o.rearrange("(r ee) one -> ee (r one)", ee=8),
            allow_slow_non_contiguous=True), waits=[("cc", cc1)], inc="g1", inck=16)
        p.op("vector", lambda e: e.scalar_tensor_tensor(
            s_cnts[:, :], s_cnts[:, :], 1.0, s_pref[:, :], ALU.mult, ALU.mult,
            accum_out=s_offs[:, 0:1]), waits=[("g1", g3)])
        p.op("gpsimd", lambda e: e.memset(s_loc[:, 0:1], 0.0))
        gp0b = p.mark("gpsimd", "gp")
        p.op("vector", lambda e: e.tensor_tensor_scan(
            s_loc[:, 1:T], s_m1T[:, 0:T - 1], s_m1T[:, 0:T - 1], 0.0, ALU.add, ALU.bypass),
            waits=[("gp", gp0b)])
        p.op("vector", lambda e: e.tensor_scalar_add(s_loc[:, :], s_loc[:, :], s_offs[:, 0:1]))
        if debug:
            s_dbgl1 = phg.enter_context(nc.sbuf_tensor("s_dbgl1", [8, T], F32))
            s_dbgmk = phg.enter_context(nc.sbuf_tensor("s_dbgmk", [8, T], F32))
            p.op("vector", lambda e: e.tensor_copy(s_dbgl1[:, :], s_loc[:, :]))
        vgap()
        p.op("vector", lambda e: e.scalar_tensor_tensor(
            s_m1T[:, :], s_loc[:, :], CAP, s_m1T[:, :], ALU.is_lt, ALU.mult))
        if debug:
            p.op("vector", lambda e: e.tensor_copy(s_dbgmk[:, :], s_m1T[:, :]))
        evm1k = p.mark("vector", "ev")
        p.op("gpsimd", lambda e: e.memset(s_loc[:, 0:1], 0.0), waits=[("ev", evm1k)])
        gp1 = p.mark("gpsimd", "gp")
        p.op("vector", lambda e: e.tensor_tensor_scan(
            s_loc[:, 1:T], s_m1T[:, 0:T - 1], s_m1T[:, 0:T - 1], 0.0, ALU.add, ALU.bypass),
            waits=[("gp", gp1)])
        ev21 = p.op("vector", lambda e: e.scalar_tensor_tensor(
            s_loc[:, :], s_loc[:, :], s_e512[:, 0:1], s_m1T[:, :], ALU.add, ALU.mult), inc="ev")
        for c in range(4):
            off = (c % 2) * 512
            dst = pA if c < 2 else pB
            p.op("tensor", lambda e, c=c, off=off, dst=dst: e.matmul(
                dst[0:1, off:off + 512], s_ones8[:, :], s_loc[:, c * 512:(c + 1) * 512],
                start=True, stop=True),
                waits=[("ev", ev21), ("ev", m("ev")), ("ac", m("ac")), ("gp", gp_set)]
            if c == 0 else ())
        mmS = p.mark("tensor", "mm")
        for c in range(4):
            off = (c % 2) * 512
            dst = pA if c < 2 else pB
            p.op("vector", lambda e, c=c, off=off, dst=dst: e.tensor_copy(
                s_Srow[:, c * 512:(c + 1) * 512], dst[0:1, off:off + 512]), waits=[("mm", mmS)])
        evSr = p.mark("vector", "ev")
        for c in range(4):
            off = (c % 2) * 512
            dst = pA if c < 2 else pB
            p.op("tensor", lambda e, c=c, off=off, dst=dst: e.matmul(
                dst[0:1, off:off + 512], s_ones8[:, :], s_m1T[:, c * 512:(c + 1) * 512],
                start=True, stop=True), waits=[("ev", evSr)] if c == 0 else ())
        mmK = p.mark("tensor", "mm")
        for c in range(4):
            off = (c % 2) * 512
            dst = pA if c < 2 else pB
            p.op("vector", lambda e, c=c, off=off, dst=dst: e.tensor_copy(
                s_tmpr[:, c * 512:(c + 1) * 512], dst[0:1, off:off + 512]), waits=[("mm", mmK)])
        vgap()
        p.op("vector", lambda e: e.scalar_tensor_tensor(
            s_Srow[:, :], s_tmpr[:, :], -SBIG, s_Srow[:, :], ALU.mult, ALU.add))
        vgap()
        ev22 = p.op("vector", lambda e: e.tensor_scalar_add(s_Srow[:, :], s_Srow[:, :], SBIG), inc="ev")
        evprev = ev22
        for t in range(NT):
            mmt = p.op("tensor", lambda e, t=t: e.transpose(
                pTf[:, 0:1], s_Srow[:, t * 128:(t + 1) * 128], s_id[0:1, 0:1]),
                waits=[("ev", evprev)], inc="mm")
            p.op("vector", lambda e, t=t: e.tensor_copy(s_Si[:, t:t + 1], pTf[:, 0:1]),
                 waits=[("mm", mmt)])
            vgap()
            p.op("vector", lambda e, t=t: e.tensor_single_scalar(
                s_Fi[:, t:t + 1], s_Si[:, t:t + 1], 127, ALU.bitwise_and))
            p.op("vector", lambda e, t=t: e.tensor_single_scalar(
                s_Si2[:, t:t + 1], s_Si[:, t:t + 1], 7, ALU.logical_shift_right))
            p.op("vector", lambda e, t=t: e.tensor_single_scalar(
                s_Fi[:, t:t + 1], s_Fi[:, t:t + 1], 5, ALU.logical_shift_left))
            vgap()
            evprev = p.op("vector", lambda e, t=t: e.tensor_add(
                s_Fi[:, t:t + 1], s_Fi[:, t:t + 1], s_Si2[:, t:t + 1]), inc="ev")
        sf_ev = evprev
        p.op("gpsimd", lambda e: e.memset(s_tos[:, :], 1 << 28))
        gp2 = p.mark("gpsimd", "gp")
        g4 = p.op("sync", lambda e: e.dma_start(out=d_tos[:, :], in_=s_tos[:, :]), waits=[("gp", gp2)],
                  inc="g2", inck=16)
        g4 = p.op("sync", lambda e: e.dma_start(out=s_fi32[:, :], in_=d_tos[0:1, 0:1]),
                  waits=[("g2", m("g2"))], inc="g2", inck=16)
        last_sc = 0
        for t in range(NT):
            w = [("g2", g4), ("ev", sf_ev)] if t == 0 else []
            last_sc = p.op("gpsimd", lambda e, t=t: e.indirect_dma_start(
                d_tos.rearrange("pp (c one) -> (pp c) one", one=1),
                bass.IndirectOffsetOnAxis(s_Fi[:, t:t + 1], 0),
                s_iotok[:, t:t + 1], None, bounds_check=bc_reg(e, NSLOT - 1), oob_is_err=False),
                waits=w, inc="ts", inck=16)
        last_sc = p.op("gpsimd", lambda e: e.dma_start(out=s_fi32[:, :], in_=d_tos[0:1, 0:1]),
                       waits=[("ts", m("ts"))], inc="ts", inck=16)
        g5 = p.op("sync", lambda e: e.dma_start(out=s_tos[:, :], in_=d_tos[:, :]),
                  waits=[("ts", last_sc)], inc="g3", inck=16)
        if debug:
            p.op("sync", lambda e: e.dma_start(out=dbg["dbg_Srow"][:, :], in_=s_Srow[:, :]),
                 inc="dbg", inck=16)
            p.op("sync", lambda e: e.dma_start(out=dbg["dbg_cnts"][:, :], in_=s_cnts[:, :]),
                 inc="dbg", inck=16)
            p.op("sync", lambda e: e.dma_start(out=dbg["dbg_loc"][:, :], in_=s_loc[:, :]),
                 inc="dbg", inck=16)
            p.op("sync", lambda e: e.dma_start(out=dbg["dbg_g"][:, :], in_=s_gcol[:, :]),
                 inc="dbg", inck=16)
            p.op("sync", lambda e: e.dma_start(out=dbg["dbg_tos"][:, :], in_=s_tos[:, :]),
                 waits=[("g3", g5)], inc="dbg", inck=16)
            p.op("sync", lambda e: e.dma_start(out=dbg["dbg_xnT"][:, :], in_=d_xnT[:, :]),
                 waits=list(xnsp_tot), inc="dbg", inck=16)
            p.op("sync", lambda e: e.dma_start(out=dbg["dbg_offs"][:, :], in_=s_offs[:, :]),
                 inc="dbg", inck=16)
            p.op("sync", lambda e: e.dma_start(out=dbg["dbg_loc1"][:, :], in_=s_dbgl1[:, :]),
                 inc="dbg", inck=16)
            p.op("sync", lambda e: e.dma_start(out=dbg["dbg_m1k"][:, :], in_=s_dbgmk[:, :]),
                 inc="dbg", inck=16)
            p.op("sync", lambda e: e.dma_start(out=dbg["dbg_lgt"][:, :], in_=s_dbglgt[:, :]),
                 inc="dbg", inck=16)
        phg.close()
        mscope.close()
        p.barrier()

        if upto < 5:
            raise _StopBuild()
        phq = ES2()
        sbq = lambda n, s, d: phq.enter_context(nc.sbuf_tensor(n, s, d))
        s_poT = sbq("s_poT", [128, NF * T], BF16)
        s_zr = sbq("s_zr", [H, T], F32)
        s_bo = sbq("s_bo", [H, F], F32)
        s_kvb = sbq("s_kvb", [128, 8 * 65], BF16)
        s_dent = [sbq(f"s_dent{i}", [1, 512], F32) for i in range(2)]

        g6b = p.op("sync", lambda e: e.dma_start(out=s_bo[:, :], in_=bones[:, :]), inc="g4", inck=16)
        g6 = p.op("sync", lambda e: e.dma_start(out=s_kvst[:, :], in_=cc_kv_o[:, :]),
                  waits=[("cc", cc2)], inc="g5", inck=16)
        p.op("vector", lambda e: e.tensor_scalar_mul(s_kv[:, :], s_kvst[0:64, :], s_b0[0:64, 0:1]),
             waits=[("g5", g6)])
        p.op("vector", lambda e: e.tensor_scalar_mul(s_kvtmp[:, :], s_kvst[64:128, :],
             s_b1c[64:128, 0:1]))
        ev23 = p.op("vector", lambda e: e.tensor_add(s_kv[:, :], s_kv[:, :], s_kvtmp[:, :]), inc="ev")
        p.op("vector", lambda e: e.tensor_copy(
            s_kvb[0:64, :].rearrange("pp (pr c) -> pp pr c", c=65),
            s_kv.ap().rearrange("pp (pr two c) -> pp pr two c", two=2, c=65)[:, :, 0, :]))
        ev25 = p.op("vector", lambda e: e.tensor_copy(
            s_kvb[64:128, :].rearrange("pp (pr c) -> pp pr c", c=65),
            s_kv.ap().rearrange("pp (pr two c) -> pp pr two c", two=2, c=65)[:, :, 1, :]), inc="ev")
        if debug:
            p.op("sync", lambda e: e.dma_start(out=dbg["dbg_kv"][:, :], in_=s_kv[:, :]),
                 waits=[("ev", ev23)], inc="dbg", inck=16)

        qsub = ES2()
        s_wq = qsub.enter_context(nc.sbuf_tensor("s_wq", [128, NF * F], BF16))
        s_x1c = qsub.enter_context(nc.sbuf_tensor("s_x1c", [128, NF * 512], BF16))
        s_qsq = qsub.enter_context(nc.sbuf_tensor("s_qsq", [128, 512], F32))
        s_qph = qsub.enter_context(nc.sbuf_tensor("s_qph", [128, 512], BF16))

        wq_ld = []
        wq_fold = []
        for kt in range(NF):
            ww = []
            if kt >= 2:
                ww.append(("ev", wq_fold[kt - 2]))
            wq_ld.append(p.op("sync", lambda e, kt=kt: e.dma_start(
                out=s_wq[:, kt * F:(kt + 1) * F], in_=d_bq[0:F, kt * 128:(kt + 1) * 128], transpose=True),
                waits=ww, inc=f"qw{kt % 2}", inck=16))
            wq_fold.append(p.op("vector", lambda e, kt=kt: e.tensor_scalar_mul(
                s_wq[:, kt * F:(kt + 1) * F], s_wq[:, kt * F:(kt + 1) * F], s_s1[:, kt:kt + 1]),
                waits=[(f"qw{kt % 2}", wq_ld[kt])], inc="ev"))

        dent_dma = [0, 0]
        mm_q_prev = 0
        ld_x1c = [0] * 4
        for c in range(4):
            cwx = list(x1sp_tot) if c == 0 else [("mm", m("mm"))]
            ld_x1c[c] = p.op("sync", lambda e, c=c: e.dma_start(
                out=s_x1c.ap().rearrange("pp (f cc) -> pp f cc", f=NF),
                in_=d_x1T.rearrange("(f pp) tok -> pp f tok", pp=128)[:, :, c * 512:(c + 1) * 512]),
                waits=cwx, inc="xc", inck=16)
            for pr in range(NF):
                qw = [("ev", wq_fold[-1]), ("xc", ld_x1c[c]), ("ac", m("ac"))]
                for kt in range(NF):
                    p.op("tensor", lambda e, pr=pr, kt=kt, c=c: e.matmul(
                        pA[:, 0:512], s_wq[:, kt * F + pr * 128: kt * F + pr * 128 + 128],
                        s_x1c[:, kt * 512:(kt + 1) * 512],
                        start=(kt == 0), stop=(kt == NF - 1)), waits=qw if kt == 0 else ())
                mmq = p.mark("tensor", "mm")
                acq1 = p.op("scalar", lambda e: e.activation(s_qsq[:, :], pA[:, 0:512], AF.Square),
                            waits=[("mm", mmq)], inc="ac")
                acq2 = p.op("scalar", lambda e: e.activation(s_qph[:, :], s_qsq[:, :], AF.Exp,
                            scale=-0.5), inc="ac")
                nw_ = [("ac", acq2), ("ev", ev25), ("ev", m("ev"))]
                p.op("tensor", lambda e, pr=pr: e.matmul(
                    pC[0:65, 0:512], s_kvb[0:64, pr * 65:(pr + 1) * 65],
                    s_qph[0:64, :], start=True, stop=True), waits=nw_)
                p.op("tensor", lambda e, pr=pr: e.matmul(
                    pC[0:65, 512:1024], s_kvb[64:128, pr * 65:(pr + 1) * 65],
                    s_qph[64:128, :], start=True, stop=True))
                mmn = p.mark("tensor", "mm")
                p.op("vector", lambda e, pr=pr, c=c: e.tensor_copy(
                    fm(s_poT, pr, c * 512, (c + 1) * 512)[0:64, :], pC[0:64, 0:512]),
                    waits=[("mm", mmn)])
                p.op("vector", lambda e, pr=pr, c=c: e.tensor_copy(
                    fm(s_poT, pr, c * 512, (c + 1) * 512)[64:128, :], pC[0:64, 512:1024]))
                dw0 = [("dn0", dent_dma[0])] if dent_dma[0] else []
                p.op("vector", lambda e: e.tensor_copy(s_dent[0][:, :], pC[64:65, 0:512]), waits=dw0)
                dw1 = [("dn1", dent_dma[1])] if dent_dma[1] else []
                p.op("vector", lambda e: e.tensor_copy(s_dent[1][:, :], pC[64:65, 512:1024]), waits=dw1)
                for hh in range(2):
                    p.op("vector", lambda e, hh=hh: e.tensor_scalar_add(
                        s_dent[hh][:, :], s_dent[hh][:, :], 1e-6))
                    p.op("vector", lambda e, hh=hh: e.reciprocal(s_dent[hh][:, :], s_dent[hh][:, :]))
                    evz = p.op("vector", lambda e: e.tensor_copy(s_vgap[:, :], s_vgap[:, :]),
                               inc="ev")
                    dent_dma[hh] = p.op("scalar", lambda e, pr=pr, hh=hh, c=c: e.dma_start(
                        out=s_zr[2 * pr + hh:2 * pr + hh + 1, c * 512:(c + 1) * 512],
                        in_=s_dent[hh][:, :]), waits=[("ev", evz)], inc=f"dn{hh}", inck=16)
            mm_q_prev = m("mm")
        qsub.close()
        zw = [("dn0", m("dn0")), ("dn1", m("dn1")), ("ac", m("ac")), ("ev", m("ev"))]
        first = True
        for ft in range(NF):
            for c in range(4):
                dst = pA if (c % 2 == 0) else pB
                sl = (ft % 2) * 512
                mmz = p.op("tensor", lambda e, ft=ft, c=c, dst=dst, sl=sl: e.matmul(
                    dst[:, sl:sl + 512], s_bo[:, ft * 128:(ft + 1) * 128],
                    s_zr[:, c * 512:(c + 1) * 512], start=True, stop=True),
                    waits=zw if first else [("ev", m("ev"))], inc="mm")
                first = False
                p.op("vector", lambda e, ft=ft, c=c, dst=dst, sl=sl: e.tensor_mul(
                    fm(s_poT, ft, c * 512, (c + 1) * 512), fm(s_poT, ft, c * 512, (c + 1) * 512),
                    dst[:, sl:sl + 512]), waits=[("mm", mmz)])
                p.mark("vector", "ev")
        poz_ev = m("ev")
        if debug:
            p.op("sync", lambda e: e.dma_start(out=dbg["dbg_poT"].rearrange("(f pp) tok -> pp f tok", pp=128),
                 in_=s_poT.ap().rearrange("pp (f tok) -> pp f tok", f=NF)),
                 waits=[("ev", poz_ev)], inc="dbg", inck=16)

        if upto < 6:
            raise _StopBuild()
        pjsub = ES2()
        s_wp = pjsub.enter_context(nc.sbuf_tensor("s_wp", [128, NF * F], BF16))
        s_xnt = [pjsub.enter_context(nc.sbuf_tensor(f"s_xnt{i}", [128, 512], F32)) for i in range(2)]
        wp_ld = []
        wp_fold = []
        for kt in range(NF):
            ww = [("wc", wcast_done), ("mm", m("mm")), ("ac", m("ac")),
                  ("ev", m("ev"))] if kt == 0 else []
            if kt >= 2:
                ww.append(("ev", wp_fold[kt - 2]))
            wp_ld.append(p.op("sync", lambda e, kt=kt: e.dma_start(
                out=s_wp[:, kt * F:(kt + 1) * F], in_=d_bp[0:F, kt * 128:(kt + 1) * 128], transpose=True),
                waits=ww, inc=f"pw{kt % 2}", inck=16))
            wp_fold.append(p.op("vector", lambda e, kt=kt: e.tensor_mul(
                s_wp[:, kt * F:(kt + 1) * F], s_wp[:, kt * F:(kt + 1) * F], s_gmb[:, :]),
                waits=[(f"pw{kt % 2}", wp_ld[kt])], inc="ev"))
        ev_pj = {}
        for ot in range(NF):
            for c in range(4):
                idx = ot * 4 + c
                lw = ([("ev", wp_fold[-1])] + list(xnsp_tot)) if idx == 0 else []
                if idx >= 2:
                    lw.append(("ev", ev_pj[idx - 2]))
                xl = p.op("sync", lambda e, ot=ot, c=c: e.dma_start(
                    out=s_xnt[c % 2][:, :], in_=d_xnT[ot * 128:(ot + 1) * 128, c * 512:(c + 1) * 512]),
                    waits=lw, inc=f"pj{c % 2}", inck=16)
                dst = pA if c % 2 == 0 else pB
                sl = (ot % 2) * 512
                for kt in range(NF):
                    p.op("tensor", lambda e, ot=ot, c=c, kt=kt, dst=dst, sl=sl: e.matmul(
                        dst[:, sl:sl + 512], s_wp[:, kt * F + ot * 128: kt * F + ot * 128 + 128],
                        fm(s_poT, kt, c * 512, (c + 1) * 512),
                        start=(kt == 0), stop=(kt == NF - 1)),
                        waits=([("ev", wp_fold[-1])] if idx == 0 else [("ev", m("ev"))]) if kt == 0 else ())
                mmp = p.mark("tensor", "mm")
                ev_pj[idx] = p.op("vector", lambda e, ot=ot, c=c, dst=dst, sl=sl: e.scalar_tensor_tensor(
                    fm(s_bigB, ot, c * 512, (c + 1) * 512), dst[:, sl:sl + 512],
                    s_pgb[:, ot:ot + 1], s_xnt[c % 2][:, :], ALU.add, ALU.add),
                    waits=[("mm", mmp), (f"pj{c % 2}", xl)], inc="ev")
        if debug:
            p.op("sync", lambda e: e.dma_start(out=dbg["dbg_x2T"].rearrange("(f pp) tok -> pp f tok", pp=128),
                 in_=s_bigB.ap().rearrange("pp (f tok) -> pp f tok", f=NF)),
                 waits=[("ev", m("ev"))], inc="dbg", inck=16)
        pjsub.close()
        phq.close()
        kvscope.close()
        p.barrier()

        if upto < 7:
            raise _StopBuild()
        phc = ES2()
        sbc = lambda n, s, d: phc.enter_context(nc.sbuf_tensor(n, s, d))
        s_wcv = sbc("s_wcv", [128, 16 * F], BF16)
        s_xc1 = sbc("s_xc1", [128, NF * 512], BF16)
        s_hdd = sbc("s_hdd", [128, 16 * 512], BF16)
        s_xsq = sbc("s_xsq", [128, 512], F32)
        s_r2 = sbc("s_r2", [1, 512], F32)
        s_xnt2 = [sbc(f"s_xnt2{i}", [128, 512], F32) for i in range(2)]

        for c in range(4):
            prev_mm = m("mm")
            for ft in range(NF):
                acs = p.op("scalar", lambda e, ft=ft, c=c: e.activation(
                    s_xsq[:, :], fm(s_bigB, ft, c * 512, (c + 1) * 512), AF.Square),
                    waits=[("mm", prev_mm)], inc="ac")
                prev_mm = p.op("tensor", lambda e, ft=ft: e.matmul(
                    pC[0:1, 0:512], s_ones128[:, 0:1], s_xsq[:, :],
                    start=(ft == 0), stop=(ft == NF - 1)), waits=[("ac", acs)], inc="mm")
            acr = p.op("scalar", lambda e: e.activation(s_r2[:, :], pC[0:1, 0:512], AF.Sqrt),
                       waits=[("mm", prev_mm)], inc="ac")
            p.op("vector", lambda e: e.reciprocal(s_r2[:, :], s_r2[:, :]), waits=[("ac", acr)])
            vgap()
            evr = p.op("vector", lambda e: e.tensor_scalar_mul(s_r2[:, :], s_r2[:, :], RMS_C), inc="ev")
            mmb = p.op("tensor", lambda e: e.matmul(pC[:, 512:1024], s_onesr[:, :], s_r2[:, :],
                       start=True, stop=True), waits=[("ev", evr)], inc="mm")
            for ft in range(NF):
                p.op("vector", lambda e, ft=ft, c=c: e.tensor_mul(
                    s_xc1[:, ft * 512:(ft + 1) * 512], fm(s_bigB, ft, c * 512, (c + 1) * 512),
                    pC[:, 512:1024]), waits=[("mm", mmb)] if ft == 0 else ())
            evx = p.mark("vector", "ev")
            w1_ld = []
            w1f = []
            for kt in range(NF):
                w1w = []
                if kt == 0:
                    w1w = [("mm", m("mm"))] if c > 0 else [("wc", wcast_done)]
                if kt >= 2:
                    w1w.append(("ev", w1f[kt - 2]))
                w1_ld.append(p.op("sync", lambda e, kt=kt: e.dma_start(
                    out=s_wcv[:, kt * HID:(kt + 1) * HID], in_=d_b1[0:HID, kt * 128:(kt + 1) * 128],
                    transpose=True), waits=w1w, inc=f"w1{kt % 2}", inck=16))
                w1f.append(p.op("vector", lambda e, kt=kt: e.tensor_scalar_mul(
                    s_wcv[:, kt * HID:(kt + 1) * HID], s_wcv[:, kt * HID:(kt + 1) * HID],
                    s_s2[:, kt:kt + 1]), waits=[(f"w1{kt % 2}", w1_ld[kt])], inc="ev"))
            ac_prev = [0, 0]
            for ot in range(16):
                dst = pA if ot % 2 == 0 else pB
                hw = [("ev", w1f[-1]), ("ev", evx)] if ot == 0 else []
                if ot >= 2:
                    hw.append(("ac", ac_prev[ot % 2]))
                for kt in range(NF):
                    p.op("tensor", lambda e, ot=ot, kt=kt, dst=dst: e.matmul(
                        dst[:, 0:512], s_wcv[:, kt * HID + ot * 128: kt * HID + ot * 128 + 128],
                        s_xc1[:, kt * 512:(kt + 1) * 512], start=(kt == 0), stop=(kt == NF - 1)),
                        waits=hw if kt == 0 else ())
                mmc = p.mark("tensor", "mm")
                ac_prev[ot % 2] = p.op("scalar", lambda e, ot=ot, dst=dst: e.activation(
                    s_hdd[:, ot * 512:(ot + 1) * 512], dst[:, 0:512], AF.Gelu,
                    bias=s_c1b[:, ot:ot + 1]), waits=[("mm", mmc)], inc="ac")
            hdd_ac = m("ac")
            w2_ld = []
            w2f = []
            for kt in range(16):
                w2w = [("ac", hdd_ac), ("wc", wcast_done)] if kt == 0 else []
                if kt >= 2:
                    w2w.append(("ev", w2f[kt - 2]))
                w2_ld.append(p.op("sync", lambda e, kt=kt: e.dma_start(
                    out=s_wcv[:, kt * F:(kt + 1) * F], in_=d_b2[0:F, kt * 128:(kt + 1) * 128],
                    transpose=True), waits=w2w, inc=f"w2{kt % 2}", inck=16))
                w2f.append(p.op("vector", lambda e, kt=kt: e.tensor_mul(
                    s_wcv[:, kt * F:(kt + 1) * F], s_wcv[:, kt * F:(kt + 1) * F], s_gmb[:, :]),
                    waits=[(f"w2{kt % 2}", w2_ld[kt])], inc="ev"))
            ev_c2 = {}
            for ot in range(NF):
                xw = [("ev", w2f[-1])] if ot == 0 else []
                if ot >= 2:
                    xw.append(("ev", ev_c2[ot - 2]))
                xl = p.op("sync", lambda e, ot=ot, c=c: e.dma_start(
                    out=s_xnt2[ot % 2][:, :], in_=d_xnT[ot * 128:(ot + 1) * 128, c * 512:(c + 1) * 512]),
                    waits=xw + (xnsp_tot if (c == 0 and ot == 0) else []), inc=f"cv{ot % 2}", inck=16)
                dst = pA if ot % 2 == 0 else pB
                for kt in range(16):
                    p.op("tensor", lambda e, ot=ot, kt=kt, dst=dst: e.matmul(
                        dst[:, 512:1024], s_wcv[:, kt * F + ot * 128: kt * F + ot * 128 + 128],
                        s_hdd[:, kt * 512:(kt + 1) * 512], start=(kt == 0), stop=(kt == 15)),
                        waits=([("ev", w2f[-1])] if ot == 0 else [("ev", ev_c2.get(ot - 2, 0))])
                        if kt == 0 else ())
                mm2 = p.mark("tensor", "mm")
                p.op("vector", lambda e, ot=ot, c=c, dst=dst: e.scalar_tensor_tensor(
                    fm(s_bigB, ot, c * 512, (c + 1) * 512), dst[:, 512:1024], s_c2gb[:, ot:ot + 1],
                    fm(s_bigB, ot, c * 512, (c + 1) * 512), ALU.add, ALU.add),
                    waits=[("mm", mm2), (f"cv{ot % 2}", xl)])
                ev_c2[ot] = p.op("vector", lambda e, ot=ot, c=c: e.tensor_add(
                    fm(s_bigB, ot, c * 512, (c + 1) * 512), fm(s_bigB, ot, c * 512, (c + 1) * 512),
                    s_xnt2[ot % 2][:, :]), inc="ev")
        phc.close()
        if debug:
            p.op("sync", lambda e: e.dma_start(out=dbg["dbg_acc"].rearrange("(f pp) tok -> pp f tok", pp=128),
                 in_=s_bigB.ap().rearrange("pp (f tok) -> pp f tok", f=NF)),
                 waits=[("ev", m("ev"))], inc="dbg", inck=16)
        p.barrier()

        if upto < 8:
            raise _StopBuild()
        phr = ES2()
        sbr = lambda n, s, d: phr.enter_context(nc.sbuf_tensor(n, s, d))
        s_wri = [sbr(f"s_wri{i}", [128, 8 * 1024], BF16) for i in range(2)]
        s_rh = sbr("s_rh", [128, 32 * 512], BF16)
        s_wro = sbr("s_wro", [128, 32 * 128], BF16)
        s_sA = sbr("s_sA", [128, 512], F32)
        s_sB = sbr("s_sB", [128, 512], F32)
        s_rme = sbr("s_rme", [128, 512], BF16)

        def gelu_chain(psum_fn, bias_col, biass_col, out_fn, scrA, scrB, pre_waits):
            a1 = p.op("scalar", lambda e: e.activation(scrA[:, :], psum_fn(), AF.Square,
                      bias=biass_col, scale=GELU_S), waits=pre_waits, inc="ac")
            p.op("vector", lambda e: e.tensor_scalar_add(scrB[:, :], scrA[:, :], 1.0),
                 waits=[("ac", a1)])
            e2 = p.op("vector", lambda e: e.scalar_tensor_tensor(
                scrA[:, :], psum_fn(), bias_col, scrB[:, :], ALU.add, ALU.mult), inc="ev")
            a2 = p.op("scalar", lambda e: e.activation(scrB[:, :], scrA[:, :], AF.Sigmoid,
                      scale=GELU_A), waits=[("ev", e2)], inc="ac")
            e3 = p.op("vector", lambda e: e.scalar_tensor_tensor(
                out_fn(), psum_fn(), bias_col, scrB[:, :], ALU.add, ALU.mult),
                waits=[("ac", a2)], inc="ev")
            return e3

        wmm_prev = {}
        rm_prev = [None]
        rch = []
        for c in range(4):
            for g in range(4):
                glw = []
                if not (c == 0 and g < 2):
                    glw.append(("mm", wmm_prev.get((c, g), m("mm"))))
                gl = p.op("gpsimd", lambda e, g=g: e.dma_start(
                    out=s_wri[g % 2][:, :],
                    in_=ri_w.rearrange("(kt pp) i -> pp kt i", pp=128)[:, :, g * 1024:(g + 1) * 1024]),
                    waits=glw, inc=f"ri{g % 2}", inck=16)
                for it in range(8):
                    ig = g * 8 + it
                    dst = pA if it % 2 == 0 else pB
                    mw = [(f"ri{g % 2}", gl)] if it == 0 else []
                    if len(rch) >= 2:
                        mw.append(("ev", rch[-2]))
                    for kt in range(NF):
                        p.op("tensor", lambda e, g=g, it=it, kt=kt, dst=dst, c=c: e.matmul(
                            dst[:, 0:512], s_wri[g % 2][:, kt * 1024 + it * 128: kt * 1024 + it * 128 + 128],
                            fm(s_aoT, kt, c * 512, (c + 1) * 512), start=(kt == 0), stop=(kt == NF - 1)),
                            waits=mw if kt == 0 else ())
                    mmi = p.mark("tensor", "mm")
                    psf = (lambda dst: (lambda: dst[:, 0:512]))(dst)
                    outf = (lambda ig: (lambda: s_rh[:, ig * 512:(ig + 1) * 512]))(ig)
                    rch.append(gelu_chain(psf, s_rib[:, ig:ig + 1], s_ribs[:, ig:ig + 1],
                                          outf, s_sA, s_sB, [("mm", mmi)]))
                wmm_prev[(c, g + 2)] = m("mm")
                wmm_prev[(c + 1, g - 2)] = m("mm")
            rh_ev = m("ev")
            for ot in range(NF):
                olw = [] if (c == 0 and ot == 0) else [("mm", m("mm"))]
                ol = p.op("gpsimd", lambda e, ot=ot: e.dma_start(
                    out=s_wro[:, :],
                    in_=ro_w.rearrange("(kt pp) o -> pp kt o", pp=128)[:, :, ot * 128:(ot + 1) * 128]),
                    waits=olw, inc="ro", inck=16)
                dst = pA if ot % 2 == 0 else pB
                ow = [("ro", ol), ("ev", rh_ev)]
                for kt in range(32):
                    p.op("tensor", lambda e, ot=ot, kt=kt, dst=dst: e.matmul(
                        dst[:, 512:1024], s_wro[:, kt * 128:(kt + 1) * 128],
                        s_rh[:, kt * 512:(kt + 1) * 512], start=(kt == 0), stop=(kt == 31)),
                        waits=ow if kt == 0 else ())
                mmo = p.mark("tensor", "mm")
                acw2 = [("mm", mmo)]
                if rm_prev[0] is not None:
                    acw2.append(rm_prev[0])
                aco = p.op("scalar", lambda e, ot=ot, dst=dst: e.activation(
                    s_rme[:, :], dst[:, 512:1024], AF.Identity, bias=s_rob[:, ot:ot + 1]),
                    waits=acw2, inc="ac")
                rmk = p.op("scalar", lambda e, ot=ot, c=c: e.dma_start(
                    out=d_rm[ot * 128:(ot + 1) * 128, c * 512:(c + 1) * 512], in_=s_rme[:, :]),
                    waits=[("ac", aco)], inc=f"rm{ot % 2}", inck=16)
                rm_prev[0] = (f"rm{ot % 2}", rmk)
        p.op("scalar", lambda e: e.dma_start(out=s_fbf[:, :], in_=d_rm[0:1, 0:1]),
             waits=[("rm0", m("rm0")), ("rm1", m("rm1"))], inc="rm0", inck=16)
        phr.close()
        p.barrier()

        if upto < 9:
            raise _StopBuild()
        phe = ES2()
        sbe = lambda n, s, d: phe.enter_context(nc.sbuf_tensor(n, s, d))
        s_dt = [sbe(f"s_dt{i}", [128, F], BF16) for i in range(2)]
        s_dispT = sbe("s_dispT", [128, NF * 512], BF16)
        s_h = sbe("s_h", [128, 32 * 512], BF16)
        s_wie = [sbe(f"s_wie{i}", [128, 8 * 1024], BF16) for i in range(2)]
        s_wo2 = sbe("s_wo2", [128, 8 * 512], BF16)
        s_eA = sbe("s_eA", [128, 512], F32)
        s_eB = sbe("s_eB", [128, 512], F32)
        s_eo = sbe("s_eo", [128, 512], BF16)
        s_opbe = sbe("s_opbe", [1, F], BF16)

        mm_dtr_prev = 0
        ev_dtr_lastexp = 0
        eo_prev = [None]
        ech = []
        for ee in range(8):
            opw = [("mm", m("mm"))] if ee > 0 else [("g3", g5)]
            opl = p.op("gpsimd", lambda e, ee=ee: e.dma_start(
                out=s_opbe[:, :], in_=opb_d[ee:ee + 1, :]), waits=opw, inc="ob", inck=16)
            ev_dtr = 0
            for sti in range(4):
                st = ee * 4 + sti
                gw = []
                if ee == 0 and sti == 0:
                    gw = [("g3", g5), ("a0", m("a0")), ("a1", m("a1"))]
                if sti >= 2:
                    gw.append(("mm", m("mm")))
                elif ee > 0:
                    gw.append(("mm", mm_dtr_prev))
                p.op("gpsimd", lambda e, sti=sti: e.memset(s_dt[sti % 2][:, :], 0.0), waits=gw)
                gpm = p.mark("gpsimd", "gp")
                gath = p.op("gpsimd", lambda e, sti=sti, st=st: e.indirect_dma_start(
                    s_dt[sti % 2][:, :], None, d_ao[:, :],
                    bass.IndirectOffsetOnAxis(s_tos[:, st:st + 1], 0),
                    bounds_check=bc_reg(e, T - 1), oob_is_err=False), waits=[("gp", gpm)],
                    inc=f"dt{sti % 2}", inck=16)
                tw = [(f"dt{sti % 2}", gath)]
                if sti > 0:
                    tw.append(("ev", ev_dtr))
                elif ee > 0:
                    tw.append(("ev", ev_dtr_lastexp))
                for ft in range(NF):
                    p.op("tensor", lambda e, sti=sti, ft=ft: e.transpose(
                        pTb[:, ft * 128:(ft + 1) * 128], s_dt[sti % 2][:, ft * 128:(ft + 1) * 128],
                        s_idb[:, :]), waits=tw if ft == 0 else ())
                mmtr = p.mark("tensor", "mm")
                cpw = [("mm", mmtr)]
                if sti == 0 and ee > 0:
                    cpw.append(("mm", mm_dtr_prev))
                ev_dtr = p.op("vector", lambda e, sti=sti: e.tensor_copy(
                    s_dispT.ap().rearrange("pp (f s) -> pp f s", f=NF)[:, :, sti * 128:(sti + 1) * 128],
                    pTb.ap().rearrange("pp (f cc) -> pp f cc", f=NF)), waits=cpw, inc="ev")
            ev_dtr_lastexp = ev_dtr
            mm_dtr_prev = m("mm")
            disp_ev = ev_dtr
            for g in range(4):
                glw = []
                if not (ee == 0 and g < 2):
                    glw.append(("mm", m("mm")))
                gl = p.op("gpsimd", lambda e, ee=ee, g=g: e.dma_start(
                    out=s_wie[g % 2][:, :],
                    in_=inter_w[ee * F:(ee + 1) * F, :].rearrange("(kt pp) i -> pp kt i", pp=128)
                    [:, :, g * 1024:(g + 1) * 1024]), waits=glw, inc=f"wi{g % 2}", inck=16)
                for it in range(8):
                    ig = g * 8 + it
                    dst = pA if it % 2 == 0 else pB
                    mw = [(f"wi{g % 2}", gl), ("ev", disp_ev)] if it == 0 else []
                    if len(ech) >= 2:
                        mw.append(("ev", ech[-2]))
                    for kt in range(NF):
                        p.op("tensor", lambda e, g=g, it=it, kt=kt, dst=dst: e.matmul(
                            dst[:, 0:512], s_wie[g % 2][:, kt * 1024 + it * 128: kt * 1024 + it * 128 + 128],
                            s_dispT[:, kt * 512:(kt + 1) * 512], start=(kt == 0), stop=(kt == NF - 1)),
                            waits=mw if kt == 0 else ())
                    mmi = p.mark("tensor", "mm")
                    psf = (lambda dst: (lambda: dst[:, 0:512]))(dst)
                    outf = (lambda ig: (lambda: s_h[:, ig * 512:(ig + 1) * 512]))(ig)
                    ech.append(gelu_chain(psf, s_ib[:, ee * 32 + ig: ee * 32 + ig + 1],
                                          s_ibs[:, ee * 32 + ig: ee * 32 + ig + 1], outf,
                                          s_eA, s_eB, [("mm", mmi)]))
            h_ev = m("ev")
            for oh in range(2):
                for kg in range(4):
                    olw = []
                    if not (ee == 0 and oh == 0 and kg == 0):
                        olw.append(("mm", m("mm")))
                    ol = p.op("gpsimd", lambda e, ee=ee, kg=kg, oh=oh: e.dma_start(
                        out=s_wo2[:, :],
                        in_=output_w[ee * INTER + kg * 1024: ee * INTER + (kg + 1) * 1024,
                                     oh * 512:(oh + 1) * 512]
                        .rearrange("(kt pp) o -> pp kt o", pp=128)), waits=olw, inc="wo", inck=16)
                    for kt in range(8):
                        kglob = kg * 8 + kt
                        for mt in range(4):
                            dstp = pA if mt < 2 else pB
                            doff = (mt % 2) * 512
                            mw = []
                            if kglob == 0 and mt == 0:
                                mw = [("wo", ol), ("ev", h_ev), ("ev", m("ev"))]
                            elif kt == 0 and mt == 0:
                                mw = [("wo", ol)]
                            p.op("tensor", lambda e, kg=kg, kt=kt, mt=mt, dstp=dstp, doff=doff,
                                 kglob=kglob: e.matmul(
                                dstp[:, doff:doff + 512],
                                s_h[:, kglob * 512 + mt * 128: kglob * 512 + mt * 128 + 128],
                                s_wo2[:, kt * 512:(kt + 1) * 512],
                                start=(kglob == 0), stop=False), waits=mw)
                    p.mark("tensor", "mm")
                for mt in range(4):
                    dstp = pA if mt < 2 else pB
                    doff = (mt % 2) * 512
                    bw = [("ob", opl), ("gp", gp_set)] if mt == 0 else []
                    p.op("tensor", lambda e, oh=oh, mt=mt, dstp=dstp, doff=doff: e.matmul(
                        dstp[:, doff:doff + 512], s_onesrb[:, :],
                        s_opbe[:, oh * 512:(oh + 1) * 512], start=False, stop=True), waits=bw)
                    mme = p.mark("tensor", "mm")
                    evw = [("mm", mme)]
                    if eo_prev[0] is not None:
                        evw.append(eo_prev[0])
                    eve = p.op("vector", lambda e, dstp=dstp, doff=doff: e.tensor_copy(
                        s_eo[:, :], dstp[:, doff:doff + 512]), waits=evw, inc="ev")
                    eon = f"eo{(oh * 4 + mt) % 2}"
                    eok = p.op("scalar", lambda e, ee=ee, oh=oh, mt=mt: e.dma_start(
                        out=d_eo[ee * 512 + mt * 128: ee * 512 + (mt + 1) * 128,
                                 oh * 512:(oh + 1) * 512], in_=s_eo[:, :]),
                        waits=[("ev", eve)], inc=eon, inck=16)
                    eo_prev[0] = (eon, eok)
        p.op("scalar", lambda e: e.dma_start(out=s_fbf[:, :], in_=d_eo[0:1, 0:1]),
             waits=[("eo0", m("eo0")), ("eo1", m("eo1"))], inc="eo0", inck=16)
        if debug:
            p.op("sync", lambda e: e.dma_start(out=dbg["dbg_dispT"][:, :], in_=s_dispT[:, :]),
                 waits=[("ev", m("ev")), ("mm", m("mm"))], inc="dbg", inck=16)
            p.op("sync", lambda e: e.dma_start(out=dbg["dbg_h"][:, :], in_=s_h[:, :]),
                 waits=[("ev", m("ev")), ("mm", m("mm"))], inc="dbg", inck=16)
        phe.close()
        p.barrier()
        if debug:
            p.op("sync", lambda e: e.dma_start(out=dbg["dbg_eo"][:, :], in_=d_eo[:, :]),
                 waits=[("eo0", m("eo0")), ("eo1", m("eo1"))], inc="dbg", inck=16)
            p.op("sync", lambda e: e.dma_start(out=dbg["dbg_ao"][:, :], in_=d_ao[:, :]),
                 inc="dbg", inck=16)
            p.op("sync", lambda e: e.dma_start(out=dbg["dbg_rm"][:, :], in_=d_rm[:, :]),
                 waits=[("rm0", m("rm0")), ("rm1", m("rm1"))], inc="dbg", inck=16)
            p.op("sync", lambda e: e.dma_start(out=dbg["dbg_Si"][:, :], in_=s_Si[:, :]),
                 inc="dbg", inck=16)
            p.op("sync", lambda e: e.dma_start(out=dbg["dbg_gc1"][:, :], in_=s_gc1[:, :]),
                 inc="dbg", inck=16)
            p.op("sync", lambda e: e.dma_start(out=dbg["dbg_ib"][:, :], in_=s_ib[:, :]),
                 inc="dbg", inck=16)
            p.op("sync", lambda e: e.dma_start(out=dbg["dbg_iotok"][:, :], in_=s_iotok[:, :]),
                 inc="dbg", inck=16)
            p.op("sync", lambda e: e.dma_start(out=dbg["dbg_Fi"][:, :], in_=s_Fi[:, :]),
                 inc="dbg", inck=16)
            p.op("sync", lambda e: e.dma_start(out=dbg["dbg_c0"][:, :], in_=s_c0[:, :]),
                 inc="dbg", inck=16)

        if upto < 10:
            raise _StopBuild()
        phf = ES2()
        sbf = lambda n, s, d: phf.enter_context(nc.sbuf_tensor(n, s, d))
        f_x = [sbf(f"f_x{i}", [128, F], F32) for i in range(2)]
        f_rm = [sbf(f"f_rm{i}", [128, F], BF16) for i in range(2)]
        f_moe = [sbf(f"f_moe{i}", [128, F], BF16) for i in range(2)]
        f_o1 = sbf("f_o1", [128, F], F32)
        f_out = [sbf(f"f_out{i}", [128, F], F32) for i in range(2)]

        ld_fx = [0] * NT
        ld_frm = [0] * NT
        wq_moe = [0] * NT
        ev_o2 = [0] * NT
        ev_done = [0] * NT
        fin_dma = [0] * NT
        for t in range(NT):
            par = t % 2
            xw = [("ev", ev_done[t - 2])] if t >= 2 else []
            ld_fx[t] = p.op("sync", lambda e, t=t, par=par: e.dma_start(
                out=f_x[par][:, :], in_=x_sh[t * 128:(t + 1) * 128, :]), waits=xw,
                inc=f"fx{par}", inck=16)
            frmw = [("rm0", m("rm0")), ("rm1", m("rm1"))] if t == 0 else (
                [("mm", m("mm"))] if t >= 2 else [])
            ld_frm[t] = p.op("sync", lambda e, t=t, par=par: e.dma_start(
                out=f_rm[par].ap().rearrange("pp (f c) -> pp f c", f=NF),
                in_=d_rm.rearrange("(f pp) tok -> pp f tok", pp=128)[:, :, t * 128:(t + 1) * 128]),
                waits=frmw, inc=f"fr{par}", inck=16)
            gw = [("ev", ev_o2[t - 2])] if t >= 2 else (
                [("eo0", m("eo0")), ("eo1", m("eo1"))] if t == 0 else [])
            p.op("gpsimd", lambda e, par=par: e.memset(f_moe[par][:, :], 0.0), waits=gw)
            gpm = p.mark("gpsimd", "gp")
            wq_moe[t] = p.op("gpsimd", lambda e, t=t, par=par: e.indirect_dma_start(
                f_moe[par][:, :], None, d_eo[:, :],
                bass.IndirectOffsetOnAxis(s_Si[:, t:t + 1], 0),
                bounds_check=bc_reg(e, NSLOT - 1), oob_is_err=False), waits=[("gp", gpm)],
                inc=f"mo{par}", inck=16)
            tw = [(f"fr{par}", ld_frm[t])]
            if t >= 1:
                tw.append(("ev", ev_o2[t - 1]))
            for ft in range(NF):
                p.op("tensor", lambda e, par=par, ft=ft: e.transpose(
                    pTb[:, ft * 128:(ft + 1) * 128], f_rm[par][:, ft * 128:(ft + 1) * 128], s_idb[:, :]),
                    waits=tw if ft == 0 else ())
            mmr = p.mark("tensor", "mm")
            p.op("vector", lambda e, t=t, par=par: e.scalar_tensor_tensor(
                f_o1[:, :], pTb[:, :], s_c0[:, t:t + 1], f_x[par][:, :], ALU.mult, ALU.add),
                waits=[("mm", mmr), (f"fx{par}", ld_fx[t]), (f"mo{par}", wq_moe[t])])
            ev_o2[t] = p.op("vector", lambda e, t=t, par=par: e.scalar_tensor_tensor(
                f_o1[:, :], f_moe[par][:, :], s_gc1[:, t:t + 1], f_o1[:, :], ALU.mult, ALU.add),
                inc="ev")
            for r in range(2):
                tw2 = [("ev", ev_o2[t])] if r == 0 else [("ev", m("ev"))]
                if t >= 1 and r == 0:
                    tw2.append(("ev", ev_done[t - 1]))
                if t >= 2 and r == 0:
                    tw2.append((f"fy{par}", fin_dma[t - 2]))
                for j in range(4):
                    ft = r * 4 + j
                    p.op("tensor", lambda e, ft=ft, j=j, t=t: e.transpose(
                        pTf[:, j * 128:(j + 1) * 128], fm(s_bigB, ft, t * 128, (t + 1) * 128),
                        s_id[:, :]), waits=tw2 if j == 0 else ())
                mmr2 = p.mark("tensor", "mm")
                p.op("vector", lambda e, r=r, t=t, par=par: e.scalar_tensor_tensor(
                    f_out[par][:, r * 512:(r + 1) * 512], pTf[:, :], 1.0,
                    f_o1[:, r * 512:(r + 1) * 512], ALU.mult, ALU.add), waits=[("mm", mmr2)])
                ev_done[t] = p.mark("vector", "ev")
            fw = [("ev", ev_done[t])]
            if t >= 2:
                fw.append((f"fy{par}", fin_dma[t - 2]))
            fin_dma[t] = p.op("sync", lambda e, t=t, par=par: e.dma_start(
                out=y[t * 128:(t + 1) * 128, :], in_=f_out[par][:, :]), waits=fw,
                inc=f"fy{par}", inck=16)
        p.op("sync", lambda e: e.nop(), waits=[("fy0", m("fy0")), ("fy1", m("fy1"))])
        phf.close()

    except _StopBuild:
        pass
    p.op("sync", lambda e: e.nop(),
         waits=[(s, p.m(s)) for s in list(p.CNT) if s not in ("bar",)])
    p.replay()
    try:
        es.close()
    except AssertionError:
        pass
    return nc, dbg


def _host_inputs(inputs, core):
    f32 = np.float32
    d = {}
    d["x_sh"] = inputs["L0_attn_x"].reshape(-1, F)[core * T:(core + 1) * T]
    d["qkv_w"] = inputs["L0_attn_qkv_w"]
    d["proj_w"] = inputs["L0_attn_proj_w"]
    d["conv1_w"] = inputs["L0_attn_conv1_w"]
    d["conv2_w"] = inputs["L0_attn_conv2_w"]
    d["inter_w"] = inputs["L0_ffn_inter_w"].reshape(8 * F, INTER)
    d["output_w"] = inputs["L0_ffn_output_w"].reshape(8 * INTER, F)
    d["ri_w"] = inputs["L0_ffn_res_inter_w"]
    d["ro_w"] = inputs["L0_ffn_res_output_w"]
    d["wgcat"] = np.concatenate([inputs["L0_ffn_wg"], inputs["L0_ffn_res_coef"]], axis=1)
    d["ident"] = np.eye(128, dtype=f32)
    d["iotok"] = np.arange(T, dtype=np.int32).reshape(NT, 128).T
    d["e512c"] = (np.arange(8, dtype=f32) * L).reshape(8, 1)
    d["prefm"] = np.broadcast_to((np.arange(8) < core).astype(f32)[None, :], (8, 8))
    bo = np.zeros((H, F), dtype=f32)
    for g in range(H):
        bo[g, g * 64:(g + 1) * 64] = 1.0
    d["bones"] = bo
    b = core // 4
    bm = np.zeros((128, 1), dtype=f32)
    bm[b * 64:(b + 1) * 64] = 1.0
    d["bmask"] = bm
    sel0 = np.zeros((128, 1), dtype=f32); sel0[0:64] = 1.0 if b == 0 else 0.0
    sel1 = np.zeros((128, 1), dtype=f32); sel1[64:128] = 1.0 if b == 1 else 0.0
    d["bsel0"] = sel0
    d["bsel1"] = sel1
    col = lambda v: v.reshape(-1, 128).T
    d["s1c"] = col(inputs["L0_attn_scale1"])
    d["s2c"] = col(inputs["L0_attn_scale2"])
    d["gmc"] = col(inputs["L0_attn_gamma"])
    d["nwc"] = col(inputs["L0_ffn_attn_nw"])
    d["nbc"] = col(inputs["L0_ffn_attn_nb"])
    d["pbc"] = col(inputs["L0_attn_proj_b"])
    d["c1bc"] = col(inputs["L0_attn_conv1_b"])
    d["c2bc"] = col(inputs["L0_attn_conv2_b"])
    d["ibc"] = inputs["L0_ffn_inter_b"].reshape(8, 32, 128).transpose(2, 0, 1).reshape(128, 8 * 32)
    d["ribc"] = col(inputs["L0_ffn_res_inter_b"])
    d["robc"] = col(inputs["L0_ffn_res_output_b"])
    d["nwr"] = inputs["L0_ffn_attn_nw"].reshape(1, F)
    d["nbr"] = inputs["L0_ffn_attn_nb"].reshape(1, F)
    d["gmr"] = inputs["L0_attn_gamma"].reshape(1, F)
    d["opb"] = inputs["L0_ffn_output_b"]
    return {k: np.ascontiguousarray(np.asarray(v)) for k, v in d.items()}


def _build_upto(debug, upto):
    try:
        return build_nc(debug=debug, upto=upto)
    except _StopBuild:
        raise


def _run(inputs, debug=False, upto=99):
    nc, dbg = build_nc(debug=debug, upto=upto)
    in_maps = [_host_inputs(inputs, c) for c in range(NCORES)]
    res = run_bass_kernel_spmd(nc, in_maps, core_ids=list(range(NCORES)))
    outs = [np.asarray(res.results[c]["y"]) for c in range(NCORES)]
    yfull = np.concatenate(outs, axis=0).reshape(2, 8192, F)
    if debug:
        return yfull, res
    return yfull


def kernel(**inputs):
    return _run(inputs, debug=False)


# version 06
# speedup vs baseline: 1.2306x; 1.0119x over previous
import numpy as np
import concourse.bass as bass
import concourse.mybir as mybir
from concourse.bass_utils import run_bass_kernel_spmd

F32 = mybir.dt.float32
BF16 = mybir.dt.bfloat16
I32 = mybir.dt.int32
AF = mybir.ActivationFunctionType
ALU = mybir.AluOpType
AX = mybir.AxisListType

NCORES = 8
T = 2048
NT = 16
F = 1024
NF = 8
H = 16
CAP = 2048.0
L = 512
NSLOT = 8 * L
NST = NSLOT // 128
INTER = 4096
HID = 2048
SBIG = 1048576.0
GELU_S = float(np.float32(np.sqrt(0.044715)))
GELU_A = float(np.float32(2.0 * np.sqrt(2.0 / np.pi)))
RMS_C = float(np.float32(1024.0 ** 0.5 + 1e-6))

ENGN = ["sync", "scalar", "vector", "tensor", "gpsimd"]


class Prog:
    def __init__(self, nc, es):
        self.nc = nc
        self.es = es
        self.q = {en: [] for en in ENGN}
        names = ["mm", "ev", "ac", "gp", "cc", "bar"]
        self.SEM = {n: es.enter_context(nc.semaphore(n)) for n in names}
        self.CNT = {n: 0 for n in names}

    def sem(self, name):
        if name not in self.SEM:
            self.SEM[name] = self.es.enter_context(self.nc.semaphore(name))
            self.CNT[name] = 0
        return name

    def op(self, eng, fn, waits=(), inc=None, inck=1):
        for s, v in waits:
            self.sem(s)
        wl = [(s, v) for (s, v) in waits if v > 0]
        if inc is not None:
            self.sem(inc)
            self.CNT[inc] += inck
        SEM = self.SEM

        def run(e, fn=fn, wl=wl, inc=inc, inck=inck):
            for s, v in wl:
                e.wait_ge(SEM[s], v)
            inst = fn(e)
            if inc is not None:
                inst.then_inc(SEM[inc], inck)
        self.q[eng].append(run)
        return self.CNT[inc] if inc is not None else None

    def m(self, name):
        return self.CNT[name]

    def mark(self, eng, sem, waits=()):
        self.sem(sem)
        for s, v in waits:
            self.sem(s)
        self.CNT[sem] += 1
        SEM = self.SEM
        wl = [(s, v) for (s, v) in waits if v > 0]

        def run(e, wl=wl, sem=sem):
            for s, v in wl:
                e.wait_ge(SEM[s], v)
            e.drain().then_inc(SEM[sem], 1)
        self.q[eng].append(run)
        return self.CNT[sem]

    def barrier(self):
        self.CNT["bar"] += 5
        tgt = self.CNT["bar"]
        for en in ENGN:
            def run(e, tgt=tgt):
                e.drain().then_inc(self.SEM["bar"], 1)
                e.wait_ge(self.SEM["bar"], tgt)
            self.q[en].append(run)

    def replay(self):
        with self.nc.Block() as blk:
            for en in ENGN:
                def section(e, en=en):
                    for f in self.q[en]:
                        f(e)
                getattr(blk, en)(section)


class _StopBuild(Exception):
    pass


def build_nc(debug=False, upto=99, p0_tiles=16):
    nc = bass.Bass("TRN2", target_bir_lowering=False)
    dtn = nc.dram_tensor

    x_sh = dtn("x_sh", [T, F], F32, kind="ExternalInput").ap()
    qkv_w = dtn("qkv_w", [3 * F, F], F32, kind="ExternalInput").ap()
    proj_w = dtn("proj_w", [F, F], F32, kind="ExternalInput").ap()
    conv1_w = dtn("conv1_w", [HID, F], F32, kind="ExternalInput").ap()
    conv2_w = dtn("conv2_w", [F, HID], F32, kind="ExternalInput").ap()
    inter_w = dtn("inter_w", [8 * F, INTER], F32, kind="ExternalInput").ap()
    output_w = dtn("output_w", [8 * INTER, F], F32, kind="ExternalInput").ap()
    ri_w = dtn("ri_w", [F, INTER], F32, kind="ExternalInput").ap()
    ro_w = dtn("ro_w", [INTER, F], F32, kind="ExternalInput").ap()
    wgcat = dtn("wgcat", [F, 10], F32, kind="ExternalInput").ap()
    ident = dtn("ident", [128, 128], F32, kind="ExternalInput").ap()
    iotok = dtn("iotok", [128, NT], I32, kind="ExternalInput").ap()
    e512c = dtn("e512c", [8, 1], F32, kind="ExternalInput").ap()
    prefm = dtn("prefm", [8, 8], F32, kind="ExternalInput").ap()
    bones = dtn("bones", [H, F], F32, kind="ExternalInput").ap()
    bmask = dtn("bmask", [128, 1], F32, kind="ExternalInput").ap()
    bsel0 = dtn("bsel0", [128, 1], F32, kind="ExternalInput").ap()
    bsel1 = dtn("bsel1", [128, 1], F32, kind="ExternalInput").ap()
    s1c_d = dtn("s1c", [128, NF], F32, kind="ExternalInput").ap()
    s2c_d = dtn("s2c", [128, NF], F32, kind="ExternalInput").ap()
    gmc_d = dtn("gmc", [128, NF], F32, kind="ExternalInput").ap()
    nwc_d = dtn("nwc", [128, NF], F32, kind="ExternalInput").ap()
    nbc_d = dtn("nbc", [128, NF], F32, kind="ExternalInput").ap()
    pbc_d = dtn("pbc", [128, NF], F32, kind="ExternalInput").ap()
    c1bc_d = dtn("c1bc", [128, HID // 128], F32, kind="ExternalInput").ap()
    c2bc_d = dtn("c2bc", [128, NF], F32, kind="ExternalInput").ap()
    ibc_d = dtn("ibc", [128, 8 * 32], F32, kind="ExternalInput").ap()
    ribc_d = dtn("ribc", [128, 32], F32, kind="ExternalInput").ap()
    robc_d = dtn("robc", [128, NF], F32, kind="ExternalInput").ap()
    nwr_d = dtn("nwr", [1, F], F32, kind="ExternalInput").ap()
    nbr_d = dtn("nbr", [1, F], F32, kind="ExternalInput").ap()
    gmr_d = dtn("gmr", [1, F], F32, kind="ExternalInput").ap()
    opb_d = dtn("opb", [8, F], F32, kind="ExternalInput").ap()

    y = dtn("y", [T, F], F32, kind="ExternalOutput").ap()

    d_bq = dtn("d_bq", [3 * F, F], BF16).ap()
    d_bp = dtn("d_bp", [F, F], BF16).ap()
    d_b1 = dtn("d_b1", [HID, F], BF16).ap()
    d_b2 = dtn("d_b2", [F, HID], BF16).ap()
    d_xnT = dtn("d_xnT", [F, T], F32).ap()
    d_x1T = dtn("d_x1T", [F, T], BF16).ap()
    d_ao = dtn("d_ao", [T, F], BF16).ap()
    d_rm = dtn("d_rm", [F, T], BF16).ap()
    d_eo = dtn("d_eo", [NSLOT, F], BF16).ap()
    d_tos = dtn("d_tos", [128, NST], I32).ap()
    cc_cnt_i = dtn("cc_cnt_i", [8, 1], F32).ap()
    cc_cnt_o = dtn("cc_cnt_o", [8 * NCORES, 1], F32).ap()
    cc_kv_i = dtn("cc_kv_i", [128, 1040], F32).ap()
    cc_kv_o = dtn("cc_kv_o", [128, 1040], F32).ap()

    dbg = {}
    if debug:
        for name, shape, ddt in [
            ("dbg_xnT", [F, T], F32), ("dbg_Srow", [1, T], F32),
            ("dbg_kv", [64, 1040], F32), ("dbg_poT", [F, T], BF16),
            ("dbg_x2T", [F, T], F32), ("dbg_cnts", [8, 8], F32),
            ("dbg_loc", [8, T], F32), ("dbg_acc", [F, T], F32),
            ("dbg_g", [128, NT], F32), ("dbg_tos", [128, NST], I32),
            ("dbg_eo", [NSLOT, F], BF16), ("dbg_ao", [T, F], BF16),
            ("dbg_dispT", [128, 8 * 512], BF16), ("dbg_h", [128, 32 * 512], BF16),
            ("dbg_rm", [F, T], BF16), ("dbg_Si", [128, NT], I32),
            ("dbg_gc1", [128, NT], F32), ("dbg_c0", [128, NT], F32),
            ("dbg_A", [128, 3 * NT], F32), ("dbg_V", [128, NT], F32),
            ("dbg_offs", [8, 1], F32), ("dbg_loc1", [8, T], F32),
            ("dbg_m1k", [8, T], F32), ("dbg_lgt", [128, 10 * NT], F32),
            ("dbg_ib", [128, 8 * 32], F32), ("dbg_iotok", [128, NT], I32),
            ("dbg_Fi", [128, NT], I32),
        ]:
            dbg[name] = dtn(name, shape, ddt, kind="ExternalOutput").ap()

    from contextlib import ExitStack
    es = ExitStack()
    p = Prog(nc, es)
    vgap = lambda: p.op("vector", lambda e: e.tensor_copy(s_vgap[:, :], s_vgap[:, :]))
    m = p.m

    sbt = lambda n, s, d: es.enter_context(nc.sbuf_tensor(n, s, d))

    s_id = sbt("s_id", [128, 128], F32)
    s_idb = sbt("s_idb", [128, 128], BF16)
    s_iotok = sbt("s_iotok", [128, NT], I32)
    s_e512 = sbt("s_e512", [8, 1], F32)
    s_pref = sbt("s_pref", [8, 8], F32)
    s_wgk = sbt("s_wgk", [128, 8 * 10], F32)
    s_bm = sbt("s_bm", [128, 1], F32)
    s_b0 = sbt("s_b0", [128, 1], F32)
    s_b1c = sbt("s_b1c", [128, 1], F32)
    s_s1 = sbt("s_s1", [128, NF], F32)
    s_s2 = sbt("s_s2", [128, NF], F32)
    s_gm = sbt("s_gm", [128, NF], F32)
    s_nw = sbt("s_nw", [128, NF], F32)
    s_nb = sbt("s_nb", [128, NF], F32)
    s_pb = sbt("s_pb", [128, NF], F32)
    s_pgb = sbt("s_pgb", [128, NF], F32)
    s_c1b = sbt("s_c1b", [128, HID // 128], F32)
    s_c2b = sbt("s_c2b", [128, NF], F32)
    s_c2gb = sbt("s_c2gb", [128, NF], F32)
    s_ib = sbt("s_ib", [128, 8 * 32], F32)
    s_ibs = sbt("s_ibs", [128, 8 * 32], F32)
    s_rib = sbt("s_rib", [128, 32], F32)
    s_ribs = sbt("s_ribs", [128, 32], F32)
    s_rob = sbt("s_rob", [128, NF], F32)
    s_ones128 = sbt("s_ones128", [128, 1], F32)
    s_ones8 = sbt("s_ones8", [8, 1], F32)
    s_onesr = sbt("s_onesr", [1, 128], F32)
    s_onesrb = sbt("s_onesrb", [1, 128], BF16)
    s_ones128b = sbt("s_ones128b", [128, 1], BF16)
    s_cnt = sbt("s_cnt", [8, 1], F32)
    s_vgap = sbt("s_vgap", [128, 1], F32)
    s_fi32 = sbt("s_fi32", [1, 1], I32)
    s_fbf = sbt("s_fbf", [1, 1], BF16)
    if debug:
        s_dbglgt = sbt("s_dbglgt", [128, 10 * NT], F32)
    s_cnts = sbt("s_cnts", [8, 8], F32)
    s_offs = sbt("s_offs", [8, 1], F32)
    s_gcol = sbt("s_gcol", [128, NT], F32)
    s_c0 = sbt("s_c0", [128, NT], F32)
    s_gc1 = sbt("s_gc1", [128, NT], F32)
    s_Si = sbt("s_Si", [128, NT], I32)
    s_Si2 = sbt("s_Si2", [128, NT], I32)
    s_Fi = sbt("s_Fi", [128, NT], I32)
    s_tos = sbt("s_tos", [128, NST], I32)
    s_gmb = sbt("s_gmb", [128, F], BF16)
    s_bigB = sbt("s_bigB", [128, NF * T], F32)
    s_aoT = sbt("s_aoT", [128, NF * T], BF16)

    pA = es.enter_context(nc.psum_tensor("pA", [128, 1024], F32))
    pB = es.enter_context(nc.psum_tensor("pB", [128, 1024], F32))
    pC = es.enter_context(nc.psum_tensor("pC", [128, 1024], F32))
    pTf = es.enter_context(nc.psum_tensor("pTf", [128, 512], F32))
    pTb = es.enter_context(nc.psum_tensor("pTb", [128, 1024], BF16))

    def fm(handle, ft, a, b):
        return handle[:, ft * T + a: ft * T + b]

    _bcregs = {}

    def bc_reg(e, val):
        if val not in _bcregs:
            _bcregs[val] = e.alloc_register(f"bc{val}")
        r = _bcregs[val]
        e.reg_mov(r, val)
        return r

    try:
        for src, dst in [(ident, s_id), (iotok, s_iotok), (e512c, s_e512), (prefm, s_pref),
                         (bmask, s_bm), (bsel0, s_b0), (bsel1, s_b1c),
                         (s1c_d, s_s1), (s2c_d, s_s2), (gmc_d, s_gm), (nwc_d, s_nw),
                         (nbc_d, s_nb), (pbc_d, s_pb), (c1bc_d, s_c1b), (c2bc_d, s_c2b),
                         (ibc_d, s_ib), (ribc_d, s_rib), (robc_d, s_rob)]:
            p.op("sync", lambda e, s=src, d=dst: e.dma_start(out=d[:, :], in_=s[:, :]), inc="c0", inck=16)
        p.op("sync", lambda e: e.dma_start(
            out=s_wgk.ap().rearrange("pp (k c) -> pp k c", k=8),
            in_=wgcat.rearrange("(k pp) c -> pp k c", pp=128)), inc="c0", inck=16)
        consts_ld = m("c0")
        for src, dst in [(qkv_w, d_bq), (proj_w, d_bp), (conv1_w, d_b1), (conv2_w, d_b2)]:
            p.op("gpsimd", lambda e, s=src, d=dst: e.dma_start(out=d[:, :], in_=s[:, :]), inc="wc", inck=16)
        p.op("gpsimd", lambda e: e.dma_start(out=s_fbf[:, :], in_=d_b2[0:1, 0:1]),
             waits=[("wc", m("wc"))], inc="wc", inck=16)
        wcast_done = m("wc")
        for buf, v in [(s_ones128, 1.0), (s_ones8, 1.0), (s_onesr, 1.0), (s_onesrb, 1.0),
                       (s_ones128b, 1.0)]:
            p.op("gpsimd", lambda e, b=buf, v=v: e.memset(b[:, :], v))
        gp_set = p.mark("gpsimd", "gp")
        p.op("vector", lambda e: e.tensor_copy(s_idb[:, :], s_id[:, :]),
             waits=[("c0", consts_ld), ("gp", gp_set)])
        p.op("vector", lambda e: e.tensor_mul(s_pgb[:, :], s_pb[:, :], s_gm[:, :]))
        p.op("vector", lambda e: e.tensor_mul(s_c2gb[:, :], s_c2b[:, :], s_gm[:, :]))
        p.op("vector", lambda e: e.tensor_scalar_mul(s_ibs[:, :], s_ib[:, :], GELU_S))
        p.op("vector", lambda e: e.tensor_scalar_mul(s_ribs[:, :], s_rib[:, :], GELU_S), inc="ev")
        p.barrier()

        if upto < 1:
            raise _StopBuild()
        from contextlib import ExitStack as ES2
        kvscope = ES2()
        s_kv = kvscope.enter_context(nc.sbuf_tensor("s_kv", [64, 1040], F32))
        s_kvst = kvscope.enter_context(nc.sbuf_tensor("s_kvst", [128, 1040], F32))
        s_kvtmp = kvscope.enter_context(nc.sbuf_tensor("s_kvtmp", [64, 1040], F32))
        mscope = ES2()
        s_m1T = mscope.enter_context(nc.sbuf_tensor("s_m1T", [8, T], F32))
        s_loc = mscope.enter_context(nc.sbuf_tensor("s_loc", [8, T], F32))
        ph0 = ES2()
        sb0 = lambda n, s, d: ph0.enter_context(nc.sbuf_tensor(n, s, d))
        rows = ES2()
        s_nwrow = rows.enter_context(nc.sbuf_tensor("s_nwrow", [1, F], F32))
        s_nbrow = rows.enter_context(nc.sbuf_tensor("s_nbrow", [1, F], F32))
        s_gmrow = rows.enter_context(nc.sbuf_tensor("s_gmrow", [1, F], F32))
        for src, dst in [(nwr_d, s_nwrow), (nbr_d, s_nbrow), (gmr_d, s_gmrow)]:
            p.op("sync", lambda e, s=src, d=dst: e.dma_start(out=d[:, :], in_=s[:, :]), inc="c1", inck=16)
        rows_ld = m("c1")
        for c in range(2):
            p.op("tensor", lambda e, c=c: e.matmul(pA[:, c * 512:(c + 1) * 512], s_onesr[:, :],
                 s_nwrow[:, c * 512:(c + 1) * 512], start=True, stop=True),
                 waits=[("c1", rows_ld), ("gp", gp_set)] if c == 0 else ())
            p.op("tensor", lambda e, c=c: e.matmul(pB[:, c * 512:(c + 1) * 512], s_onesr[:, :],
                 s_nbrow[:, c * 512:(c + 1) * 512], start=True, stop=True))
            p.op("tensor", lambda e, c=c: e.matmul(pC[:, c * 512:(c + 1) * 512], s_onesr[:, :],
                 s_gmrow[:, c * 512:(c + 1) * 512], start=True, stop=True))
        bc_mm = p.mark("tensor", "mm")
        rows.close()
        s_nwb = sb0("s_nwb", [128, F], F32)
        s_nbb = sb0("s_nbb", [128, F], F32)
        p.op("vector", lambda e: e.tensor_copy(s_nwb[:, :], pA[:, :]), waits=[("mm", bc_mm)])
        p.op("vector", lambda e: e.tensor_copy(s_nbb[:, :], pB[:, :]))
        bc_ev = p.op("vector", lambda e: e.tensor_copy(s_gmb[:, :], pC[:, :]), inc="ev")

        s_xt = [sb0(f"s_xt{i}", [128, F], F32) for i in range(2)]
        s_xn = [sb0(f"s_xn{i}", [128, F], F32) for i in range(2)]
        s_z2 = [sb0(f"s_z2{i}", [128, F], F32) for i in range(2)]
        s_scr = sb0("s_scr", [128, F], F32)
        s_aotmp = [sb0(f"s_aotmp{i}", [128, F], F32) for i in range(2)]
        s_aotok = [sb0(f"s_aotok{i}", [128, F], BF16) for i in range(2)]
        s_xr1tok = [sb0(f"s_xr1tok{i}", [128, F], BF16) for i in range(2)]
        s_xnsp = [sb0(f"s_xnsp{i}", [128, F], F32) for i in range(2)]
        s_x1sp = [sb0(f"s_x1sp{i}", [128, F], BF16) for i in range(2)]
        s_m1t = sb0("s_m1t", [128, 8], F32)
        s_lgt = sb0("s_lgt", [128, 10], F32)
        s_cA = sb0("s_cA", [128, 1], F32)
        s_cB = sb0("s_cB", [128, 1], F32)
        s_cC = sb0("s_cC", [128, 1], F32)
        s_cD = sb0("s_cD", [128, 1], F32)
        s_cE = sb0("s_cE", [128, 1], F32)
        if debug:
            s_dbgA = sb0("s_dbgA", [128, 3 * NT], F32)
            s_dbgV = sb0("s_dbgV", [128, NT], F32)

        ac_xfree = [0] * NT
        mm_xntr = [0] * NT
        mm_z2tr = [0] * NT
        mm_xrtr = [0] * NT
        mm_lg = [0] * NT
        ev_aoTc = [0] * NT
        ev_ao9 = [0] * NT
        sc_ao = [0] * NT
        ld_spl = [0] * NT
        ld_x1sp = [0] * NT
        ev_ptf_last = [0] * NT
        ev_lgt_t = [0] * NT
        ev_xr1c_t = [0] * NT

        for t in range(p0_tiles):
            par = t % 2
            lw = [("ev", bc_ev)] if t == 0 else []
            if t >= 2:
                lw.append(("ac", ac_xfree[t - 2]))
            ldx = p.op("sync", lambda e, t=t, par=par: e.dma_start(out=s_xt[par][:, :],
                       in_=x_sh[t * 128:(t + 1) * 128, :]), waits=lw, inc=f"x{par}", inck=16)
            p.op("vector", lambda e, par=par: e.reduce_sum(s_cA[:, :], s_xt[par][:, :], axis=AX.X),
                 waits=[(f"x{par}", ldx)])
            vgap()
            ev1 = p.op("vector", lambda e: e.tensor_scalar_mul(s_cB[:, :], s_cA[:, :], -1.0 / F), inc="ev")
            acw = [("ev", ev1)]
            if t >= 1:
                acw.append(("ev", ev_ao9[t - 1]))
            if debug:
                p.op("scalar", lambda e, t=t: e.activation(
                    s_dbgA[:, 3 * t:3 * t + 1], s_cB[:, 0:1], AF.Identity), waits=list(acw), inc="ac")
            ac1 = p.op("scalar", lambda e, par=par: e.activation(s_scr[:, :], s_xt[par][:, :], AF.Square,
                       bias=s_cB[:, 0:1], scale=1.0, accum_out=s_cA[:, 0:1]), waits=acw, inc="ac")
            p.op("vector", lambda e: e.tensor_scalar(s_cC[:, :], s_cA[:, :], 1.0 / F, 1e-5,
                 ALU.mult, ALU.add), waits=[("ac", ac1)])
            ev2 = p.mark("vector", "ev")
            ac2 = p.op("scalar", lambda e: e.activation(s_cC[:, :], s_cC[:, :], AF.Sqrt),
                       waits=[("ev", ev2)], inc="ac")
            p.op("vector", lambda e: e.reciprocal(s_cD[:, :], s_cC[:, :]), waits=[("ac", ac2)])
            vgap()
            ev3 = p.op("vector", lambda e: e.tensor_mul(s_cE[:, :], s_cB[:, :], s_cD[:, :]), inc="ev")
            if debug:
                p.op("vector", lambda e, t=t: e.tensor_copy(s_dbgV[:, t:t + 1], s_cE[:, 0:1]))
            xnw = [("ev", ev3)]
            if t >= 2:
                xnw += [("mm", mm_xntr[t - 2]), ("mm", mm_xrtr[t - 2]), ("mm", mm_z2tr[t - 2])]
            if debug:
                p.op("scalar", lambda e, t=t: e.activation(
                    s_dbgA[:, 3 * t + 1:3 * t + 2], s_cD[:, 0:1], AF.Identity),
                    waits=list(xnw), inc="ac")
                p.op("scalar", lambda e, t=t: e.activation(
                    s_dbgA[:, 3 * t + 2:3 * t + 3], s_cE[:, 0:1], AF.Identity), inc="ac")
            acxn = p.op("scalar", lambda e, par=par: e.activation(s_xn[par][:, :], s_xt[par][:, :], AF.Identity,
                        bias=s_cE[:, 0:1], scale=s_cD[:, 0:1]), waits=xnw, inc="ac")
            ac_xfree[t] = acxn
            p.op("vector", lambda e, par=par: e.reduce_sum(s_cA[:, :], s_xn[par][:, :], axis=AX.X),
                 waits=[("ac", acxn)])
            vgap()
            ev4 = p.op("vector", lambda e: e.tensor_scalar_mul(s_cB[:, :], s_cA[:, :], -1.0 / F), inc="ev")
            ac3 = p.op("scalar", lambda e, par=par: e.activation(s_scr[:, :], s_xn[par][:, :], AF.Square,
                       bias=s_cB[:, 0:1], scale=1.0, accum_out=s_cA[:, 0:1]), waits=[("ev", ev4)], inc="ac")
            p.op("vector", lambda e: e.tensor_scalar(s_cC[:, :], s_cA[:, :], 1.0 / F, 1e-12,
                 ALU.mult, ALU.add), waits=[("ac", ac3)])
            ev5 = p.mark("vector", "ev")
            ac4 = p.op("scalar", lambda e: e.activation(s_cC[:, :], s_cC[:, :], AF.Sqrt),
                       waits=[("ev", ev5)], inc="ac")
            p.op("vector", lambda e: e.reciprocal(s_cD[:, :], s_cC[:, :]), waits=[("ac", ac4)])
            vgap()
            ev6 = p.op("vector", lambda e: e.tensor_mul(s_cE[:, :], s_cB[:, :], s_cD[:, :]), inc="ev")
            z2w = [("ev", ev6)]
            if t >= 2:
                z2w += [("ev", ev_ao9[t - 2])]
            acz2 = p.op("scalar", lambda e, par=par: e.activation(s_z2[par][:, :], s_xn[par][:, :], AF.Identity,
                        bias=s_cE[:, 0:1], scale=s_cD[:, 0:1]), waits=z2w, inc="ac")
            ac5 = p.op("scalar", lambda e, par=par: e.activation(s_scr[:, :], s_xn[par][:, :], AF.Square,
                       bias=0.0, scale=1.0, accum_out=s_cA[:, 0:1]), inc="ac")
            ac6 = p.op("scalar", lambda e: e.activation(s_cB[:, :], s_cA[:, :], AF.Sqrt), inc="ac")
            p.op("vector", lambda e: e.reciprocal(s_cC[:, :], s_cB[:, :]), waits=[("ac", ac6)])
            vgap()
            ev7 = p.op("vector", lambda e: e.tensor_scalar_mul(s_cD[:, :], s_cC[:, :], RMS_C), inc="ev")
            xrw = [("ev", ev7)]
            if t >= 2:
                xrw.append(("mm", mm_xrtr[t - 2]))
            acxr = p.op("scalar", lambda e, par=par: e.activation(s_xr1tok[par][:, :], s_xn[par][:, :],
                        AF.Identity, bias=0.0, scale=s_cD[:, 0:1]), waits=xrw, inc="ac")
            aow = [("ac", acz2), ("ac", ac5)]
            if t >= 2:
                aow.append((f"a{par}", sc_ao[t - 2]))
            p.op("vector", lambda e, par=par: e.tensor_mul(s_scr[:, :], s_z2[par][:, :], s_nwb[:, :]),
                 waits=aow)
            ev_ao9[t] = p.op("vector", lambda e, par=par: e.tensor_add(s_aotok[par][:, :], s_scr[:, :],
                             s_nbb[:, :]), inc="ev")
            sc_ao[t] = p.op("scalar", lambda e, t=t, par=par: e.dma_start(
                out=d_ao[t * 128:(t + 1) * 128, :], in_=s_aotok[par][:, :]),
                waits=[("ev", ev_ao9[t])], inc=f"a{par}", inck=16)
            ev_rounds = []
            for r in range(2):
                tw = [("ac", acxn)]
                if t == 0 and r == 0:
                    pass
                elif r == 0:
                    tw.append(("ev", ev_ptf_last[t - 1]))
                else:
                    tw.append(("ev", ev_rounds[0]))
                if r == 0 and t >= 2:
                    tw.append((f"sp{par}", ld_spl[t - 2]))
                for j in range(4):
                    ft = r * 4 + j
                    p.op("tensor", lambda e, par=par, ft=ft, j=j: e.transpose(
                        pTf[:, j * 128:(j + 1) * 128], s_xn[par][:, ft * 128:(ft + 1) * 128], s_id[:, :]),
                        waits=tw if j == 0 else ())
                mmr = p.mark("tensor", "mm")
                ev_rounds.append(p.op("vector", lambda e, par=par, r=r: e.tensor_copy(
                    s_xnsp[par][:, r * 512:(r + 1) * 512], pTf[:, :]), waits=[("mm", mmr)], inc="ev"))
            mm_xntr[t] = m("mm")
            ld_spl[t] = p.op("sync", lambda e, t=t, par=par: e.dma_start(
                out=d_xnT.rearrange("(f pp) tok -> pp f tok", pp=128)[:, :, t * 128:(t + 1) * 128],
                in_=s_xnsp[par].ap().rearrange("pp (f c) -> pp f c", f=NF)),
                waits=[("ev", ev_rounds[1])], inc=f"sp{par}", inck=16)
            ac_blocks = []
            for r in range(2):
                if r == 0:
                    tw = [("ac", acz2), ("ev", ev_rounds[1])]
                    if t >= 2:
                        tw += [("mm", mm_lg[t - 2]), ("ev", ev_aoTc[t - 2])]
                else:
                    tw = [("ac", m("ac"))]
                for j in range(4):
                    ft = r * 4 + j
                    p.op("tensor", lambda e, par=par, ft=ft, j=j: e.transpose(
                        pTf[:, j * 128:(j + 1) * 128], s_z2[par][:, ft * 128:(ft + 1) * 128], s_id[:, :]),
                        waits=tw if j == 0 else ())
                mmr = p.mark("tensor", "mm")
                for j in range(4):
                    ft = r * 4 + j
                    p.op("scalar", lambda e, par=par, ft=ft, j=j: e.activation(
                        s_aotmp[par][:, ft * 128:(ft + 1) * 128], pTf[:, j * 128:(j + 1) * 128], AF.Identity,
                        bias=s_nb[:, ft:ft + 1], scale=s_nw[:, ft:ft + 1]),
                        waits=[("mm", mmr)] if j == 0 else ())
                ac_blocks.append(p.mark("scalar", "ac"))
            mm_z2tr[t] = m("mm")
            ao_ac = ac_blocks[1]
            ev_aoTc[t] = p.op("vector", lambda e, par=par, t=t: e.tensor_copy(
                s_aoT.ap().rearrange("pp (f tok) -> pp f tok", f=NF)[:, :, t * 128:(t + 1) * 128],
                s_aotmp[par].ap().rearrange("pp (f c) -> pp f c", f=NF)),
                waits=[("ac", ao_ac)], inc="ev")
            xtw = [("ac", acxr)]
            if t >= 1:
                xtw.append(("ev", ev_xr1c_t[t - 1]))
            if t >= 2:
                xtw.append((f"q{par}", ld_x1sp[t - 2]))
            for ft in range(NF):
                p.op("tensor", lambda e, par=par, ft=ft: e.transpose(
                    pTb[:, ft * 128:(ft + 1) * 128], s_xr1tok[par][:, ft * 128:(ft + 1) * 128], s_idb[:, :]),
                    waits=xtw if ft == 0 else ())
            mmx = p.mark("tensor", "mm")
            mm_xrtr[t] = mmx
            ev_xr1c_t[t] = p.op("vector", lambda e, par=par: e.tensor_copy(
                s_x1sp[par][:, :], pTb[:, :]), waits=[("mm", mmx)], inc="ev")
            ld_x1sp[t] = p.op("sync", lambda e, t=t, par=par: e.dma_start(
                out=d_x1T.rearrange("(f pp) tok -> pp f tok", pp=128)[:, :, t * 128:(t + 1) * 128],
                in_=s_x1sp[par].ap().rearrange("pp (f c) -> pp f c", f=NF)),
                waits=[("ev", ev_xr1c_t[t])], inc=f"q{par}", inck=16)
            lgw = [("ac", ao_ac)]
            if t >= 1:
                lgw.append(("ev", ev_lgt_t[t - 1]))
            for kt in range(NF):
                p.op("tensor", lambda e, par=par, kt=kt: e.matmul(
                    pC[:, 0:10], s_aotmp[par][:, kt * 128:(kt + 1) * 128],
                    s_wgk[:, kt * 10:(kt + 1) * 10], start=(kt == 0), stop=(kt == NF - 1)),
                    waits=lgw if kt == 0 else ())
            mm_lg[t] = p.mark("tensor", "mm")
            ev10 = p.op("vector", lambda e: e.tensor_copy(s_lgt[:, :], pC[:, 0:10]),
                        waits=[("mm", mm_lg[t])], inc="ev")
            ev_lgt_t[t] = ev10
            if debug:
                p.op("vector", lambda e, t=t: e.tensor_copy(
                    s_dbglgt[:, 10 * t:10 * (t + 1)], s_lgt[:, :]))
            vgap()
            p.op("vector", lambda e: e.reduce_max(s_cA[:, :], s_lgt[:, 0:8], axis=AX.X))
            vgap()
            ev11 = p.op("vector", lambda e: e.tensor_scalar_mul(s_cB[:, :], s_cA[:, :], -1.0), inc="ev")
            ac7 = p.op("scalar", lambda e: e.activation(s_scr[:, 0:8], s_lgt[:, 0:8], AF.Exp,
                       bias=s_cB[:, 0:1], scale=1.0, accum_out=s_cC[:, 0:1]), waits=[("ev", ev11)], inc="ac")
            p.op("vector", lambda e, t=t: e.reciprocal(s_gcol[:, t:t + 1], s_cC[:, :]), waits=[("ac", ac7)])
            p.op("vector", lambda e: e.tensor_scalar(s_m1t[:, :], s_lgt[:, 0:8], s_cA[:, 0:1], None,
                 ALU.is_equal))
            ev12 = p.op("vector", lambda e: e.tensor_sub(s_cD[:, :], s_lgt[:, 8:9], s_lgt[:, 9:10]), inc="ev")
            ac8 = p.op("scalar", lambda e, t=t: e.activation(s_c0[:, t:t + 1], s_cD[:, 0:1], AF.Sigmoid),
                       waits=[("ev", ev12)], inc="ac")
            p.op("vector", lambda e, t=t: e.tensor_scalar(s_cE[:, :], s_c0[:, t:t + 1], -1.0, 1.0,
                 ALU.mult, ALU.add), waits=[("ac", ac8)])
            vgap()
            ev13 = p.op("vector", lambda e, t=t: e.tensor_mul(s_gc1[:, t:t + 1], s_gcol[:, t:t + 1],
                        s_cE[:, 0:1]), inc="ev")
            mmt = p.op("tensor", lambda e: e.transpose(pTf[0:8, 0:128], s_m1t[:, :], s_id[:, :]),
                       waits=[("ev", ev13), ("ac", m("ac"))], inc="mm")
            ev_ptf_last[t] = p.op("vector", lambda e, t=t: e.tensor_copy(
                s_m1T[0:8, t * 128:(t + 1) * 128], pTf[0:8, 0:128]), waits=[("mm", mmt)], inc="ev")
        x1sp_tot = [("q0", m("q0")), ("q1", m("q1"))]
        xnsp_tot = [("sp0", m("sp0")), ("sp1", m("sp1"))]
        if debug and upto < 4:
            p.op("sync", lambda e: e.dma_start(out=dbg["dbg_xnT"][:, :], in_=d_xnT[:, :]),
                 waits=list(xnsp_tot), inc="dbg", inck=16)
            p.op("sync", lambda e: e.dma_start(out=dbg["dbg_ao"][:, :], in_=d_ao[:, :]),
                 waits=[("a0", m("a0")), ("a1", m("a1"))], inc="dbg", inck=16)
        if debug:
            p.op("sync", lambda e: e.dma_start(out=dbg["dbg_A"][:, :], in_=s_dbgA[:, :]),
                 waits=[("ac", m("ac"))], inc="dbg", inck=16)
            p.op("sync", lambda e: e.dma_start(out=dbg["dbg_V"][:, :], in_=s_dbgV[:, :]),
                 waits=[("ev", m("ev"))], inc="dbg", inck=16)
        ph0.close()
        p.barrier()

        if upto < 2:
            raise _StopBuild()
        p.op("vector", lambda e: e.reduce_sum(s_cnt[:, :], s_m1T[:, :], axis=AX.X))
        ev15 = p.op("vector", lambda e: e.tensor_copy(s_vgap[:, :], s_vgap[:, :]), inc="ev")
        g1 = p.op("sync", lambda e: e.dma_start(out=cc_cnt_i[:, :], in_=s_cnt[:, :]),
                  waits=[("ev", ev15)], inc="g0", inck=16)
        cc1 = p.op("gpsimd", lambda e: e.collective_compute(
            "AllGather", ALU.bypass, [list(range(NCORES))], [cc_cnt_i[:, :]], [cc_cnt_o[:, :]]),
            waits=[("g0", g1)], inc="cc")

        if upto < 3:
            raise _StopBuild()
        phkv = ES2()
        sbk = lambda n, s, d: phkv.enter_context(nc.sbuf_tensor(n, s, d))
        s_wkv = sbk("s_wkv", [128, NF * HID], BF16)
        s_x1t = [sbk(f"s_x1t{i}", [128, NF * 128], BF16) for i in range(2)]
        s_kc = [sbk(f"s_kc{i}", [128, F], BF16) for i in range(2)]
        s_vc = [sbk(f"s_vc{i}", [128, F], BF16) for i in range(2)]
        s_ksq = sbk("s_ksq", [128, F], F32)

        wkv_ld = []
        wkv_fold = []
        for kt in range(NF):
            ww = [("wc", wcast_done)] if kt == 0 else []
            if kt >= 2:
                ww.append(("ev", wkv_fold[kt - 2]))
            wkv_ld.append(p.op("sync", lambda e, kt=kt: e.dma_start(
                out=s_wkv[:, kt * HID:(kt + 1) * HID],
                in_=d_bq[F:3 * F, kt * 128:(kt + 1) * 128], transpose=True),
                waits=ww, inc=f"wk{kt % 2}", inck=16))
            wkv_fold.append(p.op("vector", lambda e, kt=kt: e.tensor_scalar_mul(
                s_wkv[:, kt * HID:(kt + 1) * HID], s_wkv[:, kt * HID:(kt + 1) * HID], s_s1[:, kt:kt + 1]),
                waits=[(f"wk{kt % 2}", wkv_ld[kt])], inc="ev"))

        ac_k = [0] * NT
        ev_v = [0] * NT
        mm_kvt = [0] * NT
        mm_kvacc = [0] * NT
        kvadd = [0] * NT
        ld_x1 = [0] * NT
        p.op("gpsimd", lambda e: e.memset(s_kv[:, :], 0.0))
        gp_kv0 = p.mark("gpsimd", "gp")
        for t in range(NT):
            par = t % 2
            xw = list(x1sp_tot) if t == 0 else []
            if t >= 2:
                xw.append(("mm", mm_kvt[t - 2]))
            ld_x1[t] = p.op("sync", lambda e, t=t, par=par: e.dma_start(
                out=s_x1t[par].ap().rearrange("pp (f c) -> pp f c", f=NF),
                in_=d_x1T.rearrange("(f pp) tok -> pp f tok", pp=128)[:, :, t * 128:(t + 1) * 128]),
                waits=xw, inc=f"k{par}", inck=16)
            mw = [("ev", wkv_fold[-1]), (f"k{par}", ld_x1[t])]
            if t >= 1:
                mw.append(("ac", ac_k[t - 1]))
                mw.append(("ev", ev_v[t - 1]))
            for kt in range(NF):
                for half in range(2):
                    p.op("tensor", lambda e, par=par, kt=kt, half=half: e.matmul(
                        pA[:, half * 512:(half + 1) * 512],
                        s_x1t[par][:, kt * 128:(kt + 1) * 128],
                        s_wkv[:, kt * HID + half * 512: kt * HID + (half + 1) * 512],
                        start=(kt == 0), stop=(kt == NF - 1)), waits=mw if (kt == 0 and half == 0) else ())
                    p.op("tensor", lambda e, par=par, kt=kt, half=half: e.matmul(
                        pB[:, half * 512:(half + 1) * 512],
                        s_x1t[par][:, kt * 128:(kt + 1) * 128],
                        s_wkv[:, kt * HID + 1024 + half * 512: kt * HID + 1024 + (half + 1) * 512],
                        start=(kt == 0), stop=(kt == NF - 1)))
            mm_kvt[t] = p.mark("tensor", "mm")
            kw = [("mm", mm_kvt[t])]
            if t >= 2:
                kw.append(("mm", mm_kvacc[t - 2]))
            ack1 = p.op("scalar", lambda e: e.activation(s_ksq[:, :], pA[:, :], AF.Square),
                        waits=kw, inc="ac")
            ack2 = p.op("scalar", lambda e, par=par: e.activation(s_kc[par][:, :], s_ksq[:, :], AF.Exp,
                        scale=-0.5), inc="ac")
            ac_k[t] = ack2
            vw = [("mm", mm_kvt[t])]
            if t >= 2:
                vw.append(("mm", mm_kvacc[t - 2]))
            ev_v[t] = p.op("vector", lambda e, par=par: e.tensor_copy(s_vc[par][:, :], pB[:, :]),
                           waits=vw, inc="ev")
            kvw = [("ac", ack2), ("ev", ev_v[t])]
            if t == 0:
                kvw.append(("gp", gp_kv0))
            else:
                kvw.append(("ev", kvadd[t - 1]))
            for h in range(H):
                if h < 7:
                    pdst, cA0 = pC, h * 65
                elif h < 14:
                    pdst, cA0 = pC, 512 + (h - 7) * 65
                else:
                    pdst, cA0 = pTf, (h - 14) * 65
                p.op("tensor", lambda e, par=par, h=h, cA0=cA0, pdst=pdst: e.matmul(
                    pdst[0:64, cA0:cA0 + 64], s_kc[par][:, h * 64:(h + 1) * 64],
                    s_vc[par][:, h * 64:(h + 1) * 64], start=True, stop=False),
                    waits=kvw if h == 0 else ())
                p.op("tensor", lambda e, par=par, h=h, cA0=cA0, pdst=pdst: e.matmul(
                    pdst[0:64, cA0 + 64:cA0 + 65], s_kc[par][:, h * 64:(h + 1) * 64],
                    s_ones128b[:, 0:1], start=False, stop=True))
            mm_kvacc[t] = p.mark("tensor", "mm")
            p.op("vector", lambda e: e.tensor_add(s_kv[:, 0:455], s_kv[:, 0:455], pC[0:64, 0:455]),
                 waits=[("mm", mm_kvacc[t])])
            p.op("vector", lambda e: e.tensor_add(s_kv[:, 455:910], s_kv[:, 455:910], pC[0:64, 512:967]))
            kvadd[t] = p.op("vector", lambda e: e.tensor_add(
                s_kv[:, 910:1040], s_kv[:, 910:1040], pTf[0:64, 0:130]), inc="ev")
        p.op("vector", lambda e: e.tensor_copy(s_kvst[0:64, :], s_kv[:, :]),
             waits=[("ev", kvadd[NT - 1])])
        p.op("vector", lambda e: e.tensor_copy(s_kvst[64:128, :], s_kv[:, :]))
        ev18 = p.op("vector", lambda e: e.tensor_scalar_mul(s_kvst[:, :], s_kvst[:, :], s_bm[:, 0:1]),
                    inc="ev")
        g2 = p.op("sync", lambda e: e.dma_start(out=cc_kv_i[:, :], in_=s_kvst[:, :]),
                  waits=[("ev", ev18)], inc="g0", inck=16)
        cc2 = p.op("gpsimd", lambda e: e.collective_compute(
            "AllReduce", ALU.add, [list(range(NCORES))], [cc_kv_i[:, :]], [cc_kv_o[:, :]]),
            waits=[("g0", g2)], inc="cc")
        phkv.close()

        if upto < 4:
            raise _StopBuild()
        phg = ES2()
        sbg = lambda n, s, d: phg.enter_context(nc.sbuf_tensor(n, s, d))
        s_Srow = sbg("s_Srow", [1, T], F32)
        s_tmpr = sbg("s_tmpr", [1, T], F32)

        g3 = p.op("sync", lambda e: e.dma_start(
            out=s_cnts[:, :], in_=cc_cnt_o.rearrange("(r ee) one -> ee (r one)", ee=8),
            allow_slow_non_contiguous=True), waits=[("cc", cc1)], inc="g1", inck=16)
        p.op("vector", lambda e: e.scalar_tensor_tensor(
            s_cnts[:, :], s_cnts[:, :], 1.0, s_pref[:, :], ALU.mult, ALU.mult,
            accum_out=s_offs[:, 0:1]), waits=[("g1", g3)])
        p.op("gpsimd", lambda e: e.memset(s_loc[:, 0:1], 0.0))
        gp0b = p.mark("gpsimd", "gp")
        p.op("vector", lambda e: e.tensor_tensor_scan(
            s_loc[:, 1:T], s_m1T[:, 0:T - 1], s_m1T[:, 0:T - 1], 0.0, ALU.add, ALU.bypass),
            waits=[("gp", gp0b)])
        p.op("vector", lambda e: e.tensor_scalar_add(s_loc[:, :], s_loc[:, :], s_offs[:, 0:1]))
        if debug:
            s_dbgl1 = phg.enter_context(nc.sbuf_tensor("s_dbgl1", [8, T], F32))
            s_dbgmk = phg.enter_context(nc.sbuf_tensor("s_dbgmk", [8, T], F32))
            p.op("vector", lambda e: e.tensor_copy(s_dbgl1[:, :], s_loc[:, :]))
        vgap()
        p.op("vector", lambda e: e.scalar_tensor_tensor(
            s_m1T[:, :], s_loc[:, :], CAP, s_m1T[:, :], ALU.is_lt, ALU.mult))
        if debug:
            p.op("vector", lambda e: e.tensor_copy(s_dbgmk[:, :], s_m1T[:, :]))
        evm1k = p.mark("vector", "ev")
        p.op("gpsimd", lambda e: e.memset(s_loc[:, 0:1], 0.0), waits=[("ev", evm1k)])
        gp1 = p.mark("gpsimd", "gp")
        p.op("vector", lambda e: e.tensor_tensor_scan(
            s_loc[:, 1:T], s_m1T[:, 0:T - 1], s_m1T[:, 0:T - 1], 0.0, ALU.add, ALU.bypass),
            waits=[("gp", gp1)])
        ev21 = p.op("vector", lambda e: e.scalar_tensor_tensor(
            s_loc[:, :], s_loc[:, :], s_e512[:, 0:1], s_m1T[:, :], ALU.add, ALU.mult), inc="ev")
        for c in range(4):
            off = (c % 2) * 512
            dst = pA if c < 2 else pB
            p.op("tensor", lambda e, c=c, off=off, dst=dst: e.matmul(
                dst[0:1, off:off + 512], s_ones8[:, :], s_loc[:, c * 512:(c + 1) * 512],
                start=True, stop=True),
                waits=[("ev", ev21), ("ev", m("ev")), ("ac", m("ac")), ("gp", gp_set)]
            if c == 0 else ())
        mmS = p.mark("tensor", "mm")
        for c in range(4):
            off = (c % 2) * 512
            dst = pA if c < 2 else pB
            p.op("vector", lambda e, c=c, off=off, dst=dst: e.tensor_copy(
                s_Srow[:, c * 512:(c + 1) * 512], dst[0:1, off:off + 512]), waits=[("mm", mmS)])
        evSr = p.mark("vector", "ev")
        for c in range(4):
            off = (c % 2) * 512
            dst = pA if c < 2 else pB
            p.op("tensor", lambda e, c=c, off=off, dst=dst: e.matmul(
                dst[0:1, off:off + 512], s_ones8[:, :], s_m1T[:, c * 512:(c + 1) * 512],
                start=True, stop=True), waits=[("ev", evSr)] if c == 0 else ())
        mmK = p.mark("tensor", "mm")
        for c in range(4):
            off = (c % 2) * 512
            dst = pA if c < 2 else pB
            p.op("vector", lambda e, c=c, off=off, dst=dst: e.tensor_copy(
                s_tmpr[:, c * 512:(c + 1) * 512], dst[0:1, off:off + 512]), waits=[("mm", mmK)])
        vgap()
        p.op("vector", lambda e: e.scalar_tensor_tensor(
            s_Srow[:, :], s_tmpr[:, :], -SBIG, s_Srow[:, :], ALU.mult, ALU.add))
        vgap()
        ev22 = p.op("vector", lambda e: e.tensor_scalar_add(s_Srow[:, :], s_Srow[:, :], SBIG), inc="ev")
        evprev = ev22
        for t in range(NT):
            mmt = p.op("tensor", lambda e, t=t: e.transpose(
                pTf[:, 0:1], s_Srow[:, t * 128:(t + 1) * 128], s_id[0:1, 0:1]),
                waits=[("ev", evprev)], inc="mm")
            p.op("vector", lambda e, t=t: e.tensor_copy(s_Si[:, t:t + 1], pTf[:, 0:1]),
                 waits=[("mm", mmt)])
            vgap()
            p.op("vector", lambda e, t=t: e.tensor_single_scalar(
                s_Fi[:, t:t + 1], s_Si[:, t:t + 1], 127, ALU.bitwise_and))
            p.op("vector", lambda e, t=t: e.tensor_single_scalar(
                s_Si2[:, t:t + 1], s_Si[:, t:t + 1], 7, ALU.logical_shift_right))
            p.op("vector", lambda e, t=t: e.tensor_single_scalar(
                s_Fi[:, t:t + 1], s_Fi[:, t:t + 1], 5, ALU.logical_shift_left))
            vgap()
            evprev = p.op("vector", lambda e, t=t: e.tensor_add(
                s_Fi[:, t:t + 1], s_Fi[:, t:t + 1], s_Si2[:, t:t + 1]), inc="ev")
        sf_ev = evprev
        p.op("gpsimd", lambda e: e.memset(s_tos[:, :], 1 << 28))
        gp2 = p.mark("gpsimd", "gp")
        g4 = p.op("sync", lambda e: e.dma_start(out=d_tos[:, :], in_=s_tos[:, :]), waits=[("gp", gp2)],
                  inc="g2", inck=16)
        g4 = p.op("sync", lambda e: e.dma_start(out=s_fi32[:, :], in_=d_tos[0:1, 0:1]),
                  waits=[("g2", m("g2"))], inc="g2", inck=16)
        last_sc = 0
        for t in range(NT):
            w = [("g2", g4), ("ev", sf_ev)] if t == 0 else []
            last_sc = p.op("gpsimd", lambda e, t=t: e.indirect_dma_start(
                d_tos.rearrange("pp (c one) -> (pp c) one", one=1),
                bass.IndirectOffsetOnAxis(s_Fi[:, t:t + 1], 0),
                s_iotok[:, t:t + 1], None, bounds_check=bc_reg(e, NSLOT - 1), oob_is_err=False),
                waits=w, inc="ts", inck=16)
        last_sc = p.op("gpsimd", lambda e: e.dma_start(out=s_fi32[:, :], in_=d_tos[0:1, 0:1]),
                       waits=[("ts", m("ts"))], inc="ts", inck=16)
        g5 = p.op("sync", lambda e: e.dma_start(out=s_tos[:, :], in_=d_tos[:, :]),
                  waits=[("ts", last_sc)], inc="g3", inck=16)
        if debug:
            p.op("sync", lambda e: e.dma_start(out=dbg["dbg_Srow"][:, :], in_=s_Srow[:, :]),
                 inc="dbg", inck=16)
            p.op("sync", lambda e: e.dma_start(out=dbg["dbg_cnts"][:, :], in_=s_cnts[:, :]),
                 inc="dbg", inck=16)
            p.op("sync", lambda e: e.dma_start(out=dbg["dbg_loc"][:, :], in_=s_loc[:, :]),
                 inc="dbg", inck=16)
            p.op("sync", lambda e: e.dma_start(out=dbg["dbg_g"][:, :], in_=s_gcol[:, :]),
                 inc="dbg", inck=16)
            p.op("sync", lambda e: e.dma_start(out=dbg["dbg_tos"][:, :], in_=s_tos[:, :]),
                 waits=[("g3", g5)], inc="dbg", inck=16)
            p.op("sync", lambda e: e.dma_start(out=dbg["dbg_xnT"][:, :], in_=d_xnT[:, :]),
                 waits=list(xnsp_tot), inc="dbg", inck=16)
            p.op("sync", lambda e: e.dma_start(out=dbg["dbg_offs"][:, :], in_=s_offs[:, :]),
                 inc="dbg", inck=16)
            p.op("sync", lambda e: e.dma_start(out=dbg["dbg_loc1"][:, :], in_=s_dbgl1[:, :]),
                 inc="dbg", inck=16)
            p.op("sync", lambda e: e.dma_start(out=dbg["dbg_m1k"][:, :], in_=s_dbgmk[:, :]),
                 inc="dbg", inck=16)
            p.op("sync", lambda e: e.dma_start(out=dbg["dbg_lgt"][:, :], in_=s_dbglgt[:, :]),
                 inc="dbg", inck=16)
        phg.close()
        mscope.close()
        p.barrier()

        if upto < 5:
            raise _StopBuild()
        phq = ES2()
        sbq = lambda n, s, d: phq.enter_context(nc.sbuf_tensor(n, s, d))
        s_poT = sbq("s_poT", [128, NF * T], BF16)
        s_zr = sbq("s_zr", [H, T], F32)
        s_bo = sbq("s_bo", [H, F], F32)
        s_kvb = sbq("s_kvb", [128, 8 * 65], BF16)
        s_dent = [sbq(f"s_dent{i}", [1, 512], F32) for i in range(2)]

        g6b = p.op("sync", lambda e: e.dma_start(out=s_bo[:, :], in_=bones[:, :]), inc="g4", inck=16)
        g6 = p.op("sync", lambda e: e.dma_start(out=s_kvst[:, :], in_=cc_kv_o[:, :]),
                  waits=[("cc", cc2)], inc="g5", inck=16)
        p.op("vector", lambda e: e.tensor_scalar_mul(s_kv[:, :], s_kvst[0:64, :], s_b0[0:64, 0:1]),
             waits=[("g5", g6)])
        p.op("vector", lambda e: e.tensor_scalar_mul(s_kvtmp[:, :], s_kvst[64:128, :],
             s_b1c[64:128, 0:1]))
        ev23 = p.op("vector", lambda e: e.tensor_add(s_kv[:, :], s_kv[:, :], s_kvtmp[:, :]), inc="ev")
        p.op("vector", lambda e: e.tensor_copy(
            s_kvb[0:64, :].rearrange("pp (pr c) -> pp pr c", c=65),
            s_kv.ap().rearrange("pp (pr two c) -> pp pr two c", two=2, c=65)[:, :, 0, :]))
        ev25 = p.op("vector", lambda e: e.tensor_copy(
            s_kvb[64:128, :].rearrange("pp (pr c) -> pp pr c", c=65),
            s_kv.ap().rearrange("pp (pr two c) -> pp pr two c", two=2, c=65)[:, :, 1, :]), inc="ev")
        if debug:
            p.op("sync", lambda e: e.dma_start(out=dbg["dbg_kv"][:, :], in_=s_kv[:, :]),
                 waits=[("ev", ev23)], inc="dbg", inck=16)

        qsub = ES2()
        s_wq = qsub.enter_context(nc.sbuf_tensor("s_wq", [128, NF * F], BF16))
        s_x1c = qsub.enter_context(nc.sbuf_tensor("s_x1c", [128, NF * 512], BF16))
        s_qsq = qsub.enter_context(nc.sbuf_tensor("s_qsq", [128, 512], F32))
        s_qph = qsub.enter_context(nc.sbuf_tensor("s_qph", [128, 512], BF16))

        wq_ld = []
        wq_fold = []
        for kt in range(NF):
            ww = []
            if kt >= 2:
                ww.append(("ev", wq_fold[kt - 2]))
            wq_ld.append(p.op("sync", lambda e, kt=kt: e.dma_start(
                out=s_wq[:, kt * F:(kt + 1) * F], in_=d_bq[0:F, kt * 128:(kt + 1) * 128], transpose=True),
                waits=ww, inc=f"qw{kt % 2}", inck=16))
            wq_fold.append(p.op("vector", lambda e, kt=kt: e.tensor_scalar_mul(
                s_wq[:, kt * F:(kt + 1) * F], s_wq[:, kt * F:(kt + 1) * F], s_s1[:, kt:kt + 1]),
                waits=[(f"qw{kt % 2}", wq_ld[kt])], inc="ev"))

        dent_dma = [0, 0]
        mm_q_prev = 0
        ld_x1c = [0] * 4
        for c in range(4):
            cwx = list(x1sp_tot) if c == 0 else [("mm", m("mm"))]
            ld_x1c[c] = p.op("sync", lambda e, c=c: e.dma_start(
                out=s_x1c.ap().rearrange("pp (f cc) -> pp f cc", f=NF),
                in_=d_x1T.rearrange("(f pp) tok -> pp f tok", pp=128)[:, :, c * 512:(c + 1) * 512]),
                waits=cwx, inc="xc", inck=16)
            for pr in range(NF):
                qw = [("ev", wq_fold[-1]), ("xc", ld_x1c[c]), ("ac", m("ac"))]
                for kt in range(NF):
                    p.op("tensor", lambda e, pr=pr, kt=kt, c=c: e.matmul(
                        pA[:, 0:512], s_wq[:, kt * F + pr * 128: kt * F + pr * 128 + 128],
                        s_x1c[:, kt * 512:(kt + 1) * 512],
                        start=(kt == 0), stop=(kt == NF - 1)), waits=qw if kt == 0 else ())
                mmq = p.mark("tensor", "mm")
                acq1 = p.op("scalar", lambda e: e.activation(s_qsq[:, :], pA[:, 0:512], AF.Square),
                            waits=[("mm", mmq)], inc="ac")
                acq2 = p.op("scalar", lambda e: e.activation(s_qph[:, :], s_qsq[:, :], AF.Exp,
                            scale=-0.5), inc="ac")
                nw_ = [("ac", acq2), ("ev", ev25), ("ev", m("ev"))]
                p.op("tensor", lambda e, pr=pr: e.matmul(
                    pC[0:65, 0:512], s_kvb[0:64, pr * 65:(pr + 1) * 65],
                    s_qph[0:64, :], start=True, stop=True), waits=nw_)
                p.op("tensor", lambda e, pr=pr: e.matmul(
                    pC[0:65, 512:1024], s_kvb[64:128, pr * 65:(pr + 1) * 65],
                    s_qph[64:128, :], start=True, stop=True))
                mmn = p.mark("tensor", "mm")
                p.op("vector", lambda e, pr=pr, c=c: e.tensor_copy(
                    fm(s_poT, pr, c * 512, (c + 1) * 512)[0:64, :], pC[0:64, 0:512]),
                    waits=[("mm", mmn)])
                p.op("vector", lambda e, pr=pr, c=c: e.tensor_copy(
                    fm(s_poT, pr, c * 512, (c + 1) * 512)[64:128, :], pC[0:64, 512:1024]))
                dw0 = [("dn0", dent_dma[0])] if dent_dma[0] else []
                p.op("vector", lambda e: e.tensor_copy(s_dent[0][:, :], pC[64:65, 0:512]), waits=dw0)
                dw1 = [("dn1", dent_dma[1])] if dent_dma[1] else []
                p.op("vector", lambda e: e.tensor_copy(s_dent[1][:, :], pC[64:65, 512:1024]), waits=dw1)
                for hh in range(2):
                    p.op("vector", lambda e, hh=hh: e.tensor_scalar_add(
                        s_dent[hh][:, :], s_dent[hh][:, :], 1e-6))
                    p.op("vector", lambda e, hh=hh: e.reciprocal(s_dent[hh][:, :], s_dent[hh][:, :]))
                    evz = p.op("vector", lambda e: e.tensor_copy(s_vgap[:, :], s_vgap[:, :]),
                               inc="ev")
                    dent_dma[hh] = p.op("scalar", lambda e, pr=pr, hh=hh, c=c: e.dma_start(
                        out=s_zr[2 * pr + hh:2 * pr + hh + 1, c * 512:(c + 1) * 512],
                        in_=s_dent[hh][:, :]), waits=[("ev", evz)], inc=f"dn{hh}", inck=16)
            mm_q_prev = m("mm")
        qsub.close()
        zw = [("dn0", m("dn0")), ("dn1", m("dn1")), ("ac", m("ac")), ("ev", m("ev"))]
        first = True
        for ft in range(NF):
            for c in range(4):
                dst = pA if (c % 2 == 0) else pB
                sl = (ft % 2) * 512
                mmz = p.op("tensor", lambda e, ft=ft, c=c, dst=dst, sl=sl: e.matmul(
                    dst[:, sl:sl + 512], s_bo[:, ft * 128:(ft + 1) * 128],
                    s_zr[:, c * 512:(c + 1) * 512], start=True, stop=True),
                    waits=zw if first else [("ev", m("ev"))], inc="mm")
                first = False
                p.op("vector", lambda e, ft=ft, c=c, dst=dst, sl=sl: e.tensor_mul(
                    fm(s_poT, ft, c * 512, (c + 1) * 512), fm(s_poT, ft, c * 512, (c + 1) * 512),
                    dst[:, sl:sl + 512]), waits=[("mm", mmz)])
                p.mark("vector", "ev")
        poz_ev = m("ev")
        if debug:
            p.op("sync", lambda e: e.dma_start(out=dbg["dbg_poT"].rearrange("(f pp) tok -> pp f tok", pp=128),
                 in_=s_poT.ap().rearrange("pp (f tok) -> pp f tok", f=NF)),
                 waits=[("ev", poz_ev)], inc="dbg", inck=16)

        if upto < 6:
            raise _StopBuild()
        pjsub = ES2()
        s_wp = pjsub.enter_context(nc.sbuf_tensor("s_wp", [128, NF * F], BF16))
        s_xnt = [pjsub.enter_context(nc.sbuf_tensor(f"s_xnt{i}", [128, 512], F32)) for i in range(2)]
        wp_ld = []
        wp_fold = []
        for kt in range(NF):
            ww = [("wc", wcast_done), ("mm", m("mm")), ("ac", m("ac")),
                  ("ev", m("ev"))] if kt == 0 else []
            if kt >= 2:
                ww.append(("ev", wp_fold[kt - 2]))
            wp_ld.append(p.op("sync", lambda e, kt=kt: e.dma_start(
                out=s_wp[:, kt * F:(kt + 1) * F], in_=d_bp[0:F, kt * 128:(kt + 1) * 128], transpose=True),
                waits=ww, inc=f"pw{kt % 2}", inck=16))
            wp_fold.append(p.op("vector", lambda e, kt=kt: e.tensor_mul(
                s_wp[:, kt * F:(kt + 1) * F], s_wp[:, kt * F:(kt + 1) * F], s_gmb[:, :]),
                waits=[(f"pw{kt % 2}", wp_ld[kt])], inc="ev"))
        ev_pj = {}
        for ot in range(NF):
            for c in range(4):
                idx = ot * 4 + c
                lw = ([("ev", wp_fold[-1])] + list(xnsp_tot)) if idx == 0 else []
                if idx >= 2:
                    lw.append(("ev", ev_pj[idx - 2]))
                xl = p.op("sync", lambda e, ot=ot, c=c: e.dma_start(
                    out=s_xnt[c % 2][:, :], in_=d_xnT[ot * 128:(ot + 1) * 128, c * 512:(c + 1) * 512]),
                    waits=lw, inc=f"pj{c % 2}", inck=16)
                dst = pA if c % 2 == 0 else pB
                sl = (ot % 2) * 512
                for kt in range(NF):
                    p.op("tensor", lambda e, ot=ot, c=c, kt=kt, dst=dst, sl=sl: e.matmul(
                        dst[:, sl:sl + 512], s_wp[:, kt * F + ot * 128: kt * F + ot * 128 + 128],
                        fm(s_poT, kt, c * 512, (c + 1) * 512),
                        start=(kt == 0), stop=(kt == NF - 1)),
                        waits=([("ev", wp_fold[-1])] if idx == 0 else [("ev", m("ev"))]) if kt == 0 else ())
                mmp = p.mark("tensor", "mm")
                ev_pj[idx] = p.op("vector", lambda e, ot=ot, c=c, dst=dst, sl=sl: e.scalar_tensor_tensor(
                    fm(s_bigB, ot, c * 512, (c + 1) * 512), dst[:, sl:sl + 512],
                    s_pgb[:, ot:ot + 1], s_xnt[c % 2][:, :], ALU.add, ALU.add),
                    waits=[("mm", mmp), (f"pj{c % 2}", xl)], inc="ev")
        if debug:
            p.op("sync", lambda e: e.dma_start(out=dbg["dbg_x2T"].rearrange("(f pp) tok -> pp f tok", pp=128),
                 in_=s_bigB.ap().rearrange("pp (f tok) -> pp f tok", f=NF)),
                 waits=[("ev", m("ev"))], inc="dbg", inck=16)
        pjsub.close()
        phq.close()
        kvscope.close()
        p.barrier()

        if upto < 7:
            raise _StopBuild()
        phc = ES2()
        sbc = lambda n, s, d: phc.enter_context(nc.sbuf_tensor(n, s, d))
        s_wcv = sbc("s_wcv", [128, 16 * F], BF16)
        s_xc1 = sbc("s_xc1", [128, NF * 512], BF16)
        s_hdd = sbc("s_hdd", [128, 16 * 512], BF16)
        s_xsq = sbc("s_xsq", [128, 512], F32)
        s_r2 = sbc("s_r2", [1, 512], F32)
        s_xnt2 = [sbc(f"s_xnt2{i}", [128, 512], F32) for i in range(2)]

        for c in range(4):
            prev_mm = m("mm")
            for ft in range(NF):
                acs = p.op("scalar", lambda e, ft=ft, c=c: e.activation(
                    s_xsq[:, :], fm(s_bigB, ft, c * 512, (c + 1) * 512), AF.Square),
                    waits=[("mm", prev_mm)], inc="ac")
                prev_mm = p.op("tensor", lambda e, ft=ft: e.matmul(
                    pC[0:1, 0:512], s_ones128[:, 0:1], s_xsq[:, :],
                    start=(ft == 0), stop=(ft == NF - 1)), waits=[("ac", acs)], inc="mm")
            acr = p.op("scalar", lambda e: e.activation(s_r2[:, :], pC[0:1, 0:512], AF.Sqrt),
                       waits=[("mm", prev_mm)], inc="ac")
            p.op("vector", lambda e: e.reciprocal(s_r2[:, :], s_r2[:, :]), waits=[("ac", acr)])
            vgap()
            evr = p.op("vector", lambda e: e.tensor_scalar_mul(s_r2[:, :], s_r2[:, :], RMS_C), inc="ev")
            mmb = p.op("tensor", lambda e: e.matmul(pC[:, 512:1024], s_onesr[:, :], s_r2[:, :],
                       start=True, stop=True), waits=[("ev", evr)], inc="mm")
            for ft in range(NF):
                p.op("vector", lambda e, ft=ft, c=c: e.tensor_mul(
                    s_xc1[:, ft * 512:(ft + 1) * 512], fm(s_bigB, ft, c * 512, (c + 1) * 512),
                    pC[:, 512:1024]), waits=[("mm", mmb)] if ft == 0 else ())
            evx = p.mark("vector", "ev")
            w1_ld = []
            w1f = []
            for kt in range(NF):
                w1w = []
                if kt == 0:
                    w1w = [("mm", m("mm"))] if c > 0 else [("wc", wcast_done)]
                if kt >= 2:
                    w1w.append(("ev", w1f[kt - 2]))
                w1_ld.append(p.op("sync", lambda e, kt=kt: e.dma_start(
                    out=s_wcv[:, kt * HID:(kt + 1) * HID], in_=d_b1[0:HID, kt * 128:(kt + 1) * 128],
                    transpose=True), waits=w1w, inc=f"w1{kt % 2}", inck=16))
                w1f.append(p.op("vector", lambda e, kt=kt: e.tensor_scalar_mul(
                    s_wcv[:, kt * HID:(kt + 1) * HID], s_wcv[:, kt * HID:(kt + 1) * HID],
                    s_s2[:, kt:kt + 1]), waits=[(f"w1{kt % 2}", w1_ld[kt])], inc="ev"))
            ac_prev = [0, 0]
            for ot in range(16):
                dst = pA if ot % 2 == 0 else pB
                hw = [("ev", w1f[-1]), ("ev", evx)] if ot == 0 else []
                if ot >= 2:
                    hw.append(("ac", ac_prev[ot % 2]))
                for kt in range(NF):
                    p.op("tensor", lambda e, ot=ot, kt=kt, dst=dst: e.matmul(
                        dst[:, 0:512], s_wcv[:, kt * HID + ot * 128: kt * HID + ot * 128 + 128],
                        s_xc1[:, kt * 512:(kt + 1) * 512], start=(kt == 0), stop=(kt == NF - 1)),
                        waits=hw if kt == 0 else ())
                mmc = p.mark("tensor", "mm")
                ac_prev[ot % 2] = p.op("scalar", lambda e, ot=ot, dst=dst: e.activation(
                    s_hdd[:, ot * 512:(ot + 1) * 512], dst[:, 0:512], AF.Gelu,
                    bias=s_c1b[:, ot:ot + 1]), waits=[("mm", mmc)], inc="ac")
            hdd_ac = m("ac")
            w2_ld = []
            w2f = []
            for kt in range(16):
                w2w = [("ac", hdd_ac), ("wc", wcast_done)] if kt == 0 else []
                if kt >= 2:
                    w2w.append(("ev", w2f[kt - 2]))
                w2_ld.append(p.op("sync", lambda e, kt=kt: e.dma_start(
                    out=s_wcv[:, kt * F:(kt + 1) * F], in_=d_b2[0:F, kt * 128:(kt + 1) * 128],
                    transpose=True), waits=w2w, inc=f"w2{kt % 2}", inck=16))
                w2f.append(p.op("vector", lambda e, kt=kt: e.tensor_mul(
                    s_wcv[:, kt * F:(kt + 1) * F], s_wcv[:, kt * F:(kt + 1) * F], s_gmb[:, :]),
                    waits=[(f"w2{kt % 2}", w2_ld[kt])], inc="ev"))
            ev_c2 = {}
            for ot in range(NF):
                xw = [("ev", w2f[-1])] if ot == 0 else []
                if ot >= 2:
                    xw.append(("ev", ev_c2[ot - 2]))
                xl = p.op("sync", lambda e, ot=ot, c=c: e.dma_start(
                    out=s_xnt2[ot % 2][:, :], in_=d_xnT[ot * 128:(ot + 1) * 128, c * 512:(c + 1) * 512]),
                    waits=xw + (xnsp_tot if (c == 0 and ot == 0) else []), inc=f"cv{ot % 2}", inck=16)
                dst = pA if ot % 2 == 0 else pB
                for kt in range(16):
                    p.op("tensor", lambda e, ot=ot, kt=kt, dst=dst: e.matmul(
                        dst[:, 512:1024], s_wcv[:, kt * F + ot * 128: kt * F + ot * 128 + 128],
                        s_hdd[:, kt * 512:(kt + 1) * 512], start=(kt == 0), stop=(kt == 15)),
                        waits=([("ev", w2f[-1])] if ot == 0 else [("ev", ev_c2.get(ot - 2, 0))])
                        if kt == 0 else ())
                mm2 = p.mark("tensor", "mm")
                p.op("vector", lambda e, ot=ot, c=c, dst=dst: e.scalar_tensor_tensor(
                    fm(s_bigB, ot, c * 512, (c + 1) * 512), dst[:, 512:1024], s_c2gb[:, ot:ot + 1],
                    fm(s_bigB, ot, c * 512, (c + 1) * 512), ALU.add, ALU.add),
                    waits=[("mm", mm2), (f"cv{ot % 2}", xl)])
                ev_c2[ot] = p.op("vector", lambda e, ot=ot, c=c: e.tensor_add(
                    fm(s_bigB, ot, c * 512, (c + 1) * 512), fm(s_bigB, ot, c * 512, (c + 1) * 512),
                    s_xnt2[ot % 2][:, :]), inc="ev")
        phc.close()
        if debug:
            p.op("sync", lambda e: e.dma_start(out=dbg["dbg_acc"].rearrange("(f pp) tok -> pp f tok", pp=128),
                 in_=s_bigB.ap().rearrange("pp (f tok) -> pp f tok", f=NF)),
                 waits=[("ev", m("ev"))], inc="dbg", inck=16)
        p.barrier()

        if upto < 8:
            raise _StopBuild()
        phr = ES2()
        sbr = lambda n, s, d: phr.enter_context(nc.sbuf_tensor(n, s, d))
        s_wri = [sbr(f"s_wri{i}", [128, 8 * 1024], BF16) for i in range(2)]
        s_rh = sbr("s_rh", [128, 32 * 512], BF16)
        s_wro = [sbr(f"s_wro{i}", [128, 32 * 128], BF16) for i in range(2)]
        s_sA = sbr("s_sA", [128, 512], F32)
        s_sB = sbr("s_sB", [128, 512], F32)
        s_rme = sbr("s_rme", [128, 512], BF16)

        def gelu_chain(psum_fn, bias_col, biass_col, out_fn, scrA, scrB, pre_waits):
            a1 = p.op("scalar", lambda e: e.activation(scrA[:, :], psum_fn(), AF.Square,
                      bias=biass_col, scale=GELU_S), waits=pre_waits, inc="ac")
            p.op("vector", lambda e: e.tensor_scalar_add(scrB[:, :], scrA[:, :], 1.0),
                 waits=[("ac", a1)])
            e2 = p.op("vector", lambda e: e.scalar_tensor_tensor(
                scrA[:, :], psum_fn(), bias_col, scrB[:, :], ALU.add, ALU.mult), inc="ev")
            a2 = p.op("scalar", lambda e: e.activation(scrB[:, :], scrA[:, :], AF.Sigmoid,
                      scale=GELU_A), waits=[("ev", e2)], inc="ac")
            e3 = p.op("vector", lambda e: e.scalar_tensor_tensor(
                out_fn(), psum_fn(), bias_col, scrB[:, :], ALU.add, ALU.mult),
                waits=[("ac", a2)], inc="ev")
            return e3

        wmm_prev = {}
        rm_prev = [None]
        rch = []
        ro_marks = []
        ro_acos = []
        for c in range(4):
            for g in range(4):
                glw = []
                if not (c == 0 and g < 2):
                    glw.append(("mm", wmm_prev.get((c, g), m("mm"))))
                gl = p.op("gpsimd", lambda e, g=g: e.dma_start(
                    out=s_wri[g % 2][:, :],
                    in_=ri_w.rearrange("(kt pp) i -> pp kt i", pp=128)[:, :, g * 1024:(g + 1) * 1024]),
                    waits=glw, inc=f"ri{g % 2}", inck=16)
                for it in range(8):
                    ig = g * 8 + it
                    dst = pA if it % 2 == 0 else pB
                    mw = [(f"ri{g % 2}", gl)] if it == 0 else []
                    if len(rch) >= 2:
                        mw.append(("ev", rch[-2]))
                    for kt in range(NF):
                        p.op("tensor", lambda e, g=g, it=it, kt=kt, dst=dst, c=c: e.matmul(
                            dst[:, 0:512], s_wri[g % 2][:, kt * 1024 + it * 128: kt * 1024 + it * 128 + 128],
                            fm(s_aoT, kt, c * 512, (c + 1) * 512), start=(kt == 0), stop=(kt == NF - 1)),
                            waits=mw if kt == 0 else ())
                    mmi = p.mark("tensor", "mm")
                    psf = (lambda dst: (lambda: dst[:, 0:512]))(dst)
                    outf = (lambda ig: (lambda: s_rh[:, ig * 512:(ig + 1) * 512]))(ig)
                    rch.append(gelu_chain(psf, s_rib[:, ig:ig + 1], s_ribs[:, ig:ig + 1],
                                          outf, s_sA, s_sB, [("mm", mmi)]))
                wmm_prev[(c, g + 2)] = m("mm")
                wmm_prev[(c + 1, g - 2)] = m("mm")
            rh_ev = m("ev")
            for ot in range(NF):
                fl = len(ro_marks)
                olw = [("mm", ro_marks[fl - 2])] if fl >= 2 else []
                ol = p.op("gpsimd", lambda e, ot=ot, fl=fl: e.dma_start(
                    out=s_wro[fl % 2][:, :],
                    in_=ro_w.rearrange("(kt pp) o -> pp kt o", pp=128)[:, :, ot * 128:(ot + 1) * 128]),
                    waits=olw, inc=f"ro{fl % 2}", inck=16)
                dst = pA if ot % 2 == 0 else pB
                ow = [(f"ro{fl % 2}", ol), ("ev", rh_ev)]
                if fl >= 2:
                    ow.append(("ac", ro_acos[fl - 2]))
                for kt in range(32):
                    p.op("tensor", lambda e, ot=ot, kt=kt, dst=dst, fl=fl: e.matmul(
                        dst[:, 512:1024], s_wro[fl % 2][:, kt * 128:(kt + 1) * 128],
                        s_rh[:, kt * 512:(kt + 1) * 512], start=(kt == 0), stop=(kt == 31)),
                        waits=ow if kt == 0 else ())
                mmo = p.mark("tensor", "mm")
                ro_marks.append(mmo)
                acw2 = [("mm", mmo)]
                if rm_prev[0] is not None:
                    acw2.append(rm_prev[0])
                aco = p.op("scalar", lambda e, ot=ot, dst=dst: e.activation(
                    s_rme[:, :], dst[:, 512:1024], AF.Identity, bias=s_rob[:, ot:ot + 1]),
                    waits=acw2, inc="ac")
                ro_acos.append(aco)
                rmk = p.op("scalar", lambda e, ot=ot, c=c: e.dma_start(
                    out=d_rm[ot * 128:(ot + 1) * 128, c * 512:(c + 1) * 512], in_=s_rme[:, :]),
                    waits=[("ac", aco)], inc=f"rm{ot % 2}", inck=16)
                rm_prev[0] = (f"rm{ot % 2}", rmk)
        p.op("scalar", lambda e: e.dma_start(out=s_fbf[:, :], in_=d_rm[0:1, 0:1]),
             waits=[("rm0", m("rm0")), ("rm1", m("rm1"))], inc="rm0", inck=16)
        phr.close()
        p.barrier()

        if upto < 9:
            raise _StopBuild()
        phe = ES2()
        sbe = lambda n, s, d: phe.enter_context(nc.sbuf_tensor(n, s, d))
        s_dt = [sbe(f"s_dt{i}", [128, F], BF16) for i in range(2)]
        s_dispT = sbe("s_dispT", [128, NF * 512], BF16)
        s_h = sbe("s_h", [128, 32 * 512], BF16)
        s_wie = [sbe(f"s_wie{i}", [128, 8 * 1024], BF16) for i in range(2)]
        s_wo2 = [sbe(f"s_wo2{i}", [128, 8 * 512], BF16) for i in range(2)]
        s_eA = sbe("s_eA", [128, 512], F32)
        s_eB = sbe("s_eB", [128, 512], F32)
        s_eo = sbe("s_eo", [128, 512], BF16)
        s_opbe = sbe("s_opbe", [1, F], BF16)

        mm_dtr_prev = 0
        ev_dtr_lastexp = 0
        eo_prev = [None]
        ech = []
        wo_marks = []
        for ee in range(8):
            opw = [("mm", m("mm"))] if ee > 0 else [("g3", g5)]
            opl = p.op("gpsimd", lambda e, ee=ee: e.dma_start(
                out=s_opbe[:, :], in_=opb_d[ee:ee + 1, :]), waits=opw, inc="ob", inck=16)
            ev_dtr = 0
            for sti in range(4):
                st = ee * 4 + sti
                gw = []
                if ee == 0 and sti == 0:
                    gw = [("g3", g5), ("a0", m("a0")), ("a1", m("a1"))]
                if sti >= 2:
                    gw.append(("mm", m("mm")))
                elif ee > 0:
                    gw.append(("mm", mm_dtr_prev))
                p.op("gpsimd", lambda e, sti=sti: e.memset(s_dt[sti % 2][:, :], 0.0), waits=gw)
                gpm = p.mark("gpsimd", "gp")
                gath = p.op("gpsimd", lambda e, sti=sti, st=st: e.indirect_dma_start(
                    s_dt[sti % 2][:, :], None, d_ao[:, :],
                    bass.IndirectOffsetOnAxis(s_tos[:, st:st + 1], 0),
                    bounds_check=bc_reg(e, T - 1), oob_is_err=False), waits=[("gp", gpm)],
                    inc=f"dt{sti % 2}", inck=16)
                tw = [(f"dt{sti % 2}", gath)]
                if sti > 0:
                    tw.append(("ev", ev_dtr))
                elif ee > 0:
                    tw.append(("ev", ev_dtr_lastexp))
                for ft in range(NF):
                    p.op("tensor", lambda e, sti=sti, ft=ft: e.transpose(
                        pTb[:, ft * 128:(ft + 1) * 128], s_dt[sti % 2][:, ft * 128:(ft + 1) * 128],
                        s_idb[:, :]), waits=tw if ft == 0 else ())
                mmtr = p.mark("tensor", "mm")
                cpw = [("mm", mmtr)]
                if sti == 0 and ee > 0:
                    cpw.append(("mm", mm_dtr_prev))
                ev_dtr = p.op("vector", lambda e, sti=sti: e.tensor_copy(
                    s_dispT.ap().rearrange("pp (f s) -> pp f s", f=NF)[:, :, sti * 128:(sti + 1) * 128],
                    pTb.ap().rearrange("pp (f cc) -> pp f cc", f=NF)), waits=cpw, inc="ev")
            ev_dtr_lastexp = ev_dtr
            mm_dtr_prev = m("mm")
            disp_ev = ev_dtr
            for g in range(4):
                glw = []
                if not (ee == 0 and g < 2):
                    glw.append(("mm", m("mm")))
                gl = p.op("gpsimd", lambda e, ee=ee, g=g: e.dma_start(
                    out=s_wie[g % 2][:, :],
                    in_=inter_w[ee * F:(ee + 1) * F, :].rearrange("(kt pp) i -> pp kt i", pp=128)
                    [:, :, g * 1024:(g + 1) * 1024]), waits=glw, inc=f"wi{g % 2}", inck=16)
                for it in range(8):
                    ig = g * 8 + it
                    dst = pA if it % 2 == 0 else pB
                    mw = [(f"wi{g % 2}", gl), ("ev", disp_ev)] if it == 0 else []
                    if len(ech) >= 2:
                        mw.append(("ev", ech[-2]))
                    for kt in range(NF):
                        p.op("tensor", lambda e, g=g, it=it, kt=kt, dst=dst: e.matmul(
                            dst[:, 0:512], s_wie[g % 2][:, kt * 1024 + it * 128: kt * 1024 + it * 128 + 128],
                            s_dispT[:, kt * 512:(kt + 1) * 512], start=(kt == 0), stop=(kt == NF - 1)),
                            waits=mw if kt == 0 else ())
                    mmi = p.mark("tensor", "mm")
                    psf = (lambda dst: (lambda: dst[:, 0:512]))(dst)
                    outf = (lambda ig: (lambda: s_h[:, ig * 512:(ig + 1) * 512]))(ig)
                    ech.append(gelu_chain(psf, s_ib[:, ee * 32 + ig: ee * 32 + ig + 1],
                                          s_ibs[:, ee * 32 + ig: ee * 32 + ig + 1], outf,
                                          s_eA, s_eB, [("mm", mmi)]))
            h_ev = m("ev")
            for oh in range(2):
                for kg in range(4):
                    fl2 = len(wo_marks)
                    olw = [("mm", wo_marks[fl2 - 2])] if fl2 >= 2 else []
                    ol = p.op("gpsimd", lambda e, ee=ee, kg=kg, oh=oh, fl2=fl2: e.dma_start(
                        out=s_wo2[fl2 % 2][:, :],
                        in_=output_w[ee * INTER + kg * 1024: ee * INTER + (kg + 1) * 1024,
                                     oh * 512:(oh + 1) * 512]
                        .rearrange("(kt pp) o -> pp kt o", pp=128)), waits=olw,
                        inc=f"wo{fl2 % 2}", inck=16)
                    for kt in range(8):
                        kglob = kg * 8 + kt
                        for mt in range(4):
                            dstp = pA if mt < 2 else pB
                            doff = (mt % 2) * 512
                            mw = []
                            if kglob == 0 and mt == 0:
                                mw = [(f"wo{fl2 % 2}", ol), ("ev", h_ev), ("ev", m("ev"))]
                            elif kt == 0 and mt == 0:
                                mw = [(f"wo{fl2 % 2}", ol)]
                            p.op("tensor", lambda e, kg=kg, kt=kt, mt=mt, dstp=dstp, doff=doff,
                                 kglob=kglob, fl2=fl2: e.matmul(
                                dstp[:, doff:doff + 512],
                                s_h[:, kglob * 512 + mt * 128: kglob * 512 + mt * 128 + 128],
                                s_wo2[fl2 % 2][:, kt * 512:(kt + 1) * 512],
                                start=(kglob == 0), stop=False), waits=mw)
                    wo_marks.append(p.mark("tensor", "mm"))
                for mt in range(4):
                    dstp = pA if mt < 2 else pB
                    doff = (mt % 2) * 512
                    bw = [("ob", opl), ("gp", gp_set)] if mt == 0 else []
                    p.op("tensor", lambda e, oh=oh, mt=mt, dstp=dstp, doff=doff: e.matmul(
                        dstp[:, doff:doff + 512], s_onesrb[:, :],
                        s_opbe[:, oh * 512:(oh + 1) * 512], start=False, stop=True), waits=bw)
                    mme = p.mark("tensor", "mm")
                    evw = [("mm", mme)]
                    if eo_prev[0] is not None:
                        evw.append(eo_prev[0])
                    eve = p.op("vector", lambda e, dstp=dstp, doff=doff: e.tensor_copy(
                        s_eo[:, :], dstp[:, doff:doff + 512]), waits=evw, inc="ev")
                    eon = f"eo{(oh * 4 + mt) % 2}"
                    eok = p.op("scalar", lambda e, ee=ee, oh=oh, mt=mt: e.dma_start(
                        out=d_eo[ee * 512 + mt * 128: ee * 512 + (mt + 1) * 128,
                                 oh * 512:(oh + 1) * 512], in_=s_eo[:, :]),
                        waits=[("ev", eve)], inc=eon, inck=16)
                    eo_prev[0] = (eon, eok)
        p.op("scalar", lambda e: e.dma_start(out=s_fbf[:, :], in_=d_eo[0:1, 0:1]),
             waits=[("eo0", m("eo0")), ("eo1", m("eo1"))], inc="eo0", inck=16)
        if debug:
            p.op("sync", lambda e: e.dma_start(out=dbg["dbg_dispT"][:, :], in_=s_dispT[:, :]),
                 waits=[("ev", m("ev")), ("mm", m("mm"))], inc="dbg", inck=16)
            p.op("sync", lambda e: e.dma_start(out=dbg["dbg_h"][:, :], in_=s_h[:, :]),
                 waits=[("ev", m("ev")), ("mm", m("mm"))], inc="dbg", inck=16)
        phe.close()
        p.barrier()
        if debug:
            p.op("sync", lambda e: e.dma_start(out=dbg["dbg_eo"][:, :], in_=d_eo[:, :]),
                 waits=[("eo0", m("eo0")), ("eo1", m("eo1"))], inc="dbg", inck=16)
            p.op("sync", lambda e: e.dma_start(out=dbg["dbg_ao"][:, :], in_=d_ao[:, :]),
                 inc="dbg", inck=16)
            p.op("sync", lambda e: e.dma_start(out=dbg["dbg_rm"][:, :], in_=d_rm[:, :]),
                 waits=[("rm0", m("rm0")), ("rm1", m("rm1"))], inc="dbg", inck=16)
            p.op("sync", lambda e: e.dma_start(out=dbg["dbg_Si"][:, :], in_=s_Si[:, :]),
                 inc="dbg", inck=16)
            p.op("sync", lambda e: e.dma_start(out=dbg["dbg_gc1"][:, :], in_=s_gc1[:, :]),
                 inc="dbg", inck=16)
            p.op("sync", lambda e: e.dma_start(out=dbg["dbg_ib"][:, :], in_=s_ib[:, :]),
                 inc="dbg", inck=16)
            p.op("sync", lambda e: e.dma_start(out=dbg["dbg_iotok"][:, :], in_=s_iotok[:, :]),
                 inc="dbg", inck=16)
            p.op("sync", lambda e: e.dma_start(out=dbg["dbg_Fi"][:, :], in_=s_Fi[:, :]),
                 inc="dbg", inck=16)
            p.op("sync", lambda e: e.dma_start(out=dbg["dbg_c0"][:, :], in_=s_c0[:, :]),
                 inc="dbg", inck=16)

        if upto < 10:
            raise _StopBuild()
        phf = ES2()
        sbf = lambda n, s, d: phf.enter_context(nc.sbuf_tensor(n, s, d))
        f_x = [sbf(f"f_x{i}", [128, F], F32) for i in range(2)]
        f_rm = [sbf(f"f_rm{i}", [128, F], BF16) for i in range(2)]
        f_moe = [sbf(f"f_moe{i}", [128, F], BF16) for i in range(2)]
        f_o1 = sbf("f_o1", [128, F], F32)
        f_out = [sbf(f"f_out{i}", [128, F], F32) for i in range(2)]

        ld_fx = [0] * NT
        ld_frm = [0] * NT
        wq_moe = [0] * NT
        ev_o2 = [0] * NT
        ev_done = [0] * NT
        fin_dma = [0] * NT
        for t in range(NT):
            par = t % 2
            xw = [("ev", ev_done[t - 2])] if t >= 2 else []
            ld_fx[t] = p.op("sync", lambda e, t=t, par=par: e.dma_start(
                out=f_x[par][:, :], in_=x_sh[t * 128:(t + 1) * 128, :]), waits=xw,
                inc=f"fx{par}", inck=16)
            frmw = [("rm0", m("rm0")), ("rm1", m("rm1"))] if t == 0 else (
                [("mm", m("mm"))] if t >= 2 else [])
            ld_frm[t] = p.op("sync", lambda e, t=t, par=par: e.dma_start(
                out=f_rm[par].ap().rearrange("pp (f c) -> pp f c", f=NF),
                in_=d_rm.rearrange("(f pp) tok -> pp f tok", pp=128)[:, :, t * 128:(t + 1) * 128]),
                waits=frmw, inc=f"fr{par}", inck=16)
            gw = [("ev", ev_o2[t - 2])] if t >= 2 else (
                [("eo0", m("eo0")), ("eo1", m("eo1"))] if t == 0 else [])
            p.op("gpsimd", lambda e, par=par: e.memset(f_moe[par][:, :], 0.0), waits=gw)
            gpm = p.mark("gpsimd", "gp")
            wq_moe[t] = p.op("gpsimd", lambda e, t=t, par=par: e.indirect_dma_start(
                f_moe[par][:, :], None, d_eo[:, :],
                bass.IndirectOffsetOnAxis(s_Si[:, t:t + 1], 0),
                bounds_check=bc_reg(e, NSLOT - 1), oob_is_err=False), waits=[("gp", gpm)],
                inc=f"mo{par}", inck=16)
            tw = [(f"fr{par}", ld_frm[t])]
            if t >= 1:
                tw.append(("ev", ev_o2[t - 1]))
            for ft in range(NF):
                p.op("tensor", lambda e, par=par, ft=ft: e.transpose(
                    pTb[:, ft * 128:(ft + 1) * 128], f_rm[par][:, ft * 128:(ft + 1) * 128], s_idb[:, :]),
                    waits=tw if ft == 0 else ())
            mmr = p.mark("tensor", "mm")
            p.op("vector", lambda e, t=t, par=par: e.scalar_tensor_tensor(
                f_o1[:, :], pTb[:, :], s_c0[:, t:t + 1], f_x[par][:, :], ALU.mult, ALU.add),
                waits=[("mm", mmr), (f"fx{par}", ld_fx[t]), (f"mo{par}", wq_moe[t])])
            ev_o2[t] = p.op("vector", lambda e, t=t, par=par: e.scalar_tensor_tensor(
                f_o1[:, :], f_moe[par][:, :], s_gc1[:, t:t + 1], f_o1[:, :], ALU.mult, ALU.add),
                inc="ev")
            for r in range(2):
                tw2 = [("ev", ev_o2[t])] if r == 0 else [("ev", m("ev"))]
                if t >= 1 and r == 0:
                    tw2.append(("ev", ev_done[t - 1]))
                if t >= 2 and r == 0:
                    tw2.append((f"fy{par}", fin_dma[t - 2]))
                for j in range(4):
                    ft = r * 4 + j
                    p.op("tensor", lambda e, ft=ft, j=j, t=t: e.transpose(
                        pTf[:, j * 128:(j + 1) * 128], fm(s_bigB, ft, t * 128, (t + 1) * 128),
                        s_id[:, :]), waits=tw2 if j == 0 else ())
                mmr2 = p.mark("tensor", "mm")
                p.op("vector", lambda e, r=r, t=t, par=par: e.scalar_tensor_tensor(
                    f_out[par][:, r * 512:(r + 1) * 512], pTf[:, :], 1.0,
                    f_o1[:, r * 512:(r + 1) * 512], ALU.mult, ALU.add), waits=[("mm", mmr2)])
                ev_done[t] = p.mark("vector", "ev")
            fw = [("ev", ev_done[t])]
            if t >= 2:
                fw.append((f"fy{par}", fin_dma[t - 2]))
            fin_dma[t] = p.op("sync", lambda e, t=t, par=par: e.dma_start(
                out=y[t * 128:(t + 1) * 128, :], in_=f_out[par][:, :]), waits=fw,
                inc=f"fy{par}", inck=16)
        p.op("sync", lambda e: e.nop(), waits=[("fy0", m("fy0")), ("fy1", m("fy1"))])
        phf.close()

    except _StopBuild:
        pass
    p.op("sync", lambda e: e.nop(),
         waits=[(s, p.m(s)) for s in list(p.CNT) if s not in ("bar",)])
    p.replay()
    try:
        es.close()
    except AssertionError:
        pass
    return nc, dbg


def _host_inputs(inputs, core):
    f32 = np.float32
    d = {}
    d["x_sh"] = inputs["L0_attn_x"].reshape(-1, F)[core * T:(core + 1) * T]
    d["qkv_w"] = inputs["L0_attn_qkv_w"]
    d["proj_w"] = inputs["L0_attn_proj_w"]
    d["conv1_w"] = inputs["L0_attn_conv1_w"]
    d["conv2_w"] = inputs["L0_attn_conv2_w"]
    d["inter_w"] = inputs["L0_ffn_inter_w"].reshape(8 * F, INTER)
    d["output_w"] = inputs["L0_ffn_output_w"].reshape(8 * INTER, F)
    d["ri_w"] = inputs["L0_ffn_res_inter_w"]
    d["ro_w"] = inputs["L0_ffn_res_output_w"]
    d["wgcat"] = np.concatenate([inputs["L0_ffn_wg"], inputs["L0_ffn_res_coef"]], axis=1)
    d["ident"] = np.eye(128, dtype=f32)
    d["iotok"] = np.arange(T, dtype=np.int32).reshape(NT, 128).T
    d["e512c"] = (np.arange(8, dtype=f32) * L).reshape(8, 1)
    d["prefm"] = np.broadcast_to((np.arange(8) < core).astype(f32)[None, :], (8, 8))
    bo = np.zeros((H, F), dtype=f32)
    for g in range(H):
        bo[g, g * 64:(g + 1) * 64] = 1.0
    d["bones"] = bo
    b = core // 4
    bm = np.zeros((128, 1), dtype=f32)
    bm[b * 64:(b + 1) * 64] = 1.0
    d["bmask"] = bm
    sel0 = np.zeros((128, 1), dtype=f32); sel0[0:64] = 1.0 if b == 0 else 0.0
    sel1 = np.zeros((128, 1), dtype=f32); sel1[64:128] = 1.0 if b == 1 else 0.0
    d["bsel0"] = sel0
    d["bsel1"] = sel1
    col = lambda v: v.reshape(-1, 128).T
    d["s1c"] = col(inputs["L0_attn_scale1"])
    d["s2c"] = col(inputs["L0_attn_scale2"])
    d["gmc"] = col(inputs["L0_attn_gamma"])
    d["nwc"] = col(inputs["L0_ffn_attn_nw"])
    d["nbc"] = col(inputs["L0_ffn_attn_nb"])
    d["pbc"] = col(inputs["L0_attn_proj_b"])
    d["c1bc"] = col(inputs["L0_attn_conv1_b"])
    d["c2bc"] = col(inputs["L0_attn_conv2_b"])
    d["ibc"] = inputs["L0_ffn_inter_b"].reshape(8, 32, 128).transpose(2, 0, 1).reshape(128, 8 * 32)
    d["ribc"] = col(inputs["L0_ffn_res_inter_b"])
    d["robc"] = col(inputs["L0_ffn_res_output_b"])
    d["nwr"] = inputs["L0_ffn_attn_nw"].reshape(1, F)
    d["nbr"] = inputs["L0_ffn_attn_nb"].reshape(1, F)
    d["gmr"] = inputs["L0_attn_gamma"].reshape(1, F)
    d["opb"] = inputs["L0_ffn_output_b"]
    return {k: np.ascontiguousarray(np.asarray(v)) for k, v in d.items()}


def _build_upto(debug, upto):
    try:
        return build_nc(debug=debug, upto=upto)
    except _StopBuild:
        raise


def _run(inputs, debug=False, upto=99):
    nc, dbg = build_nc(debug=debug, upto=upto)
    in_maps = [_host_inputs(inputs, c) for c in range(NCORES)]
    res = run_bass_kernel_spmd(nc, in_maps, core_ids=list(range(NCORES)))
    outs = [np.asarray(res.results[c]["y"]) for c in range(NCORES)]
    yfull = np.concatenate(outs, axis=0).reshape(2, 8192, F)
    if debug:
        return yfull, res
    return yfull


def kernel(**inputs):
    return _run(inputs, debug=False)
